# version 15
# baseline (speedup 1.0000x reference)
import numpy as np
import ml_dtypes
import concourse.bass as bass
import concourse.mybir as mybir
from concourse.alu_op_type import AluOpType
from concourse.bass_utils import run_bass_kernel_spmd

F32 = mybir.dt.float32
BF16 = mybir.dt.bfloat16
I32 = mybir.dt.int32
F8 = mybir.dt.float8e4
AF = mybir.ActivationFunctionType
Alu = AluOpType

B, N, DIM = 4, 4096, 1024
H, DH, NPAIR = 16, 64, 8
NMEM, TOPK = 4, 64
QK_SCALE = 10.0
NTOK = N + NMEM
KW = 4224
NG = 32
OWNQ = 2048
NTILE = N // 128
NQT = OWNQ // 128
EPS_LN = 1e-5
NROUND = 14


def octet_q0_worst(g, i):
    return 128 * g + 16 * i + 8


def octet_width(g, i):
    return min(KW, ((octet_q0_worst(g, i) + 13 + 127) // 128) * 128)


class Sched:
    def __init__(self, nc, sem_names):
        self.nc = nc
        self.sems = {}
        self.cnt = {}
        self.tags = {}
        self.order = ("sync", "gpsimd", "vector", "scalar", "tensor")
        self.steps = {e: [] for e in self.order}
        self._h = []
        for nm in sem_names:
            h = nc.alloc_semaphore(name=nm)
            self.sems[nm] = h
            self.cnt[nm] = 0
            self._h.append(h)

    def sem_range(self):
        nums = sorted(h.num for h in self._h)
        assert nums == list(range(nums[0], nums[0] + len(nums))), nums
        return range(nums[0], nums[-1] + 1)

    def emit(self, eng, fn, inc=None, dinc=None, deps=(), tag=None):
        sem, amt = (inc, 1) if inc is not None else ((dinc, 16) if dinc is not None else (None, 0))
        if sem is not None:
            self.cnt[sem] += amt
            if tag is not None:
                self.tags[tag] = [(sem, self.cnt[sem])]
        else:
            assert tag is None
        self.steps[eng].append((tuple(deps), fn, sem, self.cnt[sem] if sem else 0, amt))

    def alias(self, newtag, oldtag):
        self.tags[newtag] = self.tags[oldtag]

    def build(self):
        nc, sems = self.nc, self.sems

        def run_stream(e, name):
            seen = {}
            for deps, fn, sem, val, amt in self.steps[name]:
                best = {}
                for d in deps:
                    v = self.tags[d]
                    for (ws, wv) in (v if isinstance(v, list) else [v]):
                        if best.get(ws, 0) < wv:
                            best[ws] = wv
                for ws, wv in best.items():
                    if seen.get(ws, 0) < wv:
                        e.wait_ge(sems[ws], wv)
                        seen[ws] = wv
                ins = fn(e)
                if sem is not None:
                    assert ins is not None
                    ins.then_inc(sems[sem], amt)

        with nc.Block() as blk:
            @blk.sync
            def _(e):
                run_stream(e, "sync")

            @blk.gpsimd
            def _(e):
                run_stream(e, "gpsimd")

            @blk.vector
            def _(e):
                run_stream(e, "vector")

            @blk.scalar
            def _(e):
                run_stream(e, "scalar")

            @blk.tensor
            def _(e):
                run_stream(e, "tensor")


def host_consts(half):
    ident = np.eye(128, dtype=ml_dtypes.bfloat16)
    tpl = np.zeros((128, 528), dtype=ml_dtypes.bfloat16)
    for p in range(128):
        tpl[p, : 384 + (p % 8) - (8 if half == 0 else 0)] = 1.0
    selpair = np.zeros((8, 128, 16), dtype=ml_dtypes.bfloat16)
    selb = np.zeros((8, 16, 128), dtype=ml_dtypes.bfloat16)
    for pr in range(8):
        for p in range(128):
            selpair[pr, p, 2 * pr + (p >= 64)] = 1.0
            selb[pr, 2 * pr + (p >= 64), p] = 1.0
    selh = np.zeros((16, 128), dtype=ml_dtypes.bfloat16)
    for p in range(128):
        selh[p // 8, p] = 1.0
    qsel = np.zeros((128, 128), dtype=ml_dtypes.bfloat16)
    for p in range(128):
        for n in range(128):
            if p % 8 == n % 8:
                qsel[p, n] = 1.0
    return {"c_ident": ident, "c_msk": tpl, "c_selpair": selpair,
            "c_selbf": selb.astype(np.float32), "c_selh": selh, "c_qsel": qsel}


def build_program(taps=(), groups=None, p2stage=16):
    if groups is None:
        groups = range(NG)
    groups = list(groups)
    nc = bass.Bass("TRN2", target_bir_lowering=False)

    def din(name, shape, dt):
        return nc.dram_tensor(name, shape, dt, kind="ExternalInput").ap()

    xfull = din("xfull", [N, DIM], F32)
    xq = din("xq", [OWNQ, DIM], F32)
    wq, wk, wv, wg, wo = (din(n, [DIM, DIM], F32) for n in ("wq", "wk", "wv", "wg", "wo"))
    wpre = din("wpre", [16, 16], F32)
    wpost = din("wpost", [16, 16], F32)
    memk = din("memk", [H, NMEM, DH], F32)
    memv = din("memv", [H, NMEM, DH], F32)
    hsc = din("hsc", [H, 1], F32)
    bg = din("bg", [DIM], F32)
    fker = din("fker", [DIM, DIM], BF16)
    fbias = din("fbias", [1, DIM], BF16)
    sin_ = din("s_in", [1, 1], F32)
    sk_ = din("s_k", [1, 1], F32)
    c_ident = din("c_ident", [128, 128], BF16)
    c_msk = din("c_msk", [128, 528], BF16)
    c_selpair = din("c_selpair", [8, 128, 16], BF16)
    c_selbf = din("c_selbf", [8, 16, 128], F32)
    c_selh = din("c_selh", [16, 128], BF16)
    c_qsel = din("c_qsel", [128, 128], BF16)

    out = nc.dram_tensor("out", [OWNQ, DIM], F32, kind="ExternalOutput").ap()
    def kind_of(n):
        return "ExternalOutput" if n in taps else "Internal"
    vscr = nc.dram_tensor("vscr", [KW, DIM], BF16, kind=kind_of("vscr")).ap()
    qscr = nc.dram_tensor("qscr", [NPAIR, 128, OWNQ], BF16, kind=kind_of("qscr")).ap()
    gscr = nc.dram_tensor("gscr", [NPAIR, 128, OWNQ], BF16, kind=kind_of("gscr")).ap()
    kscr = nc.dram_tensor("kscr", [8, 128, DIM], F8, kind="Internal").ap()

    def S(name, shape, dt):
        return nc.alloc_sbuf_tensor(name, shape, dt).ap()

    KR = [S(f"KR{p}", [128, KW], BF16) for p in range(NPAIR)]
    EB = S("EB", [128, 8 * KW], BF16)
    WBA = S("WBA", [128, 8 * DIM], BF16)
    WBB = S("WBB", [128, 8 * DIM], BF16)
    KQ8R = S("KQ8R", [128, 2 * DIM], F8)
    IDE = S("IDE", [128, 128], BF16)
    MSK = S("MSK", [128, 528], BF16)
    SELP = S("SELP", [128, 8 * 16], BF16)
    SELBF = S("SELBF", [16, 8 * 128], F32)
    SELH = S("SELH", [16, 128], BF16)
    QSEL = EB[:, 7 * KW: 7 * KW + 128]
    WREP = EB[:16, 7 * KW + 128: 7 * KW + 256]
    BDPRE = S("BDPRE", [128, 128], BF16)
    BDPOST = S("BDPOST", [128, 128], BF16)
    BDP = [S(f"BDP{i}", [128, 128], BF16) for i in range(8)]
    MHATN = S("MHATN", [128, 1], F32)
    W16 = S("W16", [16, 16], F32)
    W16B = S("W16B", [16, 16], BF16)
    WPR = S("WPR", [16, 16], F32)
    WPOB = S("WPOB", [16, 16], F32)
    ABSW = S("ABSW", [16, 16], F32)
    MH16 = S("MH16", [16, 1], F32)
    MH16B = S("MH16B", [16, 1], BF16)
    RINV = S("RINV", [16, 128], F32)
    HS16 = S("HS16", [16, 1], F32)
    HSV = S("HSV", [128, 8], F32)
    BGS = S("BGS", [128, 8], F32)
    BIAS64 = S("BIAS64", [64, DIM], BF16)
    SPROD = S("SPROD", [128, 1], F32)
    EPSV = S("EPSV", [128, 1], F32)
    ONESF = S("ONESF", [1, 128], F32)
    ONESB = S("ONESB", [1, 64], BF16)
    SREC = S("SREC", [128, 4], F32)
    XT = [EB[:, (5 + i) * KW: (5 + i) * KW + 2 * DIM].bitcast(F32) for i in range(2)]
    XLN = EB[:, 0 * KW: 0 * KW + DIM]
    XSQ = EB[:, 1 * KW: 1 * KW + DIM]
    XLT = EB[:, 2 * KW: 2 * KW + DIM]
    STAT = S("STAT", [128, 8], F32)
    SQT = S("SQT", [128, 128], BF16)
    RNB = S("RNB", [128, 128], F32)
    MCP = S("MCP", [128, NMEM], F32)
    VST = EB[:, 3 * KW: 3 * KW + DIM]
    QNT = S("QNT", [128, 128], BF16)
    FKT = EB[:, 4 * KW: 4 * KW + DIM]
    S1 = [S(f"S1_{i}", [128, 1024], BF16) for i in range(2)]
    QS = S("QS", [128, 1024], BF16)
    VS = [S(f"VS{i}", [128, DIM], BF16) for i in range(3)]
    ATT = S("ATT", [128, 1024], BF16)
    LO = S("LO", [128, 8], I32)
    TF = S("TF", [128, 8], I32)
    TFN = S("TFN", [128, 8], I32)
    CMPV = WBB.bitcast(F32)[:, 4078:4086]
    CNT4 = WBB.bitcast(F32)[:, 4086:4096]
    CNT = S("CNT", [128, 8], F32)
    GEF = S("GEF", [128, 8], I32)
    SUM8 = S("SUM8", [128, 8], F32)
    RS8 = S("RS8", [128, 8], F32)
    GS = S("GS", [128, 512], BF16)
    XRES = S("XRES", [64, DIM], F32)
    X1 = S("X1", [64, DIM], F32)
    H2B = S("H2B", [64, DIM], BF16)
    XQ8 = S("XQ8", [128, 512], F8)
    AOUT = S("AOUT", [128, 512], BF16)
    YB = S("YB", [64, DIM], BF16)

    PS = [nc.alloc_psum_tensor(f"ps{i}", [128, 512], F32).ap() for i in range(6)]
    PT = [nc.alloc_psum_tensor(f"pt{i}", [128, 1024], BF16).ap() for i in range(2)]

    sc = Sched(nc, ["DC", "DCG", "D0B", "D0BG", "DF", "SPK0", "SPK1", "X0", "X1", "SPV", "SPQ", "SPG",
                "DW2", "DQS", "DGS", "DXR", "DV0", "DV1", "DV2", "DK0", "DK1", "SPO",
                "T", "V", "A", "G"])
    nc.gpsimd.dma_reset(sc.sem_range())
    nc.gpsimd.sem_clear(sc.sem_range())
    nc.all_engine_barrier()
    E = sc.emit

    for wnm, (wt, wdst) in [("wk", (wk, WBA)), ("wv", (wv, WBB))]:
        for ck in range(8):
            E("gpsimd", lambda e, wt=wt, wdst=wdst, ck=ck: e.dma_start(
                out=wdst[:, ck * DIM:(ck + 1) * DIM], in_=wt[ck * 128:(ck + 1) * 128, :]),
                dinc="DCG", tag=f"{wnm}c{ck}")
    E("sync", lambda e: e.dma_start(out=IDE[:, :], in_=c_ident[:, :]), dinc="DC", tag="ident")
    E("sync", lambda e: e.dma_start(out=MSK[:, :], in_=c_msk[:, :]), dinc="DC", tag="msk")
    E("sync", lambda e: e.dma_start(
        out=SELP.rearrange("p (a m) -> p a m", a=8), in_=c_selpair.rearrange("a p m -> p a m")),
        dinc="DC", tag="selp")
    E("sync", lambda e: e.dma_start(
        out=SELBF.rearrange("h (a p) -> h a p", a=8), in_=c_selbf.rearrange("a h p -> h a p")),
        dinc="DC", tag="selbf")
    E("sync", lambda e: e.dma_start(out=SELH[:, :], in_=c_selh[:, :]), dinc="DC", tag="selh")
    E("sync", lambda e: e.dma_start(out=QSEL[:, :], in_=c_qsel[:, :]), dinc="DC", tag="qsel")
    E("sync", lambda e: e.dma_start(out=HS16[:, :], in_=hsc[:, :]), dinc="DC", tag="hs")
    for ck in range(8):
        E("sync", lambda e, ck=ck: e.dma_start(
            out=BGS[:, ck:ck + 1], in_=bg.rearrange("(c p) -> c p", p=128)[ck].unsqueeze(1)),
            dinc="DC", tag=(f"bgs" if ck == 7 else f"bgs{ck}"))
    E("sync", lambda e: e.dma_start(out=SREC[:1, 0:1], in_=sin_[:, :]), dinc="DC", tag="sin")
    E("sync", lambda e: e.dma_start(out=SREC[:1, 1:2], in_=sk_[:, :]), dinc="DC", tag="sk")
    E("sync", lambda e: e.dma_start(out=BIAS64[:1, :], in_=fbias[:, :]), dinc="DC", tag="fb0")
    E("sync", lambda e: e.dma_start(out=W16[:, :], in_=wpre.rearrange("a b -> b a"), allow_slow_non_contiguous=True), dinc="DC", tag="wpreT")
    E("sync", lambda e: e.dma_start(out=WPR[:, :], in_=wpre[:, :]), dinc="DC", tag="wpre")
    E("sync", lambda e: e.dma_start(out=WPOB[:, :], in_=wpost.rearrange("a b -> b a"), allow_slow_non_contiguous=True), dinc="DC", tag="wpostT")
    for h in range(H):
        E("gpsimd", lambda e, h=h: e.dma_start(
            out=KR[h // 2][64 * (h % 2):64 * (h % 2) + 64, 0:NMEM],
            in_=memk[h].rearrange("a b -> b a"), allow_slow_non_contiguous=True),
            dinc="DCG", tag=f"memk{h}")
        E("gpsimd", lambda e, h=h: e.dma_start(
            out=VST[0:NMEM, h * 64:(h + 1) * 64], in_=memv[h, :, :]), dinc="DCG", tag=f"memv{h}")
    sc.tags["DCALL"] = [("DC", sc.cnt["DC"]), ("DCG", sc.cnt["DCG"])]
    sc.alias("memvall", "DCALL")

    E("vector", lambda e: e.tensor_scalar(W16B[:, :], W16[:, :], QK_SCALE, None, Alu.mult),
      inc="V", deps=["DCALL"], tag="w16b")
    E("vector", lambda e: e.tensor_copy(
        WREP.rearrange("g (h r) -> g h r", r=8),
        W16B.rearrange("g h -> g h 1" if False else "g (h o) -> g h o", o=1).to_broadcast([16, 16, 8])),
        inc="V", deps=["w16b"], tag="wrepA")
    E("tensor", lambda e: e.matmul(PS[5][:, 128:256], SELH[:, :], WREP[:, :],
      start=True, stop=True), inc="T", deps=["wrepA", "DCALL"], tag="bdpremm")
    E("vector", lambda e: e.tensor_tensor(BDPRE[:, :], PS[5][:, 128:256], QSEL[:, :], Alu.mult),
      inc="V", deps=["bdpremm"], tag="bdpre7")
    E("vector", lambda e: e.tensor_copy(W16B[:, :], WPOB[:, :]), inc="V",
      deps=["DCALL", "bdpre7"], tag="w16post")
    E("vector", lambda e: e.tensor_copy(
        WREP.rearrange("g (h r) -> g h r", r=8),
        W16B.rearrange("g (h o) -> g h o", o=1).to_broadcast([16, 16, 8])),
        inc="V", deps=["w16post"], tag="wrepB")
    E("tensor", lambda e: e.matmul(PS[5][:, 128:256], SELH[:, :], WREP[:, :],
      start=True, stop=True), inc="T", deps=["wrepB", "bdpre7"], tag="bdpostmm")
    E("vector", lambda e: e.tensor_tensor(BDPOST[:, :], PS[5][:, 128:256], QSEL[:, :], Alu.mult),
      inc="V", deps=["bdpostmm"], tag="bdpost7")
    E("scalar", lambda e: e.activation(ABSW[:, :], WPR[:, :], AF.Abs), inc="A",
      deps=["DCALL"], tag="absw")
    E("vector", lambda e: e.tensor_reduce(MH16[:, 0:1], ABSW[:, :],
      axis=mybir.AxisListType.X, op=Alu.add), inc="V", deps=["absw"], tag="mh16")
    E("vector", lambda e: e.tensor_copy(MH16B[:, :], MH16[:, 0:1]), inc="V",
      deps=["mh16"], tag="mh16b")
    E("tensor", lambda e: e.matmul(PS[5][:, 0:1], SELH[:, :], MH16B[:, 0:1], start=True, stop=True),
      inc="T", deps=["mh16b", "DCALL"], tag="mhbc")
    E("vector", lambda e: e.tensor_scalar(MHATN[:, 0:1], PS[5][:, 0:1], -QK_SCALE, None, Alu.mult),
      inc="V", deps=["mhbc"], tag="mhat")
    for ck in range(8):
        E("tensor", lambda e, ck=ck: e.matmul(
            PS[5][:, 64 + ck:65 + ck], SELBF[:, ck * 128:(ck + 1) * 128], HS16[:, 0:1],
            start=True, stop=True), inc="T", deps=["DCALL", "DCALL", "mhat"],
            tag=f"hsb{ck}")
    E("vector", lambda e: e.tensor_copy(HSV[:, :], PS[5][:, 64:72]), inc="V",
      deps=[f"hsb{c}" for c in range(8)], tag="hsv")
    E("gpsimd", lambda e: e.memset(ONESF[:, :], 1.0), inc="G", tag="onesf")
    E("gpsimd", lambda e: e.memset(EPSV[:, :], EPS_LN), inc="G", tag="epsv")
    E("gpsimd", lambda e: e.memset(ONESB[:, :], 1.0), inc="G", tag="onesb")
    E("gpsimd", lambda e: e.memset(
        QS.rearrange("p (a q) -> p a q", a=16)[64:128, 0::2, :], 0.0), inc="G", tag="qz0")
    E("gpsimd", lambda e: e.memset(
        QS.rearrange("p (a q) -> p a q", a=16)[0:64, 1::2, :], 0.0), inc="G", tag="qz1")
    E("tensor", lambda e: e.matmul(PS[4][:, 0:2], ONESF[:, :], SREC[:1, 0:2],
      start=True, stop=True), inc="T", deps=["DCALL", "DCALL", "onesf"], tag="sbcmm")
    E("vector", lambda e: e.tensor_copy(SREC[:, 0:2], PS[4][:, 0:2]), inc="V",
      deps=["sbcmm"], tag="sbc")
    E("vector", lambda e: e.tensor_tensor(SPROD[:, 0:1], SREC[:, 0:1], SREC[:, 1:2], Alu.mult),
      inc="V", deps=["sbc"], tag="sprod")
    E("vector", lambda e: e.reciprocal(SREC[:, 2:4], SREC[:, 0:2]), inc="V",
      deps=["sprod"], tag="srec")
    for bh in range(2):
        E("tensor", lambda e, bh=bh: e.matmul(PS[4][:64, :], ONESB[:, :],
          BIAS64[:1, bh * 512:(bh + 1) * 512], start=True, stop=True), inc="T",
          deps=["DCALL", "onesb", "sbc"] + ([f"bias64_0"] if bh else []), tag=f"bias64mm{bh}")
        E("vector", lambda e, bh=bh: e.tensor_copy(
            BIAS64[:, bh * 512:(bh + 1) * 512], PS[4][:64, :]), inc="V",
            deps=[f"bias64mm{bh}"], tag=f"bias64_{bh}")
    sc.alias("bias64", "bias64_1")
    for ck in range(8):
        E("sync", lambda e, ck=ck: e.dma_start(out=FKT[:, :], in_=fker[ck * 128:(ck + 1) * 128, :]),
          dinc="DF", deps=([f"kq8_{ck-1}"] if ck else []), tag=f"fkt{ck}")
        E("vector", lambda e, ck=ck: e.tensor_scalar(
            KQ8R[:, (ck % 2) * DIM:(ck % 2 + 1) * DIM], FKT[:, :], SREC[:, 3:4], None, Alu.mult),
            inc="V", deps=[f"fkt{ck}", "srec"] + ([f"kq8sp_{ck-2}"] if ck >= 2 else []),
            tag=f"kq8_{ck}")
        E("sync", lambda e, ck=ck: e.dma_start(
            out=kscr[ck, :, :], in_=KQ8R[:, (ck % 2) * DIM:(ck % 2 + 1) * DIM]),
            dinc=f"SPK{ck % 2}", deps=[f"kq8_{ck}"], tag=f"kq8sp_{ck}")
    E("sync", lambda e: e.dma_start(out=vscr[0:NMEM, :], in_=VST[0:NMEM, :]),
      dinc="D0B", deps=["memvall"], tag="memvsp")
    E("gpsimd", lambda e: e.memset(VST[:, :], 0.0), inc="G", deps=["memvsp"], tag="vz0")
    E("sync", lambda e: e.dma_start(out=vscr[4100:KW, :], in_=VST[0:KW - 4100, :]),
      dinc="D0B", deps=["vz0"], tag="vztail")
    for p in range(NPAIR):
        E("gpsimd", lambda e, p=p: e.memset(KR[p][:, 4100:KW], 0.0), inc="G", tag=f"kz{p}")
    sc.tags["D0BALL"] = [("D0B", sc.cnt["D0B"]), ("D0BG", sc.cnt["D0BG"])]

    def emit_ln(tg, xt_ap, dep, extra_ln_deps=()):
        E("vector", lambda e: e.tensor_reduce(
            STAT[:, 0:1], xt_ap, axis=mybir.AxisListType.X, op=Alu.add),
            inc="V", deps=[dep], tag=f"{tg}sum")
        E("scalar", lambda e: e.activation(XSQ[:, :], xt_ap, AF.Square,
          accum_out=STAT[:, 1:2]), inc="A", deps=[dep], tag=f"{tg}sq")

        E("vector", lambda e: e.tensor_scalar(STAT[:, 0:1], STAT[:, 0:1], 1.0 / DIM, None, Alu.mult),
          inc="V", deps=[f"{tg}sum"], tag=f"{tg}mean")
        E("vector", lambda e: e.tensor_scalar(STAT[:, 2:3], STAT[:, 1:2], 1.0 / DIM, None, Alu.mult),
          inc="V", deps=[f"{tg}sq"], tag=f"{tg}ex2")
        E("vector", lambda e: e.tensor_tensor(STAT[:, 3:4], STAT[:, 0:1], STAT[:, 0:1], Alu.mult),
          inc="V", deps=[f"{tg}mean"], tag=f"{tg}m2")
        E("vector", lambda e: e.tensor_tensor(STAT[:, 2:3], STAT[:, 2:3], STAT[:, 3:4], Alu.subtract),
          inc="V", deps=[f"{tg}ex2", f"{tg}m2"], tag=f"{tg}var")
        E("scalar", lambda e: e.activation(STAT[:, 4:5], STAT[:, 2:3], AF.Ln, bias=EPSV[:, 0:1]),
          inc="A", deps=[f"{tg}var", "epsv"], tag=f"{tg}sd")
        E("scalar", lambda e: e.activation(STAT[:, 5:6], STAT[:, 4:5], AF.Exp, scale=-0.5),
          inc="A", deps=[f"{tg}sd"], tag=f"{tg}rstd")
        E("vector", lambda e: e.tensor_scalar(
            XLN[:, :], xt_ap, STAT[:, 0:1], STAT[:, 5:6], Alu.subtract, Alu.mult),
            inc="V", deps=[f"{tg}rstd"] + list(extra_ln_deps), tag=f"{tg}ln")

    def emit_tp(tg, prev_xlt_users):
        for fc in range(8):
            d = [f"{tg}ln", "DCALL"]
            if fc == 0:
                d += prev_xlt_users
            E("tensor", lambda e, fc=fc: e.transpose(
                PT[0][:, fc * 128:(fc + 1) * 128], XLN[:, fc * 128:(fc + 1) * 128], IDE[:, :]),
                inc="T", deps=d, tag=f"{tg}tp{fc}")
        E("vector", lambda e: e.tensor_copy(XLT[:, :], PT[0][:, :]),
          inc="V", deps=[f"{tg}tp7"], tag=f"{tg}xlt")

    def emit_l2norm(tg, src_ap, pr, ncols, write_fn, dep, prev_tag, extra_done_deps=()):
        sqd = [dep]
        if prev_tag:
            sqd.append(f"{prev_tag}ss")
        E("scalar", lambda e: e.activation(SQT[:, :ncols], src_ap, AF.Square),
          inc="A", deps=sqd, tag=f"{tg}sq")
        ssd = [f"{tg}sq", "DCALL"]
        if prev_tag:
            ssd.append(f"{prev_tag}mx")
        else:
            ssd.append("sbc")
            ssd.append("bias64_1")
        E("tensor", lambda e: e.matmul(
            PS[4][:16, 256:256 + ncols], SELP[:, pr * 16:(pr + 1) * 16], SQT[:, :ncols],
            start=True, stop=True), inc="T", deps=ssd, tag=f"{tg}ss")

        mxd = [f"{tg}ss"]
        if prev_tag:
            mxd.append(f"{prev_tag}done")
        E("vector", lambda e: e.tensor_scalar(
            RNB[:16, :ncols], PS[4][:16, 256:256 + ncols], 1e-24, None, Alu.max),
            inc="V", deps=mxd, tag=f"{tg}mx")
        E("scalar", lambda e: e.activation(RNB[:16, :ncols], RNB[:16, :ncols], AF.Ln),
          inc="A", deps=[f"{tg}mx"], tag=f"{tg}sd")
        rcd = [f"{tg}sd"]
        if prev_tag:
            rcd.append(f"{prev_tag}bc")
        E("scalar", lambda e: e.activation(RINV[:, :ncols], RNB[:16, :ncols], AF.Exp, scale=-0.5),
          inc="A", deps=rcd, tag=f"{tg}rc")
        bcd = [f"{tg}rc", "DCALL"]
        if prev_tag:
            bcd.append(f"{prev_tag}rb")
        else:
            bcd.append("hsv")
        E("tensor", lambda e: e.matmul(
            PS[5][:, 0:ncols], SELBF[:, pr * 128:(pr + 1) * 128], RINV[:, :ncols],
            start=True, stop=True), inc="T", deps=bcd, tag=f"{tg}bc")
        E("vector", lambda e: e.tensor_copy(RNB[:, :ncols], PS[5][:, 0:ncols]),
          inc="V", deps=[f"{tg}bc"], tag=f"{tg}rb")
        E("vector", lambda e: write_fn(e), inc="V",
          deps=[f"{tg}rb"] + list(extra_done_deps), tag=f"{tg}done")

    prev_chain = None
    for t in range(NTILE):
        xt_ap = XT[t % 2][:, :]
        tg = f"a{t}"
        xdeps = [f"a{t-2}ln"] if t >= 2 else []
        E("sync", lambda e, t=t, xt_ap=xt_ap: e.dma_start(
            out=xt_ap, in_=xfull[t * 128:(t + 1) * 128, :]), dinc=f"X{t % 2}", deps=xdeps, tag=f"{tg}x")
        lnd = [f"a{t-1}tp7"] if t >= 1 else []
        emit_ln(tg, xt_ap, f"{tg}x", extra_ln_deps=lnd)
        prevu = [f"a{t-1}xltfree"] if t >= 1 else []
        emit_tp(tg, prevu)
        for pr in range(NPAIR):
            bank = PS[pr % 2]
            slot = (pr // 2) * 128
            mmd = [f"{tg}xlt", "DCALL"]
            if t >= 1:
                mmd.append(f"a{t-1}k{pr}done")
            if pr >= 2:
                mmd.append(f"{tg}k{pr-2}done")
            for fc in range(8):
                E("tensor", lambda e, pr=pr, fc=fc, bank=bank, slot=slot: e.matmul(
                    bank[:, slot:slot + 128],
                    WBA[:, fc * DIM + pr * 128:fc * DIM + (pr + 1) * 128],
                    XLT[:, fc * 128:(fc + 1) * 128],
                    start=(fc == 0), stop=(fc == 7)), inc="T",
                    deps=(mmd if fc == 0 else [f"{tg}xlt"]),
                    tag=(f"{tg}k{pr}mm" if fc == 7 else None))
            def wk_out(e, pr=pr, t=t, bank=bank, slot=slot):
                return e.tensor_tensor(KR[pr][:, NMEM + t * 128: NMEM + (t + 1) * 128],
                                       bank[:, slot:slot + 128], RNB[:, :128], Alu.mult)
            emit_l2norm(f"{tg}k{pr}", bank[:, slot:slot + 128], pr, 128, wk_out,
                        f"{tg}k{pr}mm", prev_chain)
            prev_chain = f"{tg}k{pr}"
        for hv in range(2):
            vmd = [f"{tg}xlt", "DCALL"]
            if t >= 1:
                vmd.append(f"a{t-1}v{hv}ev")
            for fc in range(8):
                E("tensor", lambda e, hv=hv, fc=fc: e.matmul(
                    PS[2 + hv][:, :], XLT[:, fc * 128:(fc + 1) * 128],
                    WBB[:, fc * DIM + hv * 512: fc * DIM + (hv + 1) * 512],
                    start=(fc == 0), stop=(fc == 7)), inc="T",
                    deps=(vmd if fc == 0 else [f"{tg}xlt"]),
                    tag=(f"{tg}v{hv}mm" if fc == 7 else None))
            evd = [f"{tg}v{hv}mm"]
            if t >= 1:
                evd.append(f"a{t-1}vsp")
            else:
                evd.append("D0BALL")
            E("scalar", lambda e, hv=hv: e.copy(VST[:, hv * 512:(hv + 1) * 512], PS[2 + hv][:, :]),
              inc="A", deps=evd, tag=f"{tg}v{hv}ev")
        E("sync", lambda e, t=t: e.dma_start(
            out=vscr[NMEM + t * 128: NMEM + (t + 1) * 128, :], in_=VST[:, :]),
            dinc="SPV", deps=[f"{tg}v0ev", f"{tg}v1ev"], tag=f"{tg}vsp")
        sc.alias(f"{tg}xltfree", f"{tg}v1mm")

    for pr in range(NPAIR):
        tg = f"mn{pr}"
        cpd = ["DCALL", f"a{NTILE-1}k{pr}done"]
        if pr >= 1:
            cpd.append(f"mn{pr-1}done")
        E("vector", lambda e, pr=pr: e.tensor_copy(MCP[:, :], KR[pr][:, 0:NMEM]),
          inc="V", deps=cpd, tag=f"{tg}cp")
        def wmem(e, pr=pr):
            return e.tensor_tensor(KR[pr][:, 0:NMEM], MCP[:, :], RNB[:, :NMEM], Alu.mult)
        emit_l2norm(tg, MCP[:, :], pr, NMEM, wmem, f"{tg}cp", prev_chain)
        prev_chain = tg

    for ck in range(8):
        E("gpsimd", lambda e, ck=ck: e.dma_start(
            out=WBA[:, ck * DIM:(ck + 1) * DIM], in_=wq[ck * 128:(ck + 1) * 128, :]),
            dinc="DW2", deps=[f"a{NTILE-1}k{pr}mm" for pr in range(NPAIR)], tag=f"wqc{ck}")
        E("gpsimd", lambda e, ck=ck: e.dma_start(
            out=WBB[:, ck * DIM:(ck + 1) * DIM], in_=wg[ck * 128:(ck + 1) * 128, :]),
            dinc="DW2", deps=[f"a{NTILE-1}v0mm", f"a{NTILE-1}v1mm"], tag=f"wgc{ck}")

    sc.tags["wqgall"] = [("DW2", 16 * 16)]

    for t in range(NQT):
        xt_ap = XT[t % 2][:, :]
        tg = f"b{t}"
        xdeps = [f"b{t-2}ln"] if t >= 2 else [f"a{NTILE-1}ln", f"mn{NPAIR-1}done"]
        E("sync", lambda e, t=t, xt_ap=xt_ap: e.dma_start(
            out=xt_ap, in_=xq[t * 128:(t + 1) * 128, :]), dinc=f"X{t % 2}", deps=xdeps, tag=f"{tg}x")
        lnd = [f"b{t-1}tp7"] if t >= 1 else [f"a{NTILE-1}tp7"]
        emit_ln(tg, xt_ap, f"{tg}x", extra_ln_deps=lnd)
        prevu = [f"b{t-1}xltfree"] if t >= 1 else [f"a{NTILE-1}xltfree"]
        emit_tp(tg, prevu)
        for pr in range(NPAIR):
            bank = PS[pr % 2]
            slot = (pr // 2) * 128
            mmd = [f"{tg}xlt", "wqgall"]
            if t >= 1:
                mmd.append(f"b{t-1}q{pr}sp")
            if pr >= 2:
                mmd.append(f"{tg}q{pr-2}done")
            elif t == 0:
                mmd.append(f"a{NTILE-1}k{6+pr}done")
            for fc in range(8):
                E("tensor", lambda e, pr=pr, fc=fc, bank=bank, slot=slot: e.matmul(
                    bank[:, slot:slot + 128],
                    WBA[:, fc * DIM + pr * 128:fc * DIM + (pr + 1) * 128],
                    XLT[:, fc * 128:(fc + 1) * 128],
                    start=(fc == 0), stop=(fc == 7)), inc="T",
                    deps=(mmd if fc == 0 else [f"{tg}xlt"]),
                    tag=(f"{tg}q{pr}mm" if fc == 7 else None))
            def wq_out(e, pr=pr, bank=bank, slot=slot):
                return e.tensor_tensor(QNT[:, :], bank[:, slot:slot + 128], RNB[:, :128], Alu.mult)
            ed = [f"b{t-1}q{NPAIR-1}sp"] if t >= 1 else []
            if pr >= 1:
                ed = [f"{tg}q{pr-1}sp"]
            emit_l2norm(f"{tg}q{pr}", bank[:, slot:slot + 128], pr, 128, wq_out,
                        f"{tg}q{pr}mm", prev_chain, extra_done_deps=ed)
            prev_chain = f"{tg}q{pr}"
            E("sync", lambda e, t=t, pr=pr: e.dma_start(
                out=qscr[pr, :, t * 128:(t + 1) * 128], in_=QNT[:, :]),
                dinc="SPQ", deps=[f"{tg}q{pr}done"], tag=f"{tg}q{pr}sp")
        for pr in range(NPAIR):
            bank = PS[2 + pr % 2]
            slot = (pr // 2) * 128
            mmd = [f"{tg}xlt", "wqgall"]
            if t >= 1:
                mmd.append(f"b{t-1}g{pr}sg")
            if pr >= 2:
                mmd.append(f"{tg}g{pr-2}sg")
            elif t == 0:
                mmd.append(f"a{NTILE-1}v{pr % 2}ev")
            for fc in range(8):
                E("tensor", lambda e, pr=pr, fc=fc, bank=bank, slot=slot: e.matmul(
                    bank[:, slot:slot + 128],
                    WBB[:, fc * DIM + pr * 128:fc * DIM + (pr + 1) * 128],
                    XLT[:, fc * 128:(fc + 1) * 128],
                    start=(fc == 0), stop=(fc == 7)), inc="T",
                    deps=(mmd if fc == 0 else [f"{tg}xlt"]),
                    tag=(f"{tg}g{pr}mm" if fc == 7 else None))
            sgd = [f"{tg}g{pr}mm", "DCALL"]
            if pr >= 1:
                sgd.append(f"{tg}g{pr-1}sp")
            elif t >= 1:
                sgd.append(f"b{t-1}g{NPAIR-1}sp")
            else:
                sgd.append(f"a{NTILE-1}vsp")
            E("scalar", lambda e, pr=pr, bank=bank, slot=slot: e.activation(
                VST[:, pr * 128:(pr + 1) * 128], bank[:, slot:slot + 128], AF.Sigmoid,
                bias=BGS[:, pr:pr + 1]), inc="A", deps=sgd, tag=f"{tg}g{pr}sg")
            E("sync", lambda e, t=t, pr=pr: e.dma_start(
                out=gscr[pr, :, t * 128:(t + 1) * 128], in_=VST[:, pr * 128:(pr + 1) * 128]),
                dinc="SPG", deps=[f"{tg}g{pr}sg"], tag=f"{tg}g{pr}sp")
        sc.alias(f"{tg}xltfree", f"{tg}g{NPAIR-1}mm")

    lb = f"b{NQT-1}"
    sc.tags["woall"] = [("DW2", 16 * 24)]
    wodeps = [f"{lb}q{pr}mm" for pr in range(NPAIR)]
    for ck in range(8):
        E("gpsimd", lambda e, ck=ck: e.dma_start(
            out=WBA[:, ck * DIM:(ck + 1) * DIM], in_=wo[ck * 128:(ck + 1) * 128, :]),
            dinc="DW2", deps=wodeps, tag=f"woc{ck}")
        E("vector", lambda e, ck=ck: e.tensor_scalar(
            WBA[:, ck * DIM:(ck + 1) * DIM], WBA[:, ck * DIM:(ck + 1) * DIM],
            HSV[:, ck:ck + 1], None, Alu.mult), inc="V",
            deps=["woall", "hsv"], tag=f"wos{ck}")

    sc.tags["spkall0"] = [("SPK0", 16 * 4)]
    sc.tags["spkall1"] = [("SPK1", 16 * 4)]

    NGRP = len(groups)

    def ginfo(n):
        w = [octet_width(n, i) for i in range(8)]
        return w, max(w) // 128

    def e_qs(n, pg):
        t = f"g{n}"
        qd = [f"b{n//2}q{pr}sp" for pr in range(NPAIR)]
        if pg is not None:
            qd.append(f"g{pg}qkdone")
        E("sync", lambda e, n=n: e.dma_start(
            out=QS.rearrange("p (a q) -> p a q", a=16)[0:64, 0::2, :],
            in_=qscr.rearrange("a p q -> p a q")[0:64, :, n * 64:(n + 1) * 64]),
            dinc="DQS", deps=qd + ["qz0", "qz1"], tag=f"{t}qsa")
        E("sync", lambda e, n=n: e.dma_start(
            out=QS.rearrange("p (a q) -> p a q", a=16)[64:128, 1::2, :],
            in_=qscr.rearrange("a p q -> p a q")[64:128, :, n * 64:(n + 1) * 64]),
            dinc="DQS", deps=[f"{t}qsa"], tag=f"{t}qs")

    def e_ez(n, first):
        t = f"g{n}"
        w, _ = ginfo(n)
        Wm = max(w)
        ezt = []
        for i in range(8):
            if w[i] < Wm:
                ezd = ([f"b{NQT-1}tp7", f"b{NQT-1}g{NPAIR-1}sg", f"b{NQT-1}g{NPAIR-1}sp",
                        f"b{NQT-1}g{NPAIR-1}mm", "kq8_7", "bdpost7"] if first else [])
                E("gpsimd", lambda e, i=i, wi=w[i], Wm=Wm: e.memset(
                    EB[:, i * KW + wi: i * KW + Wm], 0.0), inc="G",
                    deps=ezd, tag=f"{t}ez{i}")
                ezt.append(f"{t}ez{i}")
        return ezt

    def e_qk(n, jb, extra02):
        t = f"g{n}"
        for p in range(8):
            bank = PS[2 + p // 4]
            col = (p % 4) * 128
            mmd = [f"{t}qs", f"a{NTILE-1}k{p}done", f"mn{p}done", f"kz{p}"]
            if jb >= 1:
                mmd.append(f"{t}ev1_{jb-1}")
            else:
                mmd += extra02
            sc_tag = f"{t}qk{jb}" if p == 7 else None
            E("tensor", lambda e, p=p, jb=jb, bank=bank, col=col: e.matmul(
                bank[:, col:col + 128],
                KR[p][:, jb * 128:(jb + 1) * 128],
                QS[:, p * 128:(p + 1) * 128],
                start=True, stop=True),
                inc="T", deps=mmd, tag=sc_tag)

    def e_ev1(n, jb, pg, pNBg):
        t = f"g{n}"
        for bi in range(2):
            def ev1(e, jb=jb, bi=bi):
                src = PS[2 + bi].rearrange("p (hh i q) -> p hh i q", hh=8, i=8, q=8)
                dst = S1[jb % 2].rearrange("p (i hh q) -> p hh i q", i=8, hh=16, q=8)[
                    :, bi * 8:(bi + 1) * 8, :, :]
                return e.tensor_copy(dst, src)
            e1d = [f"{t}qk{jb}"]
            if jb >= 2:
                e1d.append(f"{t}t1d_{jb-2}")
            elif pg is not None:
                e1d.append(f"g{pg}t1d_{pNBg-2+jb}")
            E("vector", ev1, inc="V", deps=e1d, tag=f"{t}ev1b{jb}_{bi}")
        sc.alias(f"{t}ev1_{jb}", f"{t}ev1b{jb}_1")

    def e_t1(n, jb, extra_first):
        t = f"g{n}"
        w, _ = ginfo(n)
        live = [i for i in range(8) if w[i] > jb * 128]
        ptb = PT[jb % 2]
        for i in live:
            t1d = [f"{t}ev1_{jb}"]
            if i == live[0]:
                if jb >= 2:
                    t1d.append(f"{t}ev2_{jb-2}")
                else:
                    t1d += extra_first
            E("tensor", lambda e, i=i, jb=jb, ptb=ptb: e.transpose(
                ptb[:, i * 128:(i + 1) * 128], S1[jb % 2][:, i * 128:(i + 1) * 128], IDE[:, :]),
                inc="T", deps=t1d, tag=f"{t}t1_{jb}_{i}")
        sc.alias(f"{t}t1d_{jb}", f"{t}t1_{jb}_{live[-1]}")

    def e_ev2(n, jb, gT2, NBg, ezt):
        t = f"g{n}"
        w, _ = ginfo(n)
        live = [i for i in range(8) if w[i] > jb * 128]
        ptb = PT[jb % 2]
        def ev2(e, jb=jb, live=tuple(live), ptb=ptb):
            last = None
            for i in live:
                last = e.copy(EB[:, i * KW + jb * 128: i * KW + (jb + 1) * 128],
                              ptb[:, i * 128:(i + 1) * 128])
            return last
        e2d = [f"{t}t1_{jb}_{i}" for i in live] + (ezt if jb == 0 else [])
        if gT2 is not None and jb < NBg:
            e2d.append(f"g{gT2}t2ev_{jb}")
        E("scalar", ev2, inc="A", deps=e2d, tag=f"{t}ev2_{jb}")

    def e_B(n, pg):
        t = f"g{n}"
        w, _ = ginfo(n)
        for i in range(8):
            wi = w[i]
            nch = (wi + 511) // 512
            for ci in range(nch):
                c0 = ci * 512
                cw = min(512, wi - c0)
                bank = PS[4 + ((i + ci) % 2)]
                pmd = [f"{t}ev2_{(c0 + cw - 1) // 128}", "bdpre7"]
                if ci >= 2:
                    pmd.append(f"{t}exp{i}_{ci-2}")
                elif i >= 1:
                    pmd.append(f"{t}exp{i-1}_bank{(i + ci) % 2}last")
                elif pg is not None:
                    pmd.append(f"g{pg}gateA")
                    pmd.append(f"g{pg}gate")
                E("tensor", lambda e, i=i, c0=c0, cw=cw, bank=bank: e.matmul(
                    bank[:, :cw], BDPRE[:, :], EB[:, i * KW + c0: i * KW + c0 + cw],
                    start=True, stop=True), inc="T", deps=pmd, tag=f"{t}pm{i}_{ci}")
                E("scalar", lambda e, i=i, c0=c0, cw=cw, bank=bank: e.activation(
                    EB[:, i * KW + c0: i * KW + c0 + cw], bank[:, :cw], AF.Exp,
                    bias=MHATN[:, 0:1]), inc="A", deps=[f"{t}pm{i}_{ci}", "mhat"],
                    tag=f"{t}exp{i}_{ci}")
            sc.alias(f"{t}exp{i}_last", f"{t}exp{i}_{nch-1}")
            for bank in range(2):
                cis = [ci for ci in range(nch) if (i + ci) % 2 == bank]
                sc.alias(f"{t}exp{i}_bank{bank}last",
                         f"{t}exp{i}_{cis[-1]}" if cis else f"{t}exp{i}_{nch-1}")

    ezt_by_g = {}
    for gi, g in enumerate(groups):
        tg = f"g{g}"
        prevg = groups[gi - 1] if gi >= 1 else None
        nextg = groups[gi + 1] if gi + 1 < NGRP else None
        widths, NB = ginfo(g)
        Wmax = max(widths)
        if nextg is not None:
            nwidths, NBn = ginfo(nextg)

        if gi == 0:
            e_qs(g, None)
            ezt_by_g[g] = e_ez(g, True)
            for jb in range(NB):
                e_qk(g, jb, [])
                e_ev1(g, jb, None, 0)
                e_t1(g, jb, [])
                e_ev2(g, jb, None, 0, ezt_by_g[g])
            sc.alias(f"{tg}qkdone", f"{tg}qk{NB-1}")
            for jb in range(NB, NB + 2):
                sc.alias(f"{tg}ev2_{jb}", f"{tg}ev2_{NB-1}")
                sc.alias(f"{tg}t1d_{jb}", f"{tg}t1d_{NB-1}")
            e_B(g, None)

        S012 = widths[0] + widths[1] + widths[2]
        S47 = sum(widths[4:8])
        cs3 = int((0.9 * (widths[3] + S47) - 1.25 * S012 + 1000) / 2.15) // 64 * 64
        cs3 = max(64, min(widths[3], cs3))
        csplit = cs3 < widths[3]
        for i in range(8):
            wi = widths[i]
            q0w = octet_q0_worst(g, i)
            c0m = max(0, wi - 384)
            off = 384 - q0w - 5 + c0m
            mkd = [f"{tg}exp{i}_last", "DCALL"]
            E("vector", lambda e, i=i, c0m=c0m, off=off, wi=wi: e.tensor_tensor(
                EB[:, i * KW + c0m: i * KW + wi], EB[:, i * KW + c0m: i * KW + wi],
                MSK[:, off: off + wi - c0m], Alu.mult), inc="V", deps=mkd, tag=f"{tg}msk{i}")
            if i < 3 or (i == 3 and not csplit):
                cv = 64.0
            elif i == 3:
                cv = float(128 - (widths[3] - cs3))
            else:
                cv = float(128 - wi)
            cvd = [f"g{prevg}upd_{NROUND-1}"] if gi >= 1 else [f"b{NQT-1}g{NPAIR-1}mm"]
            E("gpsimd", lambda e, i=i, cv=cv: e.memset(CMPV[:, i:i + 1], cv), inc="G",
              deps=cvd, tag=f"{tg}cmpv{i}")
        lod = [f"g{prevg}stt7"] if gi >= 1 else []
        E("gpsimd", lambda e: e.memset(LO[:, :], 0), inc="G", deps=lod, tag=f"{tg}lo")
        zslots = []
        for i in range(4, 8):
            nch = (widths[i] + 2111) // 2112
            for ci in range(nch, 2):
                zslots.append((i - 3) * 2 + ci)
        zdep = [f"g{prevg}csum_{NROUND-1}"] if gi >= 1 else [f"b{NQT-1}g{NPAIR-1}mm"]
        if zslots:
            def zc(e, zslots=tuple(zslots)):
                last = None
                for sl in zslots:
                    last = e.memset(CNT4[:, sl:sl + 1], 0.0)
                return last
            E("gpsimd", zc, inc="G", deps=zdep, tag=f"{tg}cntz")

        if nextg is not None:
            e_qs(nextg, g)
            ezt_by_g[nextg] = e_ez(nextg, False)
            hx = ([f"g{prevg}ffnev0", f"g{prevg}ffnev1"]
                  if prevg is not None else [f"{tg}ev1_{NB-1}"])
            e_qk(nextg, 0, hx)
            e_ev1(nextg, 0, g, NB)
            e_qk(nextg, 1, [])
            e_ev1(nextg, 1, g, NB)
            t1x0 = [f"g{prevg}q8"] if prevg is not None else [f"{tg}ev2_{NB-1}"]
            e_t1(nextg, 0, t1x0)
            e_qk(nextg, 2, [])
            e_ev1(nextg, 2, g, NB)
            e_t1(nextg, 1, [f"{tg}ev2_{NB-1}"])
            e_qk(nextg, 3, [])
            e_ev1(nextg, 3, g, NB)

        prev = [f"{tg}lo"] + [f"{tg}msk{i}" for i in range(8)] + [f"{tg}cmpv{i}" for i in range(8)]
        if zslots:
            prev.append(f"{tg}cntz")
        if gi == 0:
            prev += [f"b{NQT-1}g{NPAIR-1}mm"]
        for r in range(NROUND):
            step = 1 << (13 - r)
            E("vector", lambda e, step=step: e.tensor_scalar(
                TF[:, :], LO[:, :], step, 65536, Alu.add, Alu.mult),
                inc="V", deps=prev, tag=f"{tg}thb_{r}")
            E("vector", lambda e: e.tensor_scalar(
                TFN[:, :], TF[:, :], 32768, -0x80000000, Alu.subtract, Alu.add),
                inc="V", deps=[f"{tg}thb_{r}"], tag=f"{tg}thd_{r}")
            cntdeps = []
            for i in range(4):
                wi = cs3 if (i == 3 and csplit) else widths[i]
                cd = [f"{tg}thd_{r}"]
                if i >= 1:
                    cd.append(f"{tg}cnt{i-1}_{r}")
                elif r >= 1:
                    cd.append(f"{tg}cnt3_{r-1}")
                E("vector", lambda e, i=i, wi=wi: e.tensor_scalar(
                    WBB[:, :wi], EB[:, i * KW: i * KW + wi], TF.bitcast(F32)[:, i:i + 1],
                    0.0, Alu.is_ge, Alu.add, accum_out=CNT[:, i:i + 1]),
                    inc="V", deps=cd, tag=f"{tg}cnt{i}_{r}")
                cntdeps.append(f"{tg}cnt{i}_{r}")
            prevact = f"{tg}ge_{r-1}" if r >= 1 else None
            aocts = ([3] if csplit else []) + [4, 5, 6, 7]
            nch3 = (widths[3] - cs3 + 2111) // 2112 if csplit else 0
            for i in aocts:
                a0 = cs3 if i == 3 else 0
                wi = widths[i] - a0
                nch = (wi + 2111) // 2112
                for ci in range(nch):
                    c0 = a0 + ci * 2112
                    cw = min(2112, a0 + wi - c0)
                    cd = [f"{tg}thd_{r}"]
                    if prevact:
                        cd.append(prevact)
                    E("scalar", lambda e, i=i, c0=c0, cw=cw, ci=ci: e.activation(
                        WBB[:, 4352:4352 + cw], EB[:, i * KW + c0: i * KW + c0 + cw],
                        AF.Sign, bias=TFN.bitcast(F32)[:, i:i + 1],
                        accum_out=CNT4[:, (i - 3) * 2 + ci: (i - 3) * 2 + ci + 1]),
                        inc="A", deps=cd, tag=f"{tg}cc{i}_{ci}_{r}")
                    prevact = f"{tg}cc{i}_{ci}_{r}"
                    cntdeps.append(prevact)
            v5 = CNT4.rearrange("p (o c) -> p o c", o=5)
            E("vector", lambda e, v5=v5: e.tensor_tensor(CNT[:, 4:8], v5[:, 1:5, 0], v5[:, 1:5, 1],
              Alu.add), inc="V", deps=cntdeps, tag=f"{tg}csum_{r}")
            gdep = [f"{tg}csum_{r}"]
            if csplit:
                if nch3 == 2:
                    E("vector", lambda e: e.tensor_tensor(CNT4[:, 0:1], CNT4[:, 0:1],
                      CNT4[:, 1:2], Alu.add), inc="V", deps=cntdeps, tag=f"{tg}s3_{r}")
                    s3d = [f"{tg}s3_{r}"]
                else:
                    s3d = cntdeps
                E("vector", lambda e: e.scalar_tensor_tensor(
                    CNT[:, 3:4], CNT[:, 3:4], 2.0, CNT4[:, 0:1], Alu.mult, Alu.add),
                    inc="V", deps=s3d + [f"{tg}cnt3_{r}"], tag=f"{tg}cb_{r}")
                gdep.append(f"{tg}cb_{r}")
            E("vector", lambda e: e.tensor_tensor(GEF[:, :], CNT[:, :], CMPV[:, :], Alu.is_ge),
              inc="V", deps=gdep, tag=f"{tg}ge_{r}")
            E("vector", lambda e, step=step: e.scalar_tensor_tensor(
                LO[:, :], GEF[:, :], step, LO[:, :], Alu.mult, Alu.add),
                inc="V", deps=[f"{tg}ge_{r}"], tag=f"{tg}upd_{r}")
            prev = [f"{tg}upd_{r}"]
        E("vector", lambda e: e.tensor_scalar(
            TF[:, :], LO[:, :], 16, None, Alu.logical_shift_left),
            inc="V", deps=prev, tag=f"{tg}ek")
        for i in range(8):
            wi = widths[i]
            E("vector", lambda e, i=i, wi=wi: e.scalar_tensor_tensor(
                EB[:, i * KW: i * KW + wi], EB[:, i * KW: i * KW + wi],
                TF.bitcast(F32)[:, i:i + 1], EB[:, i * KW: i * KW + wi],
                Alu.is_ge, Alu.mult, accum_out=SUM8[:, i:i + 1]),
                inc="V", deps=[f"{tg}ek"], tag=f"{tg}stt{i}")
            E("vector", lambda e, i=i: e.reciprocal(RS8[:, i:i + 1], SUM8[:, i:i + 1]),
              inc="V", deps=[f"{tg}stt{i}"], tag=f"{tg}rs{i}")
            bpd = [f"{tg}rs{i}", "bdpost7"]
            if gi >= 1:
                bpd.append(f"g{prevg}t2done{i}")
            E("vector", lambda e, i=i: e.tensor_scalar(
                BDP[i][:, :], BDPOST[:, :], RS8[:, i:i + 1], None, Alu.mult),
                inc="V", deps=bpd, tag=f"{tg}bdp{i}")

        for jb in range(min(3, NB)):
            vd = [f"a{NTILE-1}vsp", "D0BALL"]
            if gi >= 1:
                pj = pNB0 - 1 - ((pNB0 - 1 - jb) % 3)
                if pj >= 0:
                    vd.append(f"g{prevg}av_{pj}")
            E("sync", lambda e, jb=jb: e.dma_start(
                out=VS[jb % 3][:, :], in_=vscr[jb * 128:(jb + 1) * 128, :]),
                dinc=f"DV{jb % 3}", deps=vd, tag=f"{tg}vl_{jb}")
        LIM = max(NB, NBn) + 1 if nextg is not None else NB + 1
        for jb in range(LIM):
            if jb < NB:
                live = [i for i in range(8) if widths[i] > jb * 128]
                for i in live:
                    td = [f"{tg}bdp{i}"]
                    if i == live[0]:
                        if jb >= 1:
                            td.append(f"{tg}t2ev_{jb-1}")
                        elif prevg is not None:
                            td += [f"g{prevg}x1_0", f"g{prevg}x1_1"]
                    E("tensor", lambda e, i=i, jb=jb: e.matmul(
                        PS[i // 4][:, (i % 4) * 128:(i % 4) * 128 + 128],
                        EB[:, i * KW + jb * 128: i * KW + (jb + 1) * 128], BDP[i][:, :],
                        start=True, stop=True), inc="T", deps=td, tag=f"{tg}t2_{jb}_{i}")
                    if jb == widths[i] // 128 - 1:
                        sc.alias(f"{tg}t2done{i}", f"{tg}t2_{jb}_{i}")

                def evt2(e, jb=jb, live=tuple(live)):
                    last = None
                    for bi in range(2):
                        sub = [i for i in live if i // 4 == bi]
                        if not sub:
                            continue
                        if len(sub) == 4:
                            src = PS[bi].rearrange("p (q hh s) -> p hh q s", q=4, hh=16, s=8)
                            dst = ATT.rearrange("p (hh i s) -> p hh i s", hh=16, i=8, s=8)[
                                :, :, bi * 4:(bi + 1) * 4, :]
                            last = e.tensor_copy(dst, src)
                        else:
                            for i in sub:
                                src = PS[bi].rearrange("p (q hh s) -> p hh q s", q=4, hh=16, s=8)[
                                    :, :, i % 4, :]
                                dst = ATT.rearrange("p (hh i s) -> p hh i s", hh=16, i=8, s=8)[:, :, i, :]
                                last = e.tensor_copy(dst, src)
                    return last
                e2d = [f"{tg}t2_{jb}_{i}" for i in live]
                if jb >= 1:
                    e2d.append(f"{tg}av_{jb-1}")
                E("vector", evt2, inc="V", deps=e2d, tag=f"{tg}t2ev_{jb}")
                dead = [i for i in range(8) if i not in live]
                if dead:
                    def zatt(e, dead=tuple(dead)):
                        last = None
                        for i in dead:
                            last = e.memset(
                                ATT.rearrange("p (hh i s) -> p i hh s", hh=16, i=8, s=8)[:, i, :, :], 0.0)
                        return last
                    E("vector", zatt, inc="V", deps=[f"{tg}t2ev_{jb}"], tag=f"{tg}tz_{jb}")
            ja = jb - 1
            if 0 <= ja < NB:
                att_tag = f"{tg}tz_{ja}" if any(widths[i] <= ja * 128 for i in range(8)) \
                    else f"{tg}t2ev_{ja}"
                for h in range(H):
                    avd = [att_tag, f"{tg}vl_{ja}"]
                    if ja == 0 and h < 2:
                        avd.append(f"{tg}exp7_last")
                        if gi >= 1:
                            avd.append(f"g{prevg}gate")
                    par = h % 2
                    E("tensor", lambda e, h=h, ja=ja, NB=NB, par=par: e.matmul(
                        PS[4 + par][64 * par:64 * par + 64, (h // 2) * 64:(h // 2) * 64 + 64],
                        VS[ja % 3][:, h * 64:(h + 1) * 64], ATT[:, h * 64:(h + 1) * 64],
                        start=(ja == 0 and h < 2), stop=(ja == NB - 1 and h >= H - 2),
                        tile_position=(0, 64 * par), skip_group_check=True),
                        inc="T", deps=avd, tag=(f"{tg}av_{ja}" if h == H - 1 else None))
                if ja + 3 < NB:
                    E("sync", lambda e, ja=ja: e.dma_start(
                        out=VS[(ja + 3) % 3][:, :], in_=vscr[(ja + 3) * 128:(ja + 4) * 128, :]),
                        dinc=f"DV{(ja + 3) % 3}", deps=[f"{tg}av_{ja}"], tag=f"{tg}vl_{ja+3}")
            if nextg is not None:
                if 4 <= jb < NBn:
                    e_qk(nextg, jb, [])
                    e_ev1(nextg, jb, g, NB)
                if 2 <= ja < NBn:
                    e_t1(nextg, ja, [])
                if 0 <= ja < NBn:
                    e_ev2(nextg, ja, g, NB, ezt_by_g[nextg])
        if nextg is not None:
            tn = f"g{nextg}"
            sc.alias(f"{tn}qkdone", f"{tn}qk{NBn-1}")
            for jb in range(NBn, NBn + 2):
                sc.alias(f"{tn}ev2_{jb}", f"{tn}ev2_{NBn-1}")
                sc.alias(f"{tn}t1d_{jb}", f"{tn}t1d_{NBn-1}")
        sc.alias(f"{tg}avdone", f"{tg}av_{NB-1}")
        for jb in range(NB, NB + 4):
            sc.alias(f"{tg}av_{jb}", f"{tg}avdone")
        sc.alias(f"{tg}av_m1", f"{tg}avdone")

        gd = [f"b{g//2}g{pr}sp" for pr in range(NPAIR)]
        if gi >= 1:
            gd.append(f"g{prevg}gate")
        E("sync", lambda e, g=g: e.dma_start(
            out=GS.rearrange("p (a q) -> p a q", a=8),
            in_=gscr.rearrange("a p q -> p a q")[:, :, g * 64:(g + 1) * 64]),
            dinc="DGS", deps=gd, tag=f"{tg}gs")
        xrd = [f"g{prevg}ost"] if gi >= 1 else []
        E("sync", lambda e, g=g: e.dma_start(out=XRES[:, :], in_=xq[g * 64:(g + 1) * 64, :]),
          dinc="DXR", deps=xrd, tag=f"{tg}xr")
        E("vector", lambda e: e.tensor_tensor(AOUT[0:64, :], PS[4][0:64, :], GS[0:64, :], Alu.mult),
          inc="V", deps=[f"{tg}avdone", f"{tg}gs"], tag=f"{tg}gateA")
        E("vector", lambda e: e.tensor_tensor(AOUT[64:128, :], PS[5][64:128, :], GS[64:128, :], Alu.mult),
          inc="V", deps=[f"{tg}avdone", f"{tg}gs"], tag=f"{tg}gate")

        if nextg is not None:
            e_B(nextg, g)

        for hv in range(2):
            o2d = [f"{tg}gate", "wos7", f"{tg}t2ev_{NB-1}"]
            for ck in range(8):
                E("tensor", lambda e, hv=hv, ck=ck: e.matmul(
                    PS[hv][:64, :], AOUT[:, ck * 64:(ck + 1) * 64],
                    WBA[:, ck * DIM + hv * 512: ck * DIM + (hv + 1) * 512],
                    start=(ck == 0), stop=(ck == 7)), inc="T",
                    deps=(o2d if ck == 0 else []),
                    tag=(f"{tg}o2_{hv}" if ck == 7 else None))
            E("vector", lambda e, hv=hv: e.tensor_tensor(
                X1[:, hv * 512:(hv + 1) * 512], PS[hv][:64, :], XRES[:, hv * 512:(hv + 1) * 512],
                Alu.add), inc="V", deps=[f"{tg}o2_{hv}", f"{tg}xr"], tag=f"{tg}x1_{hv}")

        t2 = f"{tg}L"
        E("vector", lambda e: e.tensor_reduce(
            STAT[:64, 0:1], X1[:64, :], axis=mybir.AxisListType.X, op=Alu.add),
            inc="V", deps=[f"{tg}x1_0", f"{tg}x1_1"], tag=f"{t2}sum")
        E("scalar", lambda e: e.activation(YB[:64, :], X1[:64, :], AF.Square,
          accum_out=STAT[:64, 1:2]), inc="A", deps=[f"{tg}x1_0", f"{tg}x1_1"], tag=f"{t2}sq")

        E("vector", lambda e: e.tensor_scalar(STAT[:64, 0:1], STAT[:64, 0:1], 1.0 / DIM, None, Alu.mult),
          inc="V", deps=[f"{t2}sum"], tag=f"{t2}mean")
        E("vector", lambda e: e.tensor_scalar(STAT[:64, 2:3], STAT[:64, 1:2], 1.0 / DIM, None, Alu.mult),
          inc="V", deps=[f"{t2}sq"], tag=f"{t2}ex2")
        E("vector", lambda e: e.tensor_tensor(STAT[:64, 3:4], STAT[:64, 0:1], STAT[:64, 0:1], Alu.mult),
          inc="V", deps=[f"{t2}mean"], tag=f"{t2}m2")
        E("vector", lambda e: e.tensor_tensor(STAT[:64, 2:3], STAT[:64, 2:3], STAT[:64, 3:4], Alu.subtract),
          inc="V", deps=[f"{t2}ex2", f"{t2}m2"], tag=f"{t2}var")
        E("scalar", lambda e: e.activation(STAT[:64, 4:5], STAT[:64, 2:3], AF.Ln, bias=EPSV[:64, 0:1]),
          inc="A", deps=[f"{t2}var", "epsv"], tag=f"{t2}sd")
        E("scalar", lambda e: e.activation(STAT[:64, 5:6], STAT[:64, 4:5], AF.Exp, scale=-0.5),
          inc="A", deps=[f"{t2}sd"], tag=f"{t2}rstd")
        E("vector", lambda e: e.tensor_scalar(
            H2B[:64, :], X1[:64, :], STAT[:64, 0:1], STAT[:64, 5:6], Alu.subtract, Alu.mult),
            inc="V", deps=[f"{t2}rstd"], tag=f"{t2}ln")
        for fc in range(8):
            tpd = [f"{t2}ln"]
            if fc == 0:
                if nextg is not None:
                    tpd.append(f"g{nextg}ev2_{NBn-1}")
                else:
                    tpd.append(f"{tg}ev2_{NB-1}")
            E("tensor", lambda e, fc=fc: e.transpose(
                PT[0][:, fc * 64:(fc + 1) * 64], H2B[:64, fc * 128:(fc + 1) * 128], IDE[:64, :64]),
                inc="T", deps=tpd, tag=f"{t2}tp{fc}")
        q8d = [f"{t2}tp{fc}" for fc in range(8)]
        if gi >= 1:
            q8d.append(f"g{prevg}ffnmm1")
        E("vector", lambda e: e.tensor_scalar(
            XQ8[:, :], PT[0][:, :512], SREC[:, 2:3], None, Alu.mult),
            inc="V", deps=q8d, tag=f"{tg}q8")
        for ck in range(8):
            kld = ["spkall0", "spkall1"]
            if ck >= 2:
                kld.append(f"{tg}fm{ck-2}_1")
            elif gi >= 1:
                kld.append(f"g{prevg}fm{6+ck}_1")
            E("sync", lambda e, ck=ck: e.dma_start(
                out=KQ8R[:, (ck % 2) * DIM:(ck % 2 + 1) * DIM], in_=kscr[ck, :, :]),
                dinc=f"DK{ck % 2}", deps=kld, tag=f"{tg}kl{ck}")
            for hv in range(2):
                fd = [f"{tg}q8", f"{tg}kl{ck}"]
                if ck == 0:
                    fd.append(f"{tg}o2_{hv}")
                    if nextg is not None:
                        fd.append(f"g{nextg}ev1_{NBn-1}")
                E("tensor", lambda e, hv=hv, ck=ck: e.matmul(
                    PS[2 + hv][:64, :], XQ8[:, ck * 64:(ck + 1) * 64],
                    KQ8R[:, (ck % 2) * DIM + hv * 512: (ck % 2) * DIM + (hv + 1) * 512],
                    start=(ck == 0), stop=(ck == 7)), inc="T",
                    deps=fd, tag=f"{tg}fm{ck}_{hv}")
        for hv in range(2):
            sc.alias(f"{tg}ffnmm{hv}", f"{tg}fm7_{hv}")
            fe = [f"{tg}ffnmm{hv}", "sprod"]
            if gi >= 1:
                fe.append(f"g{prevg}osum")
            E("vector", lambda e, hv=hv: e.tensor_scalar(
                YB[:64, hv * 512:(hv + 1) * 512], PS[2 + hv][:64, :], SPROD[:64, 0:1], None,
                Alu.mult), inc="V", deps=fe, tag=f"{tg}ffnev{hv}")
        E("vector", lambda e: e.tensor_tensor(YB[:64, :], YB[:64, :], BIAS64[:, :], Alu.add),
          inc="V", deps=[f"{tg}ffnev0", f"{tg}ffnev1", "bias64"], tag=f"{tg}yb")
        od = [f"{tg}yb"]
        E("vector", lambda e: e.tensor_tensor(XRES[:64, :], X1[:64, :], YB[:64, :], Alu.add),
          inc="V", deps=od, tag=f"{tg}osum")
        E("sync", lambda e, g=g: e.dma_start(
            out=out[g * 64:(g + 1) * 64, :], in_=XRES[:64, :]), dinc="SPO",
            deps=[f"{tg}osum"], tag=f"{tg}ost")
        pNB0 = NB

    sc.tags["FINAL"] = [(nm, sc.cnt[nm]) for nm in ("SPO", "SPV", "SPQ", "SPG", "SPK0", "SPK1")]
    sc.steps["sync"].append((("FINAL",), lambda e: None, None, 0, 0))
    sc.build()
    return nc


def _shard_inputs(inputs):
    x = np.asarray(inputs["L0_attn_x"])
    owns = [np.concatenate([np.arange(8 * o, 8 * o + 8) for o in range(h, 512, 2)])
            for h in (0, 1)]
    shared = {
        "wq": np.asarray(inputs["L0_attn_Wq"]),
        "wk": np.asarray(inputs["L0_attn_Wk"]),
        "wv": np.asarray(inputs["L0_attn_Wv"]),
        "wg": np.asarray(inputs["L0_attn_Wg"]),
        "wo": np.asarray(inputs["L0_attn_Wo"]),
        "wpre": np.asarray(inputs["L0_attn_W_pre_th"]),
        "wpost": np.asarray(inputs["L0_attn_W_post_th"]),
        "memk": np.asarray(inputs["L0_attn_mem_k"]),
        "memv": np.asarray(inputs["L0_attn_mem_v"]),
        "hsc": np.ascontiguousarray(np.asarray(inputs["L0_attn_head_scale_params"]).reshape(H, 1)),
        "bg": np.asarray(inputs["L0_attn_bg"]),
        "fker": np.asarray(inputs["L0_ffn_kernel"]),
        "fbias": np.ascontiguousarray(np.asarray(inputs["L0_ffn_bias"]).reshape(1, DIM)),
        "s_in": np.ascontiguousarray(np.asarray(inputs["L0_ffn_input_scale"]).reshape(1, 1)),
        "s_k": np.ascontiguousarray(np.asarray(inputs["L0_ffn_kernel_scale"]).reshape(1, 1)),
    }
    cc = [host_consts(0), host_consts(1)]
    maps = []
    for c in range(8):
        b, half = c // 2, c % 2
        m = dict(shared)
        m["xfull"] = np.ascontiguousarray(x[b])
        m["xq"] = np.ascontiguousarray(x[b][owns[half]])
        m.update(cc[half])
        maps.append(m)
    return maps, owns


_NC_CACHE = {}


def kernel(**inputs):
    if "prog" not in _NC_CACHE:
        _NC_CACHE["prog"] = build_program()
    nc = _NC_CACHE["prog"]
    maps, owns = _shard_inputs(inputs)
    res = run_bass_kernel_spmd(nc, maps, core_ids=list(range(8)))
    outp = np.zeros((B, N, DIM), dtype=np.float32)
    for c in range(8):
        b, half = c // 2, c % 2
        outp[b][owns[half]] = res.results[c]["out"]
    return outp


# version 16
# speedup vs baseline: 1.2086x; 1.2086x over previous
import numpy as np
import ml_dtypes
import concourse.bass as bass
import concourse.mybir as mybir
from concourse.alu_op_type import AluOpType
from concourse.bass_utils import run_bass_kernel_spmd

F32 = mybir.dt.float32
BF16 = mybir.dt.bfloat16
I32 = mybir.dt.int32
F8 = mybir.dt.float8e4
AF = mybir.ActivationFunctionType
Alu = AluOpType

B, N, DIM = 4, 4096, 1024
H, DH, NPAIR = 16, 64, 8
NMEM, TOPK = 4, 64
QK_SCALE = 10.0
NTOK = N + NMEM
KW = 4224
NG = 32
OWNQ = 2048
NTILE = N // 128
NQT = OWNQ // 128
EPS_LN = 1e-5
NROUND = 14


def octet_q0_worst(g, i):
    return 128 * g + 16 * i + 8


def octet_width(g, i):
    return min(KW, ((octet_q0_worst(g, i) + 13 + 127) // 128) * 128)


class Sched:
    def __init__(self, nc, sem_names):
        self.nc = nc
        self.sems = {}
        self.cnt = {}
        self.tags = {}
        self.order = ("sync", "gpsimd", "vector", "scalar", "tensor")
        self.steps = {e: [] for e in self.order}
        self._h = []
        for nm in sem_names:
            h = nc.alloc_semaphore(name=nm)
            self.sems[nm] = h
            self.cnt[nm] = 0
            self._h.append(h)

    def sem_range(self):
        nums = sorted(h.num for h in self._h)
        assert nums == list(range(nums[0], nums[0] + len(nums))), nums
        return range(nums[0], nums[-1] + 1)

    def emit(self, eng, fn, inc=None, dinc=None, deps=(), tag=None):
        sem, amt = (inc, 1) if inc is not None else ((dinc, 16) if dinc is not None else (None, 0))
        if sem is not None:
            self.cnt[sem] += amt
            if tag is not None:
                self.tags[tag] = [(sem, self.cnt[sem])]
        else:
            assert tag is None
        self.steps[eng].append((tuple(deps), fn, sem, self.cnt[sem] if sem else 0, amt))

    def alias(self, newtag, oldtag):
        self.tags[newtag] = self.tags[oldtag]

    def build(self):
        nc, sems = self.nc, self.sems

        def run_stream(e, name):
            seen = {}
            for deps, fn, sem, val, amt in self.steps[name]:
                best = {}
                for d in deps:
                    v = self.tags[d]
                    for (ws, wv) in (v if isinstance(v, list) else [v]):
                        if best.get(ws, 0) < wv:
                            best[ws] = wv
                for ws, wv in best.items():
                    if seen.get(ws, 0) < wv:
                        e.wait_ge(sems[ws], wv)
                        seen[ws] = wv
                ins = fn(e)
                if sem is not None:
                    assert ins is not None
                    ins.then_inc(sems[sem], amt)

        with nc.Block() as blk:
            @blk.sync
            def _(e):
                run_stream(e, "sync")

            @blk.gpsimd
            def _(e):
                run_stream(e, "gpsimd")

            @blk.vector
            def _(e):
                run_stream(e, "vector")

            @blk.scalar
            def _(e):
                run_stream(e, "scalar")

            @blk.tensor
            def _(e):
                run_stream(e, "tensor")


def host_consts(half):
    ident = np.eye(128, dtype=ml_dtypes.bfloat16)
    tpl = np.zeros((128, 528), dtype=ml_dtypes.bfloat16)
    for p in range(128):
        tpl[p, : 384 + (p % 8) - (8 if half == 0 else 0)] = 1.0
    selpair = np.zeros((8, 128, 16), dtype=ml_dtypes.bfloat16)
    selb = np.zeros((8, 16, 128), dtype=ml_dtypes.bfloat16)
    for pr in range(8):
        for p in range(128):
            selpair[pr, p, 2 * pr + (p >= 64)] = 1.0
            selb[pr, 2 * pr + (p >= 64), p] = 1.0
    selh = np.zeros((16, 128), dtype=ml_dtypes.bfloat16)
    for p in range(128):
        selh[p // 8, p] = 1.0
    qsel = np.zeros((128, 128), dtype=ml_dtypes.bfloat16)
    for p in range(128):
        for n in range(128):
            if p % 8 == n % 8:
                qsel[p, n] = 1.0
    return {"c_ident": ident, "c_msk": tpl, "c_selpair": selpair,
            "c_selbf": selb.astype(np.float32), "c_selh": selh, "c_qsel": qsel}


def build_program(taps=(), groups=None, p2stage=16):
    if groups is None:
        groups = range(NG)
    groups = list(groups)
    nc = bass.Bass("TRN2", target_bir_lowering=False)

    def din(name, shape, dt):
        return nc.dram_tensor(name, shape, dt, kind="ExternalInput").ap()

    xfull = din("xfull", [N, DIM], F32)
    xq = din("xq", [OWNQ, DIM], F32)
    wq, wk, wv, wg, wo = (din(n, [DIM, DIM], F32) for n in ("wq", "wk", "wv", "wg", "wo"))
    wpre = din("wpre", [16, 16], F32)
    wpost = din("wpost", [16, 16], F32)
    memk = din("memk", [H, NMEM, DH], F32)
    memv = din("memv", [H, NMEM, DH], F32)
    hsc = din("hsc", [H, 1], F32)
    bg = din("bg", [DIM], F32)
    fker = din("fker", [DIM, DIM], BF16)
    fbias = din("fbias", [1, DIM], BF16)
    sin_ = din("s_in", [1, 1], F32)
    sk_ = din("s_k", [1, 1], F32)
    c_ident = din("c_ident", [128, 128], BF16)
    c_msk = din("c_msk", [128, 528], BF16)
    c_selpair = din("c_selpair", [8, 128, 16], BF16)
    c_selbf = din("c_selbf", [8, 16, 128], F32)
    c_selh = din("c_selh", [16, 128], BF16)
    c_qsel = din("c_qsel", [128, 128], BF16)

    out = nc.dram_tensor("out", [OWNQ, DIM], F32, kind="ExternalOutput").ap()
    def kind_of(n):
        return "ExternalOutput" if n in taps else "Internal"
    vscr = nc.dram_tensor("vscr", [KW, DIM], BF16, kind=kind_of("vscr")).ap()
    qscr = nc.dram_tensor("qscr", [NPAIR, 128, OWNQ], BF16, kind=kind_of("qscr")).ap()
    gscr = nc.dram_tensor("gscr", [NPAIR, 128, OWNQ], BF16, kind=kind_of("gscr")).ap()
    kscr = nc.dram_tensor("kscr", [8, 128, DIM], F8, kind="Internal").ap()

    def S(name, shape, dt):
        return nc.alloc_sbuf_tensor(name, shape, dt).ap()

    KR = [S(f"KR{p}", [128, KW], BF16) for p in range(NPAIR)]
    EB = S("EB", [128, 8 * KW], BF16)
    WBA = S("WBA", [128, 8 * DIM], BF16)
    WBB = S("WBB", [128, 8 * DIM], BF16)
    KQ8R = S("KQ8R", [128, 2 * DIM], F8)
    IDE = S("IDE", [128, 128], BF16)
    MSK = S("MSK", [128, 528], BF16)
    SELP = S("SELP", [128, 8 * 16], BF16)
    SELBF = S("SELBF", [16, 8 * 128], F32)
    SELH = S("SELH", [16, 128], BF16)
    QSEL = EB[:, 7 * KW: 7 * KW + 128]
    WREP = EB[:16, 7 * KW + 128: 7 * KW + 256]
    BDPRE = S("BDPRE", [128, 128], BF16)
    BDPOST = S("BDPOST", [128, 128], BF16)
    BDP = [S(f"BDP{i}", [128, 128], BF16) for i in range(8)]
    MHATN = S("MHATN", [128, 1], F32)
    W16 = S("W16", [16, 16], F32)
    W16B = S("W16B", [16, 16], BF16)
    WPR = S("WPR", [16, 16], F32)
    WPOB = S("WPOB", [16, 16], F32)
    ABSW = S("ABSW", [16, 16], F32)
    MH16 = S("MH16", [16, 1], F32)
    MH16B = S("MH16B", [16, 1], BF16)
    RINV = EB.bitcast(F32)[:16, 896:1152]
    HS16 = S("HS16", [16, 1], F32)
    HSV = S("HSV", [128, 8], F32)
    BGS = S("BGS", [128, 8], F32)
    BIAS64 = S("BIAS64", [64, DIM], BF16)
    SPROD = S("SPROD", [128, 1], F32)
    EPSV = S("EPSV", [128, 1], F32)
    ONESF = S("ONESF", [1, 128], F32)
    ONESB = S("ONESB", [1, 64], BF16)
    SREC = S("SREC", [128, 4], F32)
    XT = [EB[:, (5 + i) * KW: (5 + i) * KW + 2 * DIM].bitcast(F32) for i in range(2)]
    XLN = EB[:, 0 * KW: 0 * KW + DIM]
    XSQ = EB[:, 1 * KW: 1 * KW + DIM]
    XLT = EB[:, 2 * KW: 2 * KW + DIM]
    STAT = S("STAT", [128, 8], F32)
    SQT = EB[:, 1024:1280]
    RNB = EB.bitcast(F32)[:, 640:896]
    MCP = S("MCP", [128, NMEM], F32)
    E24 = S("E24", [16, 1], F32)
    VST = EB[:, 3 * KW: 3 * KW + DIM]
    QNT = S("QNT", [128, 128], BF16)
    FKT = EB[:, 4 * KW: 4 * KW + DIM]
    S1 = [S(f"S1_{i}", [128, 1024], BF16) for i in range(2)]
    QS = S("QS", [128, 1024], BF16)
    VS = [S(f"VS{i}", [128, DIM], BF16) for i in range(3)]
    ATT = S("ATT", [128, 1024], BF16)
    LO = S("LO", [128, 8], I32)
    TF = S("TF", [128, 8], I32)
    TFN = S("TFN", [128, 8], I32)
    CMPV = WBB.bitcast(F32)[:, 4078:4086]
    CNT4 = WBB.bitcast(F32)[:, 4086:4096]
    CNT = S("CNT", [128, 8], F32)
    GEF = S("GEF", [128, 8], I32)
    SUM8 = S("SUM8", [128, 8], F32)
    RS8 = S("RS8", [128, 8], F32)
    GS = S("GS", [128, 512], BF16)
    XRES = S("XRES", [64, DIM], F32)
    X1 = S("X1", [64, DIM], F32)
    H2B = S("H2B", [64, DIM], BF16)
    XQ8 = S("XQ8", [128, 512], F8)
    AOUT = S("AOUT", [128, 512], BF16)
    YB = S("YB", [64, DIM], BF16)

    PS = [nc.alloc_psum_tensor(f"ps{i}", [128, 512], F32).ap() for i in range(6)]
    PT = [nc.alloc_psum_tensor(f"pt{i}", [128, 1024], BF16).ap() for i in range(2)]

    sc = Sched(nc, ["DC", "DCG", "D0B", "D0BG", "DF", "SPK0", "SPK1", "X0", "X1", "SPV", "SPQ", "SPG",
                "DW2", "DQS", "DGS", "DXR", "DV0", "DV1", "DV2", "DK0", "DK1", "SPO",
                "T", "V", "A", "G"])
    nc.gpsimd.dma_reset(sc.sem_range())
    nc.gpsimd.sem_clear(sc.sem_range())
    nc.all_engine_barrier()
    E = sc.emit

    for wnm, (wt, wdst) in [("wk", (wk, WBA)), ("wv", (wv, WBB))]:
        for ck in range(8):
            E("gpsimd", lambda e, wt=wt, wdst=wdst, ck=ck: e.dma_start(
                out=wdst[:, ck * DIM:(ck + 1) * DIM], in_=wt[ck * 128:(ck + 1) * 128, :]),
                dinc="DCG", tag=f"{wnm}c{ck}")
    E("sync", lambda e: e.dma_start(out=IDE[:, :], in_=c_ident[:, :]), dinc="DC", tag="ident")
    E("sync", lambda e: e.dma_start(out=MSK[:, :], in_=c_msk[:, :]), dinc="DC", tag="msk")
    E("sync", lambda e: e.dma_start(
        out=SELP.rearrange("p (a m) -> p a m", a=8), in_=c_selpair.rearrange("a p m -> p a m")),
        dinc="DC", tag="selp")
    E("sync", lambda e: e.dma_start(
        out=SELBF.rearrange("h (a p) -> h a p", a=8), in_=c_selbf.rearrange("a h p -> h a p")),
        dinc="DC", tag="selbf")
    E("sync", lambda e: e.dma_start(out=SELH[:, :], in_=c_selh[:, :]), dinc="DC", tag="selh")
    E("sync", lambda e: e.dma_start(out=QSEL[:, :], in_=c_qsel[:, :]), dinc="DC", tag="qsel")
    E("sync", lambda e: e.dma_start(out=HS16[:, :], in_=hsc[:, :]), dinc="DC", tag="hs")
    for ck in range(8):
        E("sync", lambda e, ck=ck: e.dma_start(
            out=BGS[:, ck:ck + 1], in_=bg.rearrange("(c p) -> c p", p=128)[ck].unsqueeze(1)),
            dinc="DC", tag=(f"bgs" if ck == 7 else f"bgs{ck}"))
    E("sync", lambda e: e.dma_start(out=SREC[:1, 0:1], in_=sin_[:, :]), dinc="DC", tag="sin")
    E("sync", lambda e: e.dma_start(out=SREC[:1, 1:2], in_=sk_[:, :]), dinc="DC", tag="sk")
    E("sync", lambda e: e.dma_start(out=BIAS64[:1, :], in_=fbias[:, :]), dinc="DC", tag="fb0")
    E("sync", lambda e: e.dma_start(out=W16[:, :], in_=wpre.rearrange("a b -> b a"), allow_slow_non_contiguous=True), dinc="DC", tag="wpreT")
    E("sync", lambda e: e.dma_start(out=WPR[:, :], in_=wpre[:, :]), dinc="DC", tag="wpre")
    E("sync", lambda e: e.dma_start(out=WPOB[:, :], in_=wpost.rearrange("a b -> b a"), allow_slow_non_contiguous=True), dinc="DC", tag="wpostT")
    for h in range(H):
        E("gpsimd", lambda e, h=h: e.dma_start(
            out=KR[h // 2][64 * (h % 2):64 * (h % 2) + 64, 0:NMEM],
            in_=memk[h].rearrange("a b -> b a"), allow_slow_non_contiguous=True),
            dinc="DCG", tag=f"memk{h}")
        E("gpsimd", lambda e, h=h: e.dma_start(
            out=VST[0:NMEM, h * 64:(h + 1) * 64], in_=memv[h, :, :]), dinc="DCG", tag=f"memv{h}")
    sc.tags["DCALL"] = [("DC", sc.cnt["DC"]), ("DCG", sc.cnt["DCG"])]
    sc.alias("memvall", "DCALL")

    E("vector", lambda e: e.tensor_scalar(W16B[:, :], W16[:, :], QK_SCALE, None, Alu.mult),
      inc="V", deps=["DCALL"], tag="w16b")
    E("vector", lambda e: e.tensor_copy(
        WREP.rearrange("g (h r) -> g h r", r=8),
        W16B.rearrange("g h -> g h 1" if False else "g (h o) -> g h o", o=1).to_broadcast([16, 16, 8])),
        inc="V", deps=["w16b"], tag="wrepA")
    E("tensor", lambda e: e.matmul(PS[5][:, 128:256], SELH[:, :], WREP[:, :],
      start=True, stop=True), inc="T", deps=["wrepA", "DCALL"], tag="bdpremm")
    E("vector", lambda e: e.tensor_tensor(BDPRE[:, :], PS[5][:, 128:256], QSEL[:, :], Alu.mult),
      inc="V", deps=["bdpremm"], tag="bdpre7")
    E("vector", lambda e: e.tensor_copy(W16B[:, :], WPOB[:, :]), inc="V",
      deps=["DCALL", "bdpre7"], tag="w16post")
    E("vector", lambda e: e.tensor_copy(
        WREP.rearrange("g (h r) -> g h r", r=8),
        W16B.rearrange("g (h o) -> g h o", o=1).to_broadcast([16, 16, 8])),
        inc="V", deps=["w16post"], tag="wrepB")
    E("tensor", lambda e: e.matmul(PS[5][:, 128:256], SELH[:, :], WREP[:, :],
      start=True, stop=True), inc="T", deps=["wrepB", "bdpre7"], tag="bdpostmm")
    E("vector", lambda e: e.tensor_tensor(BDPOST[:, :], PS[5][:, 128:256], QSEL[:, :], Alu.mult),
      inc="V", deps=["bdpostmm"], tag="bdpost7")
    E("scalar", lambda e: e.activation(ABSW[:, :], WPR[:, :], AF.Abs), inc="A",
      deps=["DCALL"], tag="absw")
    E("vector", lambda e: e.tensor_reduce(MH16[:, 0:1], ABSW[:, :],
      axis=mybir.AxisListType.X, op=Alu.add), inc="V", deps=["absw"], tag="mh16")
    E("vector", lambda e: e.tensor_copy(MH16B[:, :], MH16[:, 0:1]), inc="V",
      deps=["mh16"], tag="mh16b")
    E("tensor", lambda e: e.matmul(PS[5][:, 0:1], SELH[:, :], MH16B[:, 0:1], start=True, stop=True),
      inc="T", deps=["mh16b", "DCALL"], tag="mhbc")
    E("vector", lambda e: e.tensor_scalar(MHATN[:, 0:1], PS[5][:, 0:1], -QK_SCALE, None, Alu.mult),
      inc="V", deps=["mhbc"], tag="mhat")
    for ck in range(8):
        E("tensor", lambda e, ck=ck: e.matmul(
            PS[5][:, 64 + ck:65 + ck], SELBF[:, ck * 128:(ck + 1) * 128], HS16[:, 0:1],
            start=True, stop=True), inc="T", deps=["DCALL", "DCALL", "mhat"],
            tag=f"hsb{ck}")
    E("vector", lambda e: e.tensor_copy(HSV[:, :], PS[5][:, 64:72]), inc="V",
      deps=[f"hsb{c}" for c in range(8)], tag="hsv")
    E("gpsimd", lambda e: e.memset(ONESF[:, :], 1.0), inc="G", tag="onesf")
    E("gpsimd", lambda e: e.memset(EPSV[:, :], EPS_LN), inc="G", tag="epsv")
    E("gpsimd", lambda e: e.memset(E24[:, :], 1e-24), inc="G", tag="eps24")
    E("gpsimd", lambda e: e.memset(ONESB[:, :], 1.0), inc="G", tag="onesb")
    E("gpsimd", lambda e: e.memset(
        QS.rearrange("p (a q) -> p a q", a=16)[64:128, 0::2, :], 0.0), inc="G", tag="qz0")
    E("gpsimd", lambda e: e.memset(
        QS.rearrange("p (a q) -> p a q", a=16)[0:64, 1::2, :], 0.0), inc="G", tag="qz1")
    E("tensor", lambda e: e.matmul(PS[4][:, 0:2], ONESF[:, :], SREC[:1, 0:2],
      start=True, stop=True), inc="T", deps=["DCALL", "DCALL", "onesf"], tag="sbcmm")
    E("vector", lambda e: e.tensor_copy(SREC[:, 0:2], PS[4][:, 0:2]), inc="V",
      deps=["sbcmm"], tag="sbc")
    E("vector", lambda e: e.tensor_tensor(SPROD[:, 0:1], SREC[:, 0:1], SREC[:, 1:2], Alu.mult),
      inc="V", deps=["sbc"], tag="sprod")
    E("vector", lambda e: e.reciprocal(SREC[:, 2:4], SREC[:, 0:2]), inc="V",
      deps=["sprod"], tag="srec")
    for bh in range(2):
        E("tensor", lambda e, bh=bh: e.matmul(PS[4][:64, :], ONESB[:, :],
          BIAS64[:1, bh * 512:(bh + 1) * 512], start=True, stop=True), inc="T",
          deps=["DCALL", "onesb", "sbc"] + ([f"bias64_0"] if bh else []), tag=f"bias64mm{bh}")
        E("vector", lambda e, bh=bh: e.tensor_copy(
            BIAS64[:, bh * 512:(bh + 1) * 512], PS[4][:64, :]), inc="V",
            deps=[f"bias64mm{bh}"], tag=f"bias64_{bh}")
    sc.alias("bias64", "bias64_1")
    for ck in range(8):
        E("sync", lambda e, ck=ck: e.dma_start(out=FKT[:, :], in_=fker[ck * 128:(ck + 1) * 128, :]),
          dinc="DF", deps=([f"kq8_{ck-1}"] if ck else []), tag=f"fkt{ck}")
        E("vector", lambda e, ck=ck: e.tensor_scalar(
            KQ8R[:, (ck % 2) * DIM:(ck % 2 + 1) * DIM], FKT[:, :], SREC[:, 3:4], None, Alu.mult),
            inc="V", deps=[f"fkt{ck}", "srec"] + ([f"kq8sp_{ck-2}"] if ck >= 2 else []),
            tag=f"kq8_{ck}")
        E("sync", lambda e, ck=ck: e.dma_start(
            out=kscr[ck, :, :], in_=KQ8R[:, (ck % 2) * DIM:(ck % 2 + 1) * DIM]),
            dinc=f"SPK{ck % 2}", deps=[f"kq8_{ck}"], tag=f"kq8sp_{ck}")
    E("sync", lambda e: e.dma_start(out=vscr[0:NMEM, :], in_=VST[0:NMEM, :]),
      dinc="D0B", deps=["memvall"], tag="memvsp")
    E("gpsimd", lambda e: e.memset(VST[:, :], 0.0), inc="G", deps=["memvsp"], tag="vz0")
    E("sync", lambda e: e.dma_start(out=vscr[4100:KW, :], in_=VST[0:KW - 4100, :]),
      dinc="D0B", deps=["vz0"], tag="vztail")
    for p in range(NPAIR):
        E("gpsimd", lambda e, p=p: e.memset(KR[p][:, 4100:KW], 0.0), inc="G", tag=f"kz{p}")
    sc.tags["D0BALL"] = [("D0B", sc.cnt["D0B"]), ("D0BG", sc.cnt["D0BG"])]

    def emit_ln(tg, xt_ap, dep, extra_ln_deps=()):
        E("vector", lambda e: e.tensor_reduce(
            STAT[:, 0:1], xt_ap, axis=mybir.AxisListType.X, op=Alu.add),
            inc="V", deps=[dep], tag=f"{tg}sum")
        E("scalar", lambda e: e.activation(XSQ[:, :], xt_ap, AF.Square,
          accum_out=STAT[:, 1:2]), inc="A", deps=[dep], tag=f"{tg}sq")

        E("vector", lambda e: e.tensor_scalar(STAT[:, 0:1], STAT[:, 0:1], 1.0 / DIM, None, Alu.mult),
          inc="V", deps=[f"{tg}sum"], tag=f"{tg}mean")
        E("vector", lambda e: e.tensor_scalar(STAT[:, 2:3], STAT[:, 1:2], 1.0 / DIM, None, Alu.mult),
          inc="V", deps=[f"{tg}sq"], tag=f"{tg}ex2")
        E("vector", lambda e: e.tensor_tensor(STAT[:, 3:4], STAT[:, 0:1], STAT[:, 0:1], Alu.mult),
          inc="V", deps=[f"{tg}mean"], tag=f"{tg}m2")
        E("vector", lambda e: e.tensor_tensor(STAT[:, 2:3], STAT[:, 2:3], STAT[:, 3:4], Alu.subtract),
          inc="V", deps=[f"{tg}ex2", f"{tg}m2"], tag=f"{tg}var")
        E("scalar", lambda e: e.activation(STAT[:, 4:5], STAT[:, 2:3], AF.Ln, bias=EPSV[:, 0:1]),
          inc="A", deps=[f"{tg}var", "epsv"], tag=f"{tg}sd")
        E("scalar", lambda e: e.activation(STAT[:, 5:6], STAT[:, 4:5], AF.Exp, scale=-0.5),
          inc="A", deps=[f"{tg}sd"], tag=f"{tg}rstd")
        E("vector", lambda e: e.tensor_scalar(
            XLN[:, :], xt_ap, STAT[:, 0:1], STAT[:, 5:6], Alu.subtract, Alu.mult),
            inc="V", deps=[f"{tg}rstd"] + list(extra_ln_deps), tag=f"{tg}ln")

    def emit_tp(tg, prev_xlt_users):
        for fc in range(8):
            d = [f"{tg}ln", "DCALL"]
            if fc == 0:
                d += prev_xlt_users
            E("tensor", lambda e, fc=fc: e.transpose(
                PT[0][:, fc * 128:(fc + 1) * 128], XLN[:, fc * 128:(fc + 1) * 128], IDE[:, :]),
                inc="T", deps=d, tag=f"{tg}tp{fc}")
        E("vector", lambda e: e.tensor_copy(XLT[:, :], PT[0][:, :]),
          inc="V", deps=[f"{tg}tp7"], tag=f"{tg}xlt")

    _l2prev = {0: None, 1: None}
    _l2cnt = [0]

    def emit_l2norm(tg, src_ap, pr, ncols, write_fn, dep, _ignored, extra_done_deps=()):
        par = _l2cnt[0] % 2
        _l2cnt[0] += 1
        o = 128 * par
        prev_tag = _l2prev[par]
        _l2prev[par] = tg
        sqd = [dep]
        if prev_tag:
            sqd.append(f"{prev_tag}ss")
        E("scalar", lambda e: e.activation(SQT[:, o:o + ncols], src_ap, AF.Square),
          inc="A", deps=sqd, tag=f"{tg}sq")
        ssd = [f"{tg}sq", "DCALL"]
        if prev_tag:
            ssd.append(f"{prev_tag}sd")
        else:
            ssd.append("sbc")
            ssd.append("bias64_1")
        E("tensor", lambda e: e.matmul(
            PS[4][:16, 256 + o:256 + o + ncols], SELP[:, pr * 16:(pr + 1) * 16], SQT[:, o:o + ncols],
            start=True, stop=True), inc="T", deps=ssd, tag=f"{tg}ss")
        sdd = [f"{tg}ss", "eps24"]
        if prev_tag:
            sdd.append(f"{prev_tag}done")
        E("scalar", lambda e: e.activation(RNB[:16, o:o + ncols], PS[4][:16, 256 + o:256 + o + ncols],
          AF.Ln, bias=E24[:16, 0:1]), inc="A", deps=sdd, tag=f"{tg}sd")
        rcd = [f"{tg}sd"]
        if prev_tag:
            rcd.append(f"{prev_tag}bc")
        E("scalar", lambda e: e.activation(RINV[:, o:o + ncols], RNB[:16, o:o + ncols],
          AF.Exp, scale=-0.5), inc="A", deps=rcd, tag=f"{tg}rc")
        bcd = [f"{tg}rc", "DCALL"]
        if prev_tag:
            bcd.append(f"{prev_tag}rb")
        else:
            bcd.append("hsv")
        E("tensor", lambda e: e.matmul(
            PS[5][:, o:o + ncols], SELBF[:, pr * 128:(pr + 1) * 128], RINV[:, o:o + ncols],
            start=True, stop=True), inc="T", deps=bcd, tag=f"{tg}bc")
        E("vector", lambda e: e.tensor_copy(RNB[:, o:o + ncols], PS[5][:, o:o + ncols]),
          inc="V", deps=[f"{tg}bc"], tag=f"{tg}rb")
        E("vector", lambda e: write_fn(e, o), inc="V",
          deps=[f"{tg}rb"] + list(extra_done_deps), tag=f"{tg}done")

    prev_chain = None
    for t in range(NTILE):
        xt_ap = XT[t % 2][:, :]
        tg = f"a{t}"
        xdeps = [f"a{t-2}ln"] if t >= 2 else []
        E("sync", lambda e, t=t, xt_ap=xt_ap: e.dma_start(
            out=xt_ap, in_=xfull[t * 128:(t + 1) * 128, :]), dinc=f"X{t % 2}", deps=xdeps, tag=f"{tg}x")
        lnd = [f"a{t-1}tp7"] if t >= 1 else []
        emit_ln(tg, xt_ap, f"{tg}x", extra_ln_deps=lnd)
        prevu = [f"a{t-1}xltfree"] if t >= 1 else []
        emit_tp(tg, prevu)
        for pr in range(NPAIR):
            bank = PS[pr % 2]
            slot = (pr // 2) * 128
            mmd = [f"{tg}xlt", "DCALL"]
            if t >= 1:
                mmd.append(f"a{t-1}k{pr}done")
            if pr >= 2:
                mmd.append(f"{tg}k{pr-2}done")
            for fc in range(8):
                E("tensor", lambda e, pr=pr, fc=fc, bank=bank, slot=slot: e.matmul(
                    bank[:, slot:slot + 128],
                    WBA[:, fc * DIM + pr * 128:fc * DIM + (pr + 1) * 128],
                    XLT[:, fc * 128:(fc + 1) * 128],
                    start=(fc == 0), stop=(fc == 7)), inc="T",
                    deps=(mmd if fc == 0 else [f"{tg}xlt"]),
                    tag=(f"{tg}k{pr}mm" if fc == 7 else None))
            def wk_out(e, o, pr=pr, t=t, bank=bank, slot=slot):
                return e.tensor_tensor(KR[pr][:, NMEM + t * 128: NMEM + (t + 1) * 128],
                                       bank[:, slot:slot + 128], RNB[:, o:o + 128], Alu.mult)
            emit_l2norm(f"{tg}k{pr}", bank[:, slot:slot + 128], pr, 128, wk_out,
                        f"{tg}k{pr}mm", prev_chain)
            prev_chain = f"{tg}k{pr}"
        for hv in range(2):
            vmd = [f"{tg}xlt", "DCALL"]
            if t >= 1:
                vmd.append(f"a{t-1}v{hv}ev")
            for fc in range(8):
                E("tensor", lambda e, hv=hv, fc=fc: e.matmul(
                    PS[2 + hv][:, :], XLT[:, fc * 128:(fc + 1) * 128],
                    WBB[:, fc * DIM + hv * 512: fc * DIM + (hv + 1) * 512],
                    start=(fc == 0), stop=(fc == 7)), inc="T",
                    deps=(vmd if fc == 0 else [f"{tg}xlt"]),
                    tag=(f"{tg}v{hv}mm" if fc == 7 else None))
            evd = [f"{tg}v{hv}mm"]
            if t >= 1:
                evd.append(f"a{t-1}vsp")
            else:
                evd.append("D0BALL")
            E("scalar", lambda e, hv=hv: e.copy(VST[:, hv * 512:(hv + 1) * 512], PS[2 + hv][:, :]),
              inc="A", deps=evd, tag=f"{tg}v{hv}ev")
        E("sync", lambda e, t=t: e.dma_start(
            out=vscr[NMEM + t * 128: NMEM + (t + 1) * 128, :], in_=VST[:, :]),
            dinc="SPV", deps=[f"{tg}v0ev", f"{tg}v1ev"], tag=f"{tg}vsp")
        sc.alias(f"{tg}xltfree", f"{tg}v1mm")

    for pr in range(NPAIR):
        tg = f"mn{pr}"
        cpd = ["DCALL", f"a{NTILE-1}k{pr}done"]
        if pr >= 1:
            cpd.append(f"mn{pr-1}done")
        E("vector", lambda e, pr=pr: e.tensor_copy(MCP[:, :], KR[pr][:, 0:NMEM]),
          inc="V", deps=cpd, tag=f"{tg}cp")
        def wmem(e, o, pr=pr):
            return e.tensor_tensor(KR[pr][:, 0:NMEM], MCP[:, :], RNB[:, o:o + NMEM], Alu.mult)
        emit_l2norm(tg, MCP[:, :], pr, NMEM, wmem, f"{tg}cp", prev_chain)
        prev_chain = tg

    for ck in range(8):
        E("gpsimd", lambda e, ck=ck: e.dma_start(
            out=WBA[:, ck * DIM:(ck + 1) * DIM], in_=wq[ck * 128:(ck + 1) * 128, :]),
            dinc="DW2", deps=[f"a{NTILE-1}k{pr}mm" for pr in range(NPAIR)], tag=f"wqc{ck}")
        E("gpsimd", lambda e, ck=ck: e.dma_start(
            out=WBB[:, ck * DIM:(ck + 1) * DIM], in_=wg[ck * 128:(ck + 1) * 128, :]),
            dinc="DW2", deps=[f"a{NTILE-1}v0mm", f"a{NTILE-1}v1mm"], tag=f"wgc{ck}")

    sc.tags["wqgall"] = [("DW2", 16 * 16)]

    for t in range(NQT):
        xt_ap = XT[t % 2][:, :]
        tg = f"b{t}"
        xdeps = [f"b{t-2}ln"] if t >= 2 else [f"a{NTILE-1}ln", f"mn{NPAIR-1}done"]
        E("sync", lambda e, t=t, xt_ap=xt_ap: e.dma_start(
            out=xt_ap, in_=xq[t * 128:(t + 1) * 128, :]), dinc=f"X{t % 2}", deps=xdeps, tag=f"{tg}x")
        lnd = [f"b{t-1}tp7"] if t >= 1 else [f"a{NTILE-1}tp7"]
        emit_ln(tg, xt_ap, f"{tg}x", extra_ln_deps=lnd)
        prevu = [f"b{t-1}xltfree"] if t >= 1 else [f"a{NTILE-1}xltfree"]
        emit_tp(tg, prevu)
        for pr in range(NPAIR):
            bank = PS[pr % 2]
            slot = (pr // 2) * 128
            mmd = [f"{tg}xlt", "wqgall"]
            if t >= 1:
                mmd.append(f"b{t-1}q{pr}sp")
            if pr >= 2:
                mmd.append(f"{tg}q{pr-2}done")
            elif t == 0:
                mmd.append(f"a{NTILE-1}k{6+pr}done")
            for fc in range(8):
                E("tensor", lambda e, pr=pr, fc=fc, bank=bank, slot=slot: e.matmul(
                    bank[:, slot:slot + 128],
                    WBA[:, fc * DIM + pr * 128:fc * DIM + (pr + 1) * 128],
                    XLT[:, fc * 128:(fc + 1) * 128],
                    start=(fc == 0), stop=(fc == 7)), inc="T",
                    deps=(mmd if fc == 0 else [f"{tg}xlt"]),
                    tag=(f"{tg}q{pr}mm" if fc == 7 else None))
            def wq_out(e, o, pr=pr, bank=bank, slot=slot):
                return e.tensor_tensor(QNT[:, :], bank[:, slot:slot + 128], RNB[:, o:o + 128], Alu.mult)
            ed = [f"b{t-1}q{NPAIR-1}sp"] if t >= 1 else []
            if pr >= 1:
                ed = [f"{tg}q{pr-1}sp"]
            emit_l2norm(f"{tg}q{pr}", bank[:, slot:slot + 128], pr, 128, wq_out,
                        f"{tg}q{pr}mm", prev_chain, extra_done_deps=ed)
            prev_chain = f"{tg}q{pr}"
            E("sync", lambda e, t=t, pr=pr: e.dma_start(
                out=qscr[pr, :, t * 128:(t + 1) * 128], in_=QNT[:, :]),
                dinc="SPQ", deps=[f"{tg}q{pr}done"], tag=f"{tg}q{pr}sp")
        for pr in range(NPAIR):
            bank = PS[2 + pr % 2]
            slot = (pr // 2) * 128
            mmd = [f"{tg}xlt", "wqgall"]
            if t >= 1:
                mmd.append(f"b{t-1}g{pr}sg")
            if pr >= 2:
                mmd.append(f"{tg}g{pr-2}sg")
            elif t == 0:
                mmd.append(f"a{NTILE-1}v{pr % 2}ev")
            for fc in range(8):
                E("tensor", lambda e, pr=pr, fc=fc, bank=bank, slot=slot: e.matmul(
                    bank[:, slot:slot + 128],
                    WBB[:, fc * DIM + pr * 128:fc * DIM + (pr + 1) * 128],
                    XLT[:, fc * 128:(fc + 1) * 128],
                    start=(fc == 0), stop=(fc == 7)), inc="T",
                    deps=(mmd if fc == 0 else [f"{tg}xlt"]),
                    tag=(f"{tg}g{pr}mm" if fc == 7 else None))
            sgd = [f"{tg}g{pr}mm", "DCALL"]
            if pr >= 1:
                sgd.append(f"{tg}g{pr-1}sp")
            elif t >= 1:
                sgd.append(f"b{t-1}g{NPAIR-1}sp")
            else:
                sgd.append(f"a{NTILE-1}vsp")
            E("scalar", lambda e, pr=pr, bank=bank, slot=slot: e.activation(
                VST[:, pr * 128:(pr + 1) * 128], bank[:, slot:slot + 128], AF.Sigmoid,
                bias=BGS[:, pr:pr + 1]), inc="A", deps=sgd, tag=f"{tg}g{pr}sg")
            E("sync", lambda e, t=t, pr=pr: e.dma_start(
                out=gscr[pr, :, t * 128:(t + 1) * 128], in_=VST[:, pr * 128:(pr + 1) * 128]),
                dinc="SPG", deps=[f"{tg}g{pr}sg"], tag=f"{tg}g{pr}sp")
        sc.alias(f"{tg}xltfree", f"{tg}g{NPAIR-1}mm")

    lb = f"b{NQT-1}"
    sc.tags["woall"] = [("DW2", 16 * 24)]
    wodeps = [f"{lb}q{pr}mm" for pr in range(NPAIR)]
    for ck in range(8):
        E("gpsimd", lambda e, ck=ck: e.dma_start(
            out=WBA[:, ck * DIM:(ck + 1) * DIM], in_=wo[ck * 128:(ck + 1) * 128, :]),
            dinc="DW2", deps=wodeps, tag=f"woc{ck}")
        E("vector", lambda e, ck=ck: e.tensor_scalar(
            WBA[:, ck * DIM:(ck + 1) * DIM], WBA[:, ck * DIM:(ck + 1) * DIM],
            HSV[:, ck:ck + 1], None, Alu.mult), inc="V",
            deps=["woall", "hsv"], tag=f"wos{ck}")

    sc.tags["spkall0"] = [("SPK0", 16 * 4)]
    sc.tags["spkall1"] = [("SPK1", 16 * 4)]

    NGRP = len(groups)

    def ginfo(n):
        w = [octet_width(n, i) for i in range(8)]
        return w, max(w) // 128

    def e_qs(n, pg):
        t = f"g{n}"
        qd = [f"b{n//2}q{pr}sp" for pr in range(NPAIR)]
        if pg is not None:
            qd.append(f"g{pg}qkdone")
        E("sync", lambda e, n=n: e.dma_start(
            out=QS.rearrange("p (a q) -> p a q", a=16)[0:64, 0::2, :],
            in_=qscr.rearrange("a p q -> p a q")[0:64, :, n * 64:(n + 1) * 64]),
            dinc="DQS", deps=qd + ["qz0", "qz1"], tag=f"{t}qsa")
        E("sync", lambda e, n=n: e.dma_start(
            out=QS.rearrange("p (a q) -> p a q", a=16)[64:128, 1::2, :],
            in_=qscr.rearrange("a p q -> p a q")[64:128, :, n * 64:(n + 1) * 64]),
            dinc="DQS", deps=[f"{t}qsa"], tag=f"{t}qs")

    def e_ez(n, first):
        t = f"g{n}"
        w, _ = ginfo(n)
        Wm = max(w)
        ezt = []
        for i in range(8):
            if w[i] < Wm:
                ezd = ([f"b{NQT-1}tp7", f"b{NQT-1}g{NPAIR-1}sg", f"b{NQT-1}g{NPAIR-1}sp",
                        f"b{NQT-1}g{NPAIR-1}mm", "kq8_7", "bdpost7"] if first else [])
                E("gpsimd", lambda e, i=i, wi=w[i], Wm=Wm: e.memset(
                    EB[:, i * KW + wi: i * KW + Wm], 0.0), inc="G",
                    deps=ezd, tag=f"{t}ez{i}")
                ezt.append(f"{t}ez{i}")
        return ezt

    def e_qk(n, jb, extra02):
        t = f"g{n}"
        for p in range(8):
            bank = PS[2 + p // 4]
            col = (p % 4) * 128
            mmd = [f"{t}qs", f"a{NTILE-1}k{p}done", f"mn{p}done", f"kz{p}"]
            if jb >= 1:
                mmd.append(f"{t}ev1_{jb-1}")
            else:
                mmd += extra02
            sc_tag = f"{t}qk{jb}" if p == 7 else None
            E("tensor", lambda e, p=p, jb=jb, bank=bank, col=col: e.matmul(
                bank[:, col:col + 128],
                KR[p][:, jb * 128:(jb + 1) * 128],
                QS[:, p * 128:(p + 1) * 128],
                start=True, stop=True),
                inc="T", deps=mmd, tag=sc_tag)

    def e_ev1(n, jb, pg, pNBg):
        t = f"g{n}"
        for bi in range(2):
            def ev1(e, jb=jb, bi=bi):
                src = PS[2 + bi].rearrange("p (hh i q) -> p hh i q", hh=8, i=8, q=8)
                dst = S1[jb % 2].rearrange("p (i hh q) -> p hh i q", i=8, hh=16, q=8)[
                    :, bi * 8:(bi + 1) * 8, :, :]
                return e.tensor_copy(dst, src)
            e1d = [f"{t}qk{jb}"]
            if jb >= 2:
                e1d.append(f"{t}t1d_{jb-2}")
            elif pg is not None:
                e1d.append(f"g{pg}t1d_{pNBg-2+jb}")
            E("vector", ev1, inc="V", deps=e1d, tag=f"{t}ev1b{jb}_{bi}")
        sc.alias(f"{t}ev1_{jb}", f"{t}ev1b{jb}_1")

    def e_t1(n, jb, extra_first):
        t = f"g{n}"
        w, _ = ginfo(n)
        live = [i for i in range(8) if w[i] > jb * 128]
        ptb = PT[jb % 2]
        for i in live:
            t1d = [f"{t}ev1_{jb}"]
            if i == live[0]:
                if jb >= 2:
                    t1d.append(f"{t}ev2_{jb-2}")
                else:
                    t1d += extra_first
            E("tensor", lambda e, i=i, jb=jb, ptb=ptb: e.transpose(
                ptb[:, i * 128:(i + 1) * 128], S1[jb % 2][:, i * 128:(i + 1) * 128], IDE[:, :]),
                inc="T", deps=t1d, tag=f"{t}t1_{jb}_{i}")
        sc.alias(f"{t}t1d_{jb}", f"{t}t1_{jb}_{live[-1]}")

    def e_ev2(n, jb, gT2, NBg, ezt):
        t = f"g{n}"
        w, _ = ginfo(n)
        live = [i for i in range(8) if w[i] > jb * 128]
        ptb = PT[jb % 2]
        def ev2(e, jb=jb, live=tuple(live), ptb=ptb):
            last = None
            for i in live:
                last = e.copy(EB[:, i * KW + jb * 128: i * KW + (jb + 1) * 128],
                              ptb[:, i * 128:(i + 1) * 128])
            return last
        e2d = [f"{t}t1_{jb}_{i}" for i in live] + (ezt if jb == 0 else [])
        if gT2 is not None and jb < NBg:
            e2d.append(f"g{gT2}t2ev_{jb}")
        E("scalar", ev2, inc="A", deps=e2d, tag=f"{t}ev2_{jb}")

    def e_B(n, pg):
        t = f"g{n}"
        w, _ = ginfo(n)
        for i in range(8):
            wi = w[i]
            nch = (wi + 511) // 512
            for ci in range(nch):
                c0 = ci * 512
                cw = min(512, wi - c0)
                bank = PS[4 + ((i + ci) % 2)]
                pmd = [f"{t}ev2_{(c0 + cw - 1) // 128}", "bdpre7"]
                if ci >= 2:
                    pmd.append(f"{t}exp{i}_{ci-2}")
                elif i >= 1:
                    pmd.append(f"{t}exp{i-1}_bank{(i + ci) % 2}last")
                elif pg is not None:
                    pmd.append(f"g{pg}gateA")
                    pmd.append(f"g{pg}gate")
                E("tensor", lambda e, i=i, c0=c0, cw=cw, bank=bank: e.matmul(
                    bank[:, :cw], BDPRE[:, :], EB[:, i * KW + c0: i * KW + c0 + cw],
                    start=True, stop=True), inc="T", deps=pmd, tag=f"{t}pm{i}_{ci}")
                E("scalar", lambda e, i=i, c0=c0, cw=cw, bank=bank: e.activation(
                    EB[:, i * KW + c0: i * KW + c0 + cw], bank[:, :cw], AF.Exp,
                    bias=MHATN[:, 0:1]), inc="A", deps=[f"{t}pm{i}_{ci}", "mhat"],
                    tag=f"{t}exp{i}_{ci}")
            sc.alias(f"{t}exp{i}_last", f"{t}exp{i}_{nch-1}")
            for bank in range(2):
                cis = [ci for ci in range(nch) if (i + ci) % 2 == bank]
                sc.alias(f"{t}exp{i}_bank{bank}last",
                         f"{t}exp{i}_{cis[-1]}" if cis else f"{t}exp{i}_{nch-1}")

    ezt_by_g = {}
    for gi, g in enumerate(groups):
        tg = f"g{g}"
        prevg = groups[gi - 1] if gi >= 1 else None
        nextg = groups[gi + 1] if gi + 1 < NGRP else None
        widths, NB = ginfo(g)
        Wmax = max(widths)
        if nextg is not None:
            nwidths, NBn = ginfo(nextg)

        if gi == 0:
            e_qs(g, None)
            ezt_by_g[g] = e_ez(g, True)
            for jb in range(NB):
                e_qk(g, jb, [])
                e_ev1(g, jb, None, 0)
                e_t1(g, jb, [])
                e_ev2(g, jb, None, 0, ezt_by_g[g])
            sc.alias(f"{tg}qkdone", f"{tg}qk{NB-1}")
            for jb in range(NB, NB + 2):
                sc.alias(f"{tg}ev2_{jb}", f"{tg}ev2_{NB-1}")
                sc.alias(f"{tg}t1d_{jb}", f"{tg}t1d_{NB-1}")
            e_B(g, None)

        S012 = widths[0] + widths[1] + widths[2]
        S47 = sum(widths[4:8])
        cs3 = int((0.9 * (widths[3] + S47) - 1.25 * S012 + 1000) / 2.15) // 64 * 64
        cs3 = max(64, min(widths[3], cs3))
        csplit = cs3 < widths[3]
        for i in range(8):
            wi = widths[i]
            q0w = octet_q0_worst(g, i)
            c0m = max(0, wi - 384)
            off = 384 - q0w - 5 + c0m
            mkd = [f"{tg}exp{i}_last", "DCALL"]
            E("vector", lambda e, i=i, c0m=c0m, off=off, wi=wi: e.tensor_tensor(
                EB[:, i * KW + c0m: i * KW + wi], EB[:, i * KW + c0m: i * KW + wi],
                MSK[:, off: off + wi - c0m], Alu.mult), inc="V", deps=mkd, tag=f"{tg}msk{i}")
            if i < 3 or (i == 3 and not csplit):
                cv = 64.0
            elif i == 3:
                cv = float(128 - (widths[3] - cs3))
            else:
                cv = float(128 - wi)
            cvd = [f"g{prevg}upd_{NROUND-1}"] if gi >= 1 else [f"b{NQT-1}g{NPAIR-1}mm"]
            E("gpsimd", lambda e, i=i, cv=cv: e.memset(CMPV[:, i:i + 1], cv), inc="G",
              deps=cvd, tag=f"{tg}cmpv{i}")
        lod = [f"g{prevg}stt7"] if gi >= 1 else []
        E("gpsimd", lambda e: e.memset(LO[:, :], 0), inc="G", deps=lod, tag=f"{tg}lo")
        zslots = []
        for i in range(4, 8):
            nch = (widths[i] + 2111) // 2112
            for ci in range(nch, 2):
                zslots.append((i - 3) * 2 + ci)
        zdep = [f"g{prevg}csum_{NROUND-1}"] if gi >= 1 else [f"b{NQT-1}g{NPAIR-1}mm"]
        if zslots:
            def zc(e, zslots=tuple(zslots)):
                last = None
                for sl in zslots:
                    last = e.memset(CNT4[:, sl:sl + 1], 0.0)
                return last
            E("gpsimd", zc, inc="G", deps=zdep, tag=f"{tg}cntz")

        if nextg is not None:
            e_qs(nextg, g)
            ezt_by_g[nextg] = e_ez(nextg, False)
            hx = ([f"g{prevg}ffnev0", f"g{prevg}ffnev1"]
                  if prevg is not None else [f"{tg}ev1_{NB-1}"])
            e_qk(nextg, 0, hx)
            e_ev1(nextg, 0, g, NB)
            e_qk(nextg, 1, [])
            e_ev1(nextg, 1, g, NB)
            t1x0 = [f"g{prevg}q8"] if prevg is not None else [f"{tg}ev2_{NB-1}"]
            e_t1(nextg, 0, t1x0)
            e_qk(nextg, 2, [])
            e_ev1(nextg, 2, g, NB)
            e_t1(nextg, 1, [f"{tg}ev2_{NB-1}"])
            e_qk(nextg, 3, [])
            e_ev1(nextg, 3, g, NB)

        prev = [f"{tg}lo"] + [f"{tg}msk{i}" for i in range(8)] + [f"{tg}cmpv{i}" for i in range(8)]
        if zslots:
            prev.append(f"{tg}cntz")
        if gi == 0:
            prev += [f"b{NQT-1}g{NPAIR-1}mm"]
        for r in range(NROUND):
            step = 1 << (13 - r)
            E("vector", lambda e, step=step: e.tensor_scalar(
                TF[:, :], LO[:, :], step, 65536, Alu.add, Alu.mult),
                inc="V", deps=prev, tag=f"{tg}thb_{r}")
            E("vector", lambda e: e.tensor_scalar(
                TFN[:, :], TF[:, :], 32768, -0x80000000, Alu.subtract, Alu.add),
                inc="V", deps=[f"{tg}thb_{r}"], tag=f"{tg}thd_{r}")
            cntdeps = []
            for i in range(4):
                wi = cs3 if (i == 3 and csplit) else widths[i]
                cd = [f"{tg}thd_{r}"]
                if i >= 1:
                    cd.append(f"{tg}cnt{i-1}_{r}")
                elif r >= 1:
                    cd.append(f"{tg}cnt3_{r-1}")
                E("vector", lambda e, i=i, wi=wi: e.tensor_scalar(
                    WBB[:, :wi], EB[:, i * KW: i * KW + wi], TF.bitcast(F32)[:, i:i + 1],
                    0.0, Alu.is_ge, Alu.add, accum_out=CNT[:, i:i + 1]),
                    inc="V", deps=cd, tag=f"{tg}cnt{i}_{r}")
                cntdeps.append(f"{tg}cnt{i}_{r}")
            prevact = f"{tg}ge_{r-1}" if r >= 1 else None
            aocts = ([3] if csplit else []) + [4, 5, 6, 7]
            nch3 = (widths[3] - cs3 + 2111) // 2112 if csplit else 0
            for i in aocts:
                a0 = cs3 if i == 3 else 0
                wi = widths[i] - a0
                nch = (wi + 2111) // 2112
                for ci in range(nch):
                    c0 = a0 + ci * 2112
                    cw = min(2112, a0 + wi - c0)
                    cd = [f"{tg}thd_{r}"]
                    if prevact:
                        cd.append(prevact)
                    E("scalar", lambda e, i=i, c0=c0, cw=cw, ci=ci: e.activation(
                        WBB[:, 4352:4352 + cw], EB[:, i * KW + c0: i * KW + c0 + cw],
                        AF.Sign, bias=TFN.bitcast(F32)[:, i:i + 1],
                        accum_out=CNT4[:, (i - 3) * 2 + ci: (i - 3) * 2 + ci + 1]),
                        inc="A", deps=cd, tag=f"{tg}cc{i}_{ci}_{r}")
                    prevact = f"{tg}cc{i}_{ci}_{r}"
                    cntdeps.append(prevact)
            v5 = CNT4.rearrange("p (o c) -> p o c", o=5)
            E("vector", lambda e, v5=v5: e.tensor_tensor(CNT[:, 4:8], v5[:, 1:5, 0], v5[:, 1:5, 1],
              Alu.add), inc="V", deps=cntdeps, tag=f"{tg}csum_{r}")
            gdep = [f"{tg}csum_{r}"]
            if csplit:
                if nch3 == 2:
                    E("vector", lambda e: e.tensor_tensor(CNT4[:, 0:1], CNT4[:, 0:1],
                      CNT4[:, 1:2], Alu.add), inc="V", deps=cntdeps, tag=f"{tg}s3_{r}")
                    s3d = [f"{tg}s3_{r}"]
                else:
                    s3d = cntdeps
                E("vector", lambda e: e.scalar_tensor_tensor(
                    CNT[:, 3:4], CNT[:, 3:4], 2.0, CNT4[:, 0:1], Alu.mult, Alu.add),
                    inc="V", deps=s3d + [f"{tg}cnt3_{r}"], tag=f"{tg}cb_{r}")
                gdep.append(f"{tg}cb_{r}")
            E("vector", lambda e: e.tensor_tensor(GEF[:, :], CNT[:, :], CMPV[:, :], Alu.is_ge),
              inc="V", deps=gdep, tag=f"{tg}ge_{r}")
            E("vector", lambda e, step=step: e.scalar_tensor_tensor(
                LO[:, :], GEF[:, :], step, LO[:, :], Alu.mult, Alu.add),
                inc="V", deps=[f"{tg}ge_{r}"], tag=f"{tg}upd_{r}")
            prev = [f"{tg}upd_{r}"]
        E("vector", lambda e: e.tensor_scalar(
            TF[:, :], LO[:, :], 16, None, Alu.logical_shift_left),
            inc="V", deps=prev, tag=f"{tg}ek")
        for i in range(8):
            wi = widths[i]
            E("vector", lambda e, i=i, wi=wi: e.scalar_tensor_tensor(
                EB[:, i * KW: i * KW + wi], EB[:, i * KW: i * KW + wi],
                TF.bitcast(F32)[:, i:i + 1], EB[:, i * KW: i * KW + wi],
                Alu.is_ge, Alu.mult, accum_out=SUM8[:, i:i + 1]),
                inc="V", deps=[f"{tg}ek"], tag=f"{tg}stt{i}")
            E("vector", lambda e, i=i: e.reciprocal(RS8[:, i:i + 1], SUM8[:, i:i + 1]),
              inc="V", deps=[f"{tg}stt{i}"], tag=f"{tg}rs{i}")
            bpd = [f"{tg}rs{i}", "bdpost7"]
            if gi >= 1:
                bpd.append(f"g{prevg}t2done{i}")
            E("vector", lambda e, i=i: e.tensor_scalar(
                BDP[i][:, :], BDPOST[:, :], RS8[:, i:i + 1], None, Alu.mult),
                inc="V", deps=bpd, tag=f"{tg}bdp{i}")

        for jb in range(min(3, NB)):
            vd = [f"a{NTILE-1}vsp", "D0BALL"]
            if gi >= 1:
                pj = pNB0 - 1 - ((pNB0 - 1 - jb) % 3)
                if pj >= 0:
                    vd.append(f"g{prevg}av_{pj}")
            E("sync", lambda e, jb=jb: e.dma_start(
                out=VS[jb % 3][:, :], in_=vscr[jb * 128:(jb + 1) * 128, :]),
                dinc=f"DV{jb % 3}", deps=vd, tag=f"{tg}vl_{jb}")
        LIM = max(NB, NBn) + 1 if nextg is not None else NB + 1
        for jb in range(LIM):
            if jb < NB:
                live = [i for i in range(8) if widths[i] > jb * 128]
                for i in live:
                    td = [f"{tg}bdp{i}"]
                    if i == live[0]:
                        if jb >= 1:
                            td.append(f"{tg}t2ev_{jb-1}")
                        elif prevg is not None:
                            td += [f"g{prevg}x1_0", f"g{prevg}x1_1"]
                    E("tensor", lambda e, i=i, jb=jb: e.matmul(
                        PS[i // 4][:, (i % 4) * 128:(i % 4) * 128 + 128],
                        EB[:, i * KW + jb * 128: i * KW + (jb + 1) * 128], BDP[i][:, :],
                        start=True, stop=True), inc="T", deps=td, tag=f"{tg}t2_{jb}_{i}")
                    if jb == widths[i] // 128 - 1:
                        sc.alias(f"{tg}t2done{i}", f"{tg}t2_{jb}_{i}")

                def evt2(e, jb=jb, live=tuple(live)):
                    last = None
                    for bi in range(2):
                        sub = [i for i in live if i // 4 == bi]
                        if not sub:
                            continue
                        if len(sub) == 4:
                            src = PS[bi].rearrange("p (q hh s) -> p hh q s", q=4, hh=16, s=8)
                            dst = ATT.rearrange("p (hh i s) -> p hh i s", hh=16, i=8, s=8)[
                                :, :, bi * 4:(bi + 1) * 4, :]
                            last = e.tensor_copy(dst, src)
                        else:
                            for i in sub:
                                src = PS[bi].rearrange("p (q hh s) -> p hh q s", q=4, hh=16, s=8)[
                                    :, :, i % 4, :]
                                dst = ATT.rearrange("p (hh i s) -> p hh i s", hh=16, i=8, s=8)[:, :, i, :]
                                last = e.tensor_copy(dst, src)
                    return last
                e2d = [f"{tg}t2_{jb}_{i}" for i in live]
                if jb >= 1:
                    e2d.append(f"{tg}av_{jb-1}")
                E("vector", evt2, inc="V", deps=e2d, tag=f"{tg}t2ev_{jb}")
                dead = [i for i in range(8) if i not in live]
                if dead:
                    def zatt(e, dead=tuple(dead)):
                        last = None
                        for i in dead:
                            last = e.memset(
                                ATT.rearrange("p (hh i s) -> p i hh s", hh=16, i=8, s=8)[:, i, :, :], 0.0)
                        return last
                    E("vector", zatt, inc="V", deps=[f"{tg}t2ev_{jb}"], tag=f"{tg}tz_{jb}")
            ja = jb - 1
            if 0 <= ja < NB:
                att_tag = f"{tg}tz_{ja}" if any(widths[i] <= ja * 128 for i in range(8)) \
                    else f"{tg}t2ev_{ja}"
                for h in range(H):
                    avd = [att_tag, f"{tg}vl_{ja}"]
                    if ja == 0 and h < 2:
                        avd.append(f"{tg}exp7_last")
                        if gi >= 1:
                            avd.append(f"g{prevg}gate")
                    par = h % 2
                    E("tensor", lambda e, h=h, ja=ja, NB=NB, par=par: e.matmul(
                        PS[4 + par][64 * par:64 * par + 64, (h // 2) * 64:(h // 2) * 64 + 64],
                        VS[ja % 3][:, h * 64:(h + 1) * 64], ATT[:, h * 64:(h + 1) * 64],
                        start=(ja == 0 and h < 2), stop=(ja == NB - 1 and h >= H - 2),
                        tile_position=(0, 64 * par), skip_group_check=True),
                        inc="T", deps=avd, tag=(f"{tg}av_{ja}" if h == H - 1 else None))
                if ja + 3 < NB:
                    E("sync", lambda e, ja=ja: e.dma_start(
                        out=VS[(ja + 3) % 3][:, :], in_=vscr[(ja + 3) * 128:(ja + 4) * 128, :]),
                        dinc=f"DV{(ja + 3) % 3}", deps=[f"{tg}av_{ja}"], tag=f"{tg}vl_{ja+3}")
            if nextg is not None:
                if 4 <= jb < NBn:
                    e_qk(nextg, jb, [])
                    e_ev1(nextg, jb, g, NB)
                if 2 <= ja < NBn:
                    e_t1(nextg, ja, [])
                if 0 <= ja < NBn:
                    e_ev2(nextg, ja, g, NB, ezt_by_g[nextg])
        if nextg is not None:
            tn = f"g{nextg}"
            sc.alias(f"{tn}qkdone", f"{tn}qk{NBn-1}")
            for jb in range(NBn, NBn + 2):
                sc.alias(f"{tn}ev2_{jb}", f"{tn}ev2_{NBn-1}")
                sc.alias(f"{tn}t1d_{jb}", f"{tn}t1d_{NBn-1}")
        sc.alias(f"{tg}avdone", f"{tg}av_{NB-1}")
        for jb in range(NB, NB + 4):
            sc.alias(f"{tg}av_{jb}", f"{tg}avdone")
        sc.alias(f"{tg}av_m1", f"{tg}avdone")

        gd = [f"b{g//2}g{pr}sp" for pr in range(NPAIR)]
        if gi >= 1:
            gd.append(f"g{prevg}gate")
        E("sync", lambda e, g=g: e.dma_start(
            out=GS.rearrange("p (a q) -> p a q", a=8),
            in_=gscr.rearrange("a p q -> p a q")[:, :, g * 64:(g + 1) * 64]),
            dinc="DGS", deps=gd, tag=f"{tg}gs")
        xrd = [f"g{prevg}ost"] if gi >= 1 else []
        E("sync", lambda e, g=g: e.dma_start(out=XRES[:, :], in_=xq[g * 64:(g + 1) * 64, :]),
          dinc="DXR", deps=xrd, tag=f"{tg}xr")
        E("vector", lambda e: e.tensor_tensor(AOUT[0:64, :], PS[4][0:64, :], GS[0:64, :], Alu.mult),
          inc="V", deps=[f"{tg}avdone", f"{tg}gs"], tag=f"{tg}gateA")
        E("vector", lambda e: e.tensor_tensor(AOUT[64:128, :], PS[5][64:128, :], GS[64:128, :], Alu.mult),
          inc="V", deps=[f"{tg}avdone", f"{tg}gs"], tag=f"{tg}gate")

        if nextg is not None:
            e_B(nextg, g)

        for hv in range(2):
            o2d = [f"{tg}gate", "wos7", f"{tg}t2ev_{NB-1}"]
            for ck in range(8):
                E("tensor", lambda e, hv=hv, ck=ck: e.matmul(
                    PS[hv][:64, :], AOUT[:, ck * 64:(ck + 1) * 64],
                    WBA[:, ck * DIM + hv * 512: ck * DIM + (hv + 1) * 512],
                    start=(ck == 0), stop=(ck == 7)), inc="T",
                    deps=(o2d if ck == 0 else []),
                    tag=(f"{tg}o2_{hv}" if ck == 7 else None))
            E("vector", lambda e, hv=hv: e.tensor_tensor(
                X1[:, hv * 512:(hv + 1) * 512], PS[hv][:64, :], XRES[:, hv * 512:(hv + 1) * 512],
                Alu.add), inc="V", deps=[f"{tg}o2_{hv}", f"{tg}xr"], tag=f"{tg}x1_{hv}")

        t2 = f"{tg}L"
        E("vector", lambda e: e.tensor_reduce(
            STAT[:64, 0:1], X1[:64, :], axis=mybir.AxisListType.X, op=Alu.add),
            inc="V", deps=[f"{tg}x1_0", f"{tg}x1_1"], tag=f"{t2}sum")
        E("scalar", lambda e: e.activation(YB[:64, :], X1[:64, :], AF.Square,
          accum_out=STAT[:64, 1:2]), inc="A", deps=[f"{tg}x1_0", f"{tg}x1_1"], tag=f"{t2}sq")

        E("vector", lambda e: e.tensor_scalar(STAT[:64, 0:1], STAT[:64, 0:1], 1.0 / DIM, None, Alu.mult),
          inc="V", deps=[f"{t2}sum"], tag=f"{t2}mean")
        E("vector", lambda e: e.tensor_scalar(STAT[:64, 2:3], STAT[:64, 1:2], 1.0 / DIM, None, Alu.mult),
          inc="V", deps=[f"{t2}sq"], tag=f"{t2}ex2")
        E("vector", lambda e: e.tensor_tensor(STAT[:64, 3:4], STAT[:64, 0:1], STAT[:64, 0:1], Alu.mult),
          inc="V", deps=[f"{t2}mean"], tag=f"{t2}m2")
        E("vector", lambda e: e.tensor_tensor(STAT[:64, 2:3], STAT[:64, 2:3], STAT[:64, 3:4], Alu.subtract),
          inc="V", deps=[f"{t2}ex2", f"{t2}m2"], tag=f"{t2}var")
        E("scalar", lambda e: e.activation(STAT[:64, 4:5], STAT[:64, 2:3], AF.Ln, bias=EPSV[:64, 0:1]),
          inc="A", deps=[f"{t2}var", "epsv"], tag=f"{t2}sd")
        E("scalar", lambda e: e.activation(STAT[:64, 5:6], STAT[:64, 4:5], AF.Exp, scale=-0.5),
          inc="A", deps=[f"{t2}sd"], tag=f"{t2}rstd")
        E("vector", lambda e: e.tensor_scalar(
            H2B[:64, :], X1[:64, :], STAT[:64, 0:1], STAT[:64, 5:6], Alu.subtract, Alu.mult),
            inc="V", deps=[f"{t2}rstd"], tag=f"{t2}ln")
        for fc in range(8):
            tpd = [f"{t2}ln"]
            if fc == 0:
                if nextg is not None:
                    tpd.append(f"g{nextg}ev2_{NBn-1}")
                else:
                    tpd.append(f"{tg}ev2_{NB-1}")
            E("tensor", lambda e, fc=fc: e.transpose(
                PT[0][:, fc * 64:(fc + 1) * 64], H2B[:64, fc * 128:(fc + 1) * 128], IDE[:64, :64]),
                inc="T", deps=tpd, tag=f"{t2}tp{fc}")
        q8d = [f"{t2}tp{fc}" for fc in range(8)]
        if gi >= 1:
            q8d.append(f"g{prevg}ffnmm1")
        E("vector", lambda e: e.tensor_scalar(
            XQ8[:, :], PT[0][:, :512], SREC[:, 2:3], None, Alu.mult),
            inc="V", deps=q8d, tag=f"{tg}q8")
        for ck in range(8):
            kld = ["spkall0", "spkall1"]
            if ck >= 2:
                kld.append(f"{tg}fm{ck-2}_1")
            elif gi >= 1:
                kld.append(f"g{prevg}fm{6+ck}_1")
            E("sync", lambda e, ck=ck: e.dma_start(
                out=KQ8R[:, (ck % 2) * DIM:(ck % 2 + 1) * DIM], in_=kscr[ck, :, :]),
                dinc=f"DK{ck % 2}", deps=kld, tag=f"{tg}kl{ck}")
            for hv in range(2):
                fd = [f"{tg}q8", f"{tg}kl{ck}"]
                if ck == 0:
                    fd.append(f"{tg}o2_{hv}")
                    if nextg is not None:
                        fd.append(f"g{nextg}ev1_{NBn-1}")
                E("tensor", lambda e, hv=hv, ck=ck: e.matmul(
                    PS[2 + hv][:64, :], XQ8[:, ck * 64:(ck + 1) * 64],
                    KQ8R[:, (ck % 2) * DIM + hv * 512: (ck % 2) * DIM + (hv + 1) * 512],
                    start=(ck == 0), stop=(ck == 7)), inc="T",
                    deps=fd, tag=f"{tg}fm{ck}_{hv}")
        for hv in range(2):
            sc.alias(f"{tg}ffnmm{hv}", f"{tg}fm7_{hv}")
            fe = [f"{tg}ffnmm{hv}", "sprod"]
            if gi >= 1:
                fe.append(f"g{prevg}osum")
            E("vector", lambda e, hv=hv: e.tensor_scalar(
                YB[:64, hv * 512:(hv + 1) * 512], PS[2 + hv][:64, :], SPROD[:64, 0:1], None,
                Alu.mult), inc="V", deps=fe, tag=f"{tg}ffnev{hv}")
        E("vector", lambda e: e.tensor_tensor(YB[:64, :], YB[:64, :], BIAS64[:, :], Alu.add),
          inc="V", deps=[f"{tg}ffnev0", f"{tg}ffnev1", "bias64"], tag=f"{tg}yb")
        od = [f"{tg}yb"]
        E("vector", lambda e: e.tensor_tensor(XRES[:64, :], X1[:64, :], YB[:64, :], Alu.add),
          inc="V", deps=od, tag=f"{tg}osum")
        E("sync", lambda e, g=g: e.dma_start(
            out=out[g * 64:(g + 1) * 64, :], in_=XRES[:64, :]), dinc="SPO",
            deps=[f"{tg}osum"], tag=f"{tg}ost")
        pNB0 = NB

    sc.tags["FINAL"] = [(nm, sc.cnt[nm]) for nm in ("SPO", "SPV", "SPQ", "SPG", "SPK0", "SPK1")]
    sc.steps["sync"].append((("FINAL",), lambda e: None, None, 0, 0))
    sc.build()
    return nc


def _shard_inputs(inputs):
    x = np.asarray(inputs["L0_attn_x"])
    owns = [np.concatenate([np.arange(8 * o, 8 * o + 8) for o in range(h, 512, 2)])
            for h in (0, 1)]
    shared = {
        "wq": np.asarray(inputs["L0_attn_Wq"]),
        "wk": np.asarray(inputs["L0_attn_Wk"]),
        "wv": np.asarray(inputs["L0_attn_Wv"]),
        "wg": np.asarray(inputs["L0_attn_Wg"]),
        "wo": np.asarray(inputs["L0_attn_Wo"]),
        "wpre": np.asarray(inputs["L0_attn_W_pre_th"]),
        "wpost": np.asarray(inputs["L0_attn_W_post_th"]),
        "memk": np.asarray(inputs["L0_attn_mem_k"]),
        "memv": np.asarray(inputs["L0_attn_mem_v"]),
        "hsc": np.ascontiguousarray(np.asarray(inputs["L0_attn_head_scale_params"]).reshape(H, 1)),
        "bg": np.asarray(inputs["L0_attn_bg"]),
        "fker": np.asarray(inputs["L0_ffn_kernel"]),
        "fbias": np.ascontiguousarray(np.asarray(inputs["L0_ffn_bias"]).reshape(1, DIM)),
        "s_in": np.ascontiguousarray(np.asarray(inputs["L0_ffn_input_scale"]).reshape(1, 1)),
        "s_k": np.ascontiguousarray(np.asarray(inputs["L0_ffn_kernel_scale"]).reshape(1, 1)),
    }
    cc = [host_consts(0), host_consts(1)]
    maps = []
    for c in range(8):
        b, half = c // 2, c % 2
        m = dict(shared)
        m["xfull"] = np.ascontiguousarray(x[b])
        m["xq"] = np.ascontiguousarray(x[b][owns[half]])
        m.update(cc[half])
        maps.append(m)
    return maps, owns


_NC_CACHE = {}


def kernel(**inputs):
    if "prog" not in _NC_CACHE:
        _NC_CACHE["prog"] = build_program()
    nc = _NC_CACHE["prog"]
    maps, owns = _shard_inputs(inputs)
    res = run_bass_kernel_spmd(nc, maps, core_ids=list(range(8)))
    outp = np.zeros((B, N, DIM), dtype=np.float32)
    for c in range(8):
        b, half = c // 2, c % 2
        outp[b][owns[half]] = res.results[c]["out"]
    return outp


# version 17
# speedup vs baseline: 1.2098x; 1.0009x over previous
import numpy as np
import ml_dtypes
import concourse.bass as bass
import concourse.mybir as mybir
from concourse.alu_op_type import AluOpType
from concourse.bass_utils import run_bass_kernel_spmd

F32 = mybir.dt.float32
BF16 = mybir.dt.bfloat16
I32 = mybir.dt.int32
F8 = mybir.dt.float8e4
AF = mybir.ActivationFunctionType
Alu = AluOpType

B, N, DIM = 4, 4096, 1024
H, DH, NPAIR = 16, 64, 8
NMEM, TOPK = 4, 64
QK_SCALE = 10.0
NTOK = N + NMEM
KW = 4224
NG = 32
OWNQ = 2048
NTILE = N // 128
NQT = OWNQ // 128
EPS_LN = 1e-5
NROUND = 14


def octet_q0_worst(g, i):
    return 128 * g + 16 * i + 8


def octet_width(g, i):
    return min(KW, ((octet_q0_worst(g, i) + 13 + 127) // 128) * 128)


class Sched:
    def __init__(self, nc, sem_names):
        self.nc = nc
        self.sems = {}
        self.cnt = {}
        self.tags = {}
        self.order = ("sync", "gpsimd", "vector", "scalar", "tensor")
        self.steps = {e: [] for e in self.order}
        self._h = []
        for nm in sem_names:
            h = nc.alloc_semaphore(name=nm)
            self.sems[nm] = h
            self.cnt[nm] = 0
            self._h.append(h)

    def sem_range(self):
        nums = sorted(h.num for h in self._h)
        assert nums == list(range(nums[0], nums[0] + len(nums))), nums
        return range(nums[0], nums[-1] + 1)

    def emit(self, eng, fn, inc=None, dinc=None, deps=(), tag=None):
        sem, amt = (inc, 1) if inc is not None else ((dinc, 16) if dinc is not None else (None, 0))
        if sem is not None:
            self.cnt[sem] += amt
            if tag is not None:
                self.tags[tag] = [(sem, self.cnt[sem])]
        else:
            assert tag is None
        self.steps[eng].append((tuple(deps), fn, sem, self.cnt[sem] if sem else 0, amt))

    def alias(self, newtag, oldtag):
        self.tags[newtag] = self.tags[oldtag]

    def build(self):
        nc, sems = self.nc, self.sems

        def run_stream(e, name):
            seen = {}
            for deps, fn, sem, val, amt in self.steps[name]:
                best = {}
                for d in deps:
                    v = self.tags[d]
                    for (ws, wv) in (v if isinstance(v, list) else [v]):
                        if best.get(ws, 0) < wv:
                            best[ws] = wv
                for ws, wv in best.items():
                    if seen.get(ws, 0) < wv:
                        e.wait_ge(sems[ws], wv)
                        seen[ws] = wv
                ins = fn(e)
                if sem is not None:
                    assert ins is not None
                    ins.then_inc(sems[sem], amt)

        with nc.Block() as blk:
            @blk.sync
            def _(e):
                run_stream(e, "sync")

            @blk.gpsimd
            def _(e):
                run_stream(e, "gpsimd")

            @blk.vector
            def _(e):
                run_stream(e, "vector")

            @blk.scalar
            def _(e):
                run_stream(e, "scalar")

            @blk.tensor
            def _(e):
                run_stream(e, "tensor")


def host_consts(half):
    ident = np.eye(128, dtype=ml_dtypes.bfloat16)
    tpl = np.zeros((128, 528), dtype=ml_dtypes.bfloat16)
    for p in range(128):
        tpl[p, : 384 + (p % 8) - (8 if half == 0 else 0)] = 1.0
    selpair = np.zeros((8, 128, 16), dtype=ml_dtypes.bfloat16)
    selb = np.zeros((8, 16, 128), dtype=ml_dtypes.bfloat16)
    for pr in range(8):
        for p in range(128):
            selpair[pr, p, 2 * pr + (p >= 64)] = 1.0
            selb[pr, 2 * pr + (p >= 64), p] = 1.0
    selh = np.zeros((16, 128), dtype=ml_dtypes.bfloat16)
    for p in range(128):
        selh[p // 8, p] = 1.0
    qsel = np.zeros((128, 128), dtype=ml_dtypes.bfloat16)
    for p in range(128):
        for n in range(128):
            if p % 8 == n % 8:
                qsel[p, n] = 1.0
    return {"c_ident": ident, "c_msk": tpl, "c_selpair": selpair,
            "c_selbf": selb.astype(np.float32), "c_selh": selh, "c_qsel": qsel}


def build_program(taps=(), groups=None, p2stage=16):
    if groups is None:
        groups = range(NG)
    groups = list(groups)
    nc = bass.Bass("TRN2", target_bir_lowering=False)

    def din(name, shape, dt):
        return nc.dram_tensor(name, shape, dt, kind="ExternalInput").ap()

    xfull = din("xfull", [N, DIM], F32)
    xq = din("xq", [OWNQ, DIM], F32)
    wq, wk, wv, wg, wo = (din(n, [DIM, DIM], F32) for n in ("wq", "wk", "wv", "wg", "wo"))
    wpre = din("wpre", [16, 16], F32)
    wpost = din("wpost", [16, 16], F32)
    memk = din("memk", [H, NMEM, DH], F32)
    memv = din("memv", [H, NMEM, DH], F32)
    hsc = din("hsc", [H, 1], F32)
    bg = din("bg", [DIM], F32)
    fker = din("fker", [DIM, DIM], BF16)
    fbias = din("fbias", [1, DIM], BF16)
    sin_ = din("s_in", [1, 1], F32)
    sk_ = din("s_k", [1, 1], F32)
    c_ident = din("c_ident", [128, 128], BF16)
    c_msk = din("c_msk", [128, 528], BF16)
    c_selpair = din("c_selpair", [8, 128, 16], BF16)
    c_selbf = din("c_selbf", [8, 16, 128], F32)
    c_selh = din("c_selh", [16, 128], BF16)
    c_qsel = din("c_qsel", [128, 128], BF16)

    out = nc.dram_tensor("out", [OWNQ, DIM], F32, kind="ExternalOutput").ap()
    def kind_of(n):
        return "ExternalOutput" if n in taps else "Internal"
    vscr = nc.dram_tensor("vscr", [KW, DIM], BF16, kind=kind_of("vscr")).ap()
    qscr = nc.dram_tensor("qscr", [NPAIR, 128, OWNQ], BF16, kind=kind_of("qscr")).ap()
    gscr = nc.dram_tensor("gscr", [NPAIR, 128, OWNQ], BF16, kind=kind_of("gscr")).ap()
    kscr = nc.dram_tensor("kscr", [8, 128, DIM], F8, kind="Internal").ap()

    def S(name, shape, dt):
        return nc.alloc_sbuf_tensor(name, shape, dt).ap()

    KR = [S(f"KR{p}", [128, KW], BF16) for p in range(NPAIR)]
    EB = S("EB", [128, 8 * KW], BF16)
    WBA = S("WBA", [128, 8 * DIM], BF16)
    WBB = S("WBB", [128, 8 * DIM], BF16)
    KQ8R = S("KQ8R", [128, 2 * DIM], F8)
    IDE = S("IDE", [128, 128], BF16)
    MSK = S("MSK", [128, 528], BF16)
    SELP = S("SELP", [128, 8 * 16], BF16)
    SELBF = S("SELBF", [16, 8 * 128], F32)
    SELH = S("SELH", [16, 128], BF16)
    QSEL = EB[:, 7 * KW: 7 * KW + 128]
    WREP = EB[:16, 7 * KW + 128: 7 * KW + 256]
    BDPRE = S("BDPRE", [128, 128], BF16)
    BDPOST = S("BDPOST", [128, 128], BF16)
    BDP = [S(f"BDP{i}", [128, 128], BF16) for i in range(8)]
    MHATN = S("MHATN", [128, 1], F32)
    W16 = S("W16", [16, 16], F32)
    W16B = S("W16B", [16, 16], BF16)
    WPR = S("WPR", [16, 16], F32)
    WPOB = S("WPOB", [16, 16], F32)
    ABSW = S("ABSW", [16, 16], F32)
    MH16 = S("MH16", [16, 1], F32)
    MH16B = S("MH16B", [16, 1], BF16)
    RINV = EB.bitcast(F32)[:16, 896:1152]
    HS16 = S("HS16", [16, 1], F32)
    HSV = S("HSV", [128, 8], F32)
    BGS = S("BGS", [128, 8], F32)
    BIAS64 = S("BIAS64", [64, DIM], BF16)
    SPROD = S("SPROD", [128, 1], F32)
    EPSV = S("EPSV", [128, 1], F32)
    ONESF = S("ONESF", [1, 128], F32)
    ONESB = S("ONESB", [1, 64], BF16)
    SREC = S("SREC", [128, 4], F32)
    XT = [EB[:, (5 + i) * KW: (5 + i) * KW + 2 * DIM].bitcast(F32) for i in range(2)]
    XLN = EB[:, 0 * KW: 0 * KW + DIM]
    XSQ = EB[:, 1 * KW: 1 * KW + DIM]
    XLT = EB[:, 2 * KW: 2 * KW + DIM]
    XLTS = [XLT, EB[:, 2304:2304 + DIM]]
    STAT = S("STAT", [128, 8], F32)
    SQT = EB[:, 1024:1280]
    RNB = EB.bitcast(F32)[:, 640:896]
    MCP = S("MCP", [128, NMEM], F32)
    E24 = S("E24", [16, 1], F32)
    VST = EB[:, 3 * KW: 3 * KW + DIM]
    QNT = S("QNT", [128, 128], BF16)
    FKT = EB[:, 4 * KW: 4 * KW + DIM]
    S1 = [S(f"S1_{i}", [128, 1024], BF16) for i in range(2)]
    QS = S("QS", [128, 1024], BF16)
    VS = [S(f"VS{i}", [128, DIM], BF16) for i in range(3)]
    ATT = S("ATT", [128, 1024], BF16)
    LO = S("LO", [128, 8], I32)
    TF = S("TF", [128, 8], I32)
    TFN = S("TFN", [128, 8], I32)
    CMPV = WBB.bitcast(F32)[:, 4078:4086]
    CNT4 = WBB.bitcast(F32)[:, 4086:4096]
    CNT = S("CNT", [128, 8], F32)
    GEF = S("GEF", [128, 8], I32)
    SUM8 = S("SUM8", [128, 8], F32)
    RS8 = S("RS8", [128, 8], F32)
    GS = S("GS", [128, 512], BF16)
    XRES = S("XRES", [64, DIM], F32)
    X1 = S("X1", [64, DIM], F32)
    H2B = S("H2B", [64, DIM], BF16)
    XQ8 = S("XQ8", [128, 512], F8)
    AOUT = S("AOUT", [128, 512], BF16)
    YB = S("YB", [64, DIM], BF16)

    PS = [nc.alloc_psum_tensor(f"ps{i}", [128, 512], F32).ap() for i in range(6)]
    PT = [nc.alloc_psum_tensor(f"pt{i}", [128, 1024], BF16).ap() for i in range(2)]

    sc = Sched(nc, ["DC", "DCG", "D0B", "D0BG", "DF", "SPK0", "SPK1", "X0", "X1", "SPV", "SPQ", "SPG",
                "DW2", "DQS", "DGS", "DXR", "DV0", "DV1", "DV2", "DK0", "DK1", "SPO",
                "T", "V", "A", "G"])
    nc.gpsimd.dma_reset(sc.sem_range())
    nc.gpsimd.sem_clear(sc.sem_range())
    nc.all_engine_barrier()
    E = sc.emit

    for wnm, (wt, wdst) in [("wk", (wk, WBA)), ("wv", (wv, WBB))]:
        for ck in range(8):
            E("gpsimd", lambda e, wt=wt, wdst=wdst, ck=ck: e.dma_start(
                out=wdst[:, ck * DIM:(ck + 1) * DIM], in_=wt[ck * 128:(ck + 1) * 128, :]),
                dinc="DCG", tag=f"{wnm}c{ck}")
    E("sync", lambda e: e.dma_start(out=IDE[:, :], in_=c_ident[:, :]), dinc="DC", tag="ident")
    E("sync", lambda e: e.dma_start(out=MSK[:, :], in_=c_msk[:, :]), dinc="DC", tag="msk")
    E("sync", lambda e: e.dma_start(
        out=SELP.rearrange("p (a m) -> p a m", a=8), in_=c_selpair.rearrange("a p m -> p a m")),
        dinc="DC", tag="selp")
    E("sync", lambda e: e.dma_start(
        out=SELBF.rearrange("h (a p) -> h a p", a=8), in_=c_selbf.rearrange("a h p -> h a p")),
        dinc="DC", tag="selbf")
    E("sync", lambda e: e.dma_start(out=SELH[:, :], in_=c_selh[:, :]), dinc="DC", tag="selh")
    E("sync", lambda e: e.dma_start(out=QSEL[:, :], in_=c_qsel[:, :]), dinc="DC", tag="qsel")
    E("sync", lambda e: e.dma_start(out=HS16[:, :], in_=hsc[:, :]), dinc="DC", tag="hs")
    for ck in range(8):
        E("sync", lambda e, ck=ck: e.dma_start(
            out=BGS[:, ck:ck + 1], in_=bg.rearrange("(c p) -> c p", p=128)[ck].unsqueeze(1)),
            dinc="DC", tag=(f"bgs" if ck == 7 else f"bgs{ck}"))
    E("sync", lambda e: e.dma_start(out=SREC[:1, 0:1], in_=sin_[:, :]), dinc="DC", tag="sin")
    E("sync", lambda e: e.dma_start(out=SREC[:1, 1:2], in_=sk_[:, :]), dinc="DC", tag="sk")
    E("sync", lambda e: e.dma_start(out=BIAS64[:1, :], in_=fbias[:, :]), dinc="DC", tag="fb0")
    E("sync", lambda e: e.dma_start(out=W16[:, :], in_=wpre.rearrange("a b -> b a"), allow_slow_non_contiguous=True), dinc="DC", tag="wpreT")
    E("sync", lambda e: e.dma_start(out=WPR[:, :], in_=wpre[:, :]), dinc="DC", tag="wpre")
    E("sync", lambda e: e.dma_start(out=WPOB[:, :], in_=wpost.rearrange("a b -> b a"), allow_slow_non_contiguous=True), dinc="DC", tag="wpostT")
    for h in range(H):
        E("gpsimd", lambda e, h=h: e.dma_start(
            out=KR[h // 2][64 * (h % 2):64 * (h % 2) + 64, 0:NMEM],
            in_=memk[h].rearrange("a b -> b a"), allow_slow_non_contiguous=True),
            dinc="DCG", tag=f"memk{h}")
        E("gpsimd", lambda e, h=h: e.dma_start(
            out=VST[0:NMEM, h * 64:(h + 1) * 64], in_=memv[h, :, :]), dinc="DCG", tag=f"memv{h}")
    sc.tags["DCALL"] = [("DC", sc.cnt["DC"]), ("DCG", sc.cnt["DCG"])]
    sc.alias("memvall", "DCALL")

    E("vector", lambda e: e.tensor_scalar(W16B[:, :], W16[:, :], QK_SCALE, None, Alu.mult),
      inc="V", deps=["DCALL"], tag="w16b")
    E("vector", lambda e: e.tensor_copy(
        WREP.rearrange("g (h r) -> g h r", r=8),
        W16B.rearrange("g h -> g h 1" if False else "g (h o) -> g h o", o=1).to_broadcast([16, 16, 8])),
        inc="V", deps=["w16b"], tag="wrepA")
    E("tensor", lambda e: e.matmul(PS[5][:, 128:256], SELH[:, :], WREP[:, :],
      start=True, stop=True), inc="T", deps=["wrepA", "DCALL"], tag="bdpremm")
    E("vector", lambda e: e.tensor_tensor(BDPRE[:, :], PS[5][:, 128:256], QSEL[:, :], Alu.mult),
      inc="V", deps=["bdpremm"], tag="bdpre7")
    E("vector", lambda e: e.tensor_copy(W16B[:, :], WPOB[:, :]), inc="V",
      deps=["DCALL", "bdpre7"], tag="w16post")
    E("vector", lambda e: e.tensor_copy(
        WREP.rearrange("g (h r) -> g h r", r=8),
        W16B.rearrange("g (h o) -> g h o", o=1).to_broadcast([16, 16, 8])),
        inc="V", deps=["w16post"], tag="wrepB")
    E("tensor", lambda e: e.matmul(PS[5][:, 128:256], SELH[:, :], WREP[:, :],
      start=True, stop=True), inc="T", deps=["wrepB", "bdpre7"], tag="bdpostmm")
    E("vector", lambda e: e.tensor_tensor(BDPOST[:, :], PS[5][:, 128:256], QSEL[:, :], Alu.mult),
      inc="V", deps=["bdpostmm"], tag="bdpost7")
    E("scalar", lambda e: e.activation(ABSW[:, :], WPR[:, :], AF.Abs), inc="A",
      deps=["DCALL"], tag="absw")
    E("vector", lambda e: e.tensor_reduce(MH16[:, 0:1], ABSW[:, :],
      axis=mybir.AxisListType.X, op=Alu.add), inc="V", deps=["absw"], tag="mh16")
    E("vector", lambda e: e.tensor_copy(MH16B[:, :], MH16[:, 0:1]), inc="V",
      deps=["mh16"], tag="mh16b")
    E("tensor", lambda e: e.matmul(PS[5][:, 0:1], SELH[:, :], MH16B[:, 0:1], start=True, stop=True),
      inc="T", deps=["mh16b", "DCALL"], tag="mhbc")
    E("vector", lambda e: e.tensor_scalar(MHATN[:, 0:1], PS[5][:, 0:1], -QK_SCALE, None, Alu.mult),
      inc="V", deps=["mhbc"], tag="mhat")
    for ck in range(8):
        E("tensor", lambda e, ck=ck: e.matmul(
            PS[5][:, 64 + ck:65 + ck], SELBF[:, ck * 128:(ck + 1) * 128], HS16[:, 0:1],
            start=True, stop=True), inc="T", deps=["DCALL", "DCALL", "mhat"],
            tag=f"hsb{ck}")
    E("vector", lambda e: e.tensor_copy(HSV[:, :], PS[5][:, 64:72]), inc="V",
      deps=[f"hsb{c}" for c in range(8)], tag="hsv")
    E("gpsimd", lambda e: e.memset(ONESF[:, :], 1.0), inc="G", tag="onesf")
    E("gpsimd", lambda e: e.memset(EPSV[:, :], EPS_LN), inc="G", tag="epsv")
    E("gpsimd", lambda e: e.memset(E24[:, :], 1e-24), inc="G", tag="eps24")
    E("gpsimd", lambda e: e.memset(ONESB[:, :], 1.0), inc="G", tag="onesb")
    E("gpsimd", lambda e: e.memset(
        QS.rearrange("p (a q) -> p a q", a=16)[64:128, 0::2, :], 0.0), inc="G", tag="qz0")
    E("gpsimd", lambda e: e.memset(
        QS.rearrange("p (a q) -> p a q", a=16)[0:64, 1::2, :], 0.0), inc="G", tag="qz1")
    E("tensor", lambda e: e.matmul(PS[4][:, 0:2], ONESF[:, :], SREC[:1, 0:2],
      start=True, stop=True), inc="T", deps=["DCALL", "DCALL", "onesf"], tag="sbcmm")
    E("vector", lambda e: e.tensor_copy(SREC[:, 0:2], PS[4][:, 0:2]), inc="V",
      deps=["sbcmm"], tag="sbc")
    E("vector", lambda e: e.tensor_tensor(SPROD[:, 0:1], SREC[:, 0:1], SREC[:, 1:2], Alu.mult),
      inc="V", deps=["sbc"], tag="sprod")
    E("vector", lambda e: e.reciprocal(SREC[:, 2:4], SREC[:, 0:2]), inc="V",
      deps=["sprod"], tag="srec")
    for bh in range(2):
        E("tensor", lambda e, bh=bh: e.matmul(PS[4][:64, :], ONESB[:, :],
          BIAS64[:1, bh * 512:(bh + 1) * 512], start=True, stop=True), inc="T",
          deps=["DCALL", "onesb", "sbc"] + ([f"bias64_0"] if bh else []), tag=f"bias64mm{bh}")
        E("vector", lambda e, bh=bh: e.tensor_copy(
            BIAS64[:, bh * 512:(bh + 1) * 512], PS[4][:64, :]), inc="V",
            deps=[f"bias64mm{bh}"], tag=f"bias64_{bh}")
    sc.alias("bias64", "bias64_1")
    for ck in range(8):
        E("sync", lambda e, ck=ck: e.dma_start(out=FKT[:, :], in_=fker[ck * 128:(ck + 1) * 128, :]),
          dinc="DF", deps=([f"kq8_{ck-1}"] if ck else []), tag=f"fkt{ck}")
        E("vector", lambda e, ck=ck: e.tensor_scalar(
            KQ8R[:, (ck % 2) * DIM:(ck % 2 + 1) * DIM], FKT[:, :], SREC[:, 3:4], None, Alu.mult),
            inc="V", deps=[f"fkt{ck}", "srec"] + ([f"kq8sp_{ck-2}"] if ck >= 2 else []),
            tag=f"kq8_{ck}")
        E("sync", lambda e, ck=ck: e.dma_start(
            out=kscr[ck, :, :], in_=KQ8R[:, (ck % 2) * DIM:(ck % 2 + 1) * DIM]),
            dinc=f"SPK{ck % 2}", deps=[f"kq8_{ck}"], tag=f"kq8sp_{ck}")
    E("sync", lambda e: e.dma_start(out=vscr[0:NMEM, :], in_=VST[0:NMEM, :]),
      dinc="D0B", deps=["memvall"], tag="memvsp")
    E("gpsimd", lambda e: e.memset(VST[:, :], 0.0), inc="G", deps=["memvsp"], tag="vz0")
    E("sync", lambda e: e.dma_start(out=vscr[4100:KW, :], in_=VST[0:KW - 4100, :]),
      dinc="D0B", deps=["vz0"], tag="vztail")
    for p in range(NPAIR):
        E("gpsimd", lambda e, p=p: e.memset(KR[p][:, 4100:KW], 0.0), inc="G", tag=f"kz{p}")
    sc.tags["D0BALL"] = [("D0B", sc.cnt["D0B"]), ("D0BG", sc.cnt["D0BG"])]

    def emit_ln(tg, xt_ap, dep, extra_ln_deps=()):
        E("vector", lambda e: e.tensor_reduce(
            STAT[:, 0:1], xt_ap, axis=mybir.AxisListType.X, op=Alu.add),
            inc="V", deps=[dep], tag=f"{tg}sum")
        E("scalar", lambda e: e.activation(XSQ[:, :], xt_ap, AF.Square,
          accum_out=STAT[:, 1:2]), inc="A", deps=[dep], tag=f"{tg}sq")

        E("vector", lambda e: e.tensor_scalar(STAT[:, 0:1], STAT[:, 0:1], 1.0 / DIM, None, Alu.mult),
          inc="V", deps=[f"{tg}sum"], tag=f"{tg}mean")
        E("vector", lambda e: e.tensor_scalar(STAT[:, 2:3], STAT[:, 1:2], 1.0 / DIM, None, Alu.mult),
          inc="V", deps=[f"{tg}sq"], tag=f"{tg}ex2")
        E("vector", lambda e: e.tensor_tensor(STAT[:, 3:4], STAT[:, 0:1], STAT[:, 0:1], Alu.mult),
          inc="V", deps=[f"{tg}mean"], tag=f"{tg}m2")
        E("vector", lambda e: e.tensor_tensor(STAT[:, 2:3], STAT[:, 2:3], STAT[:, 3:4], Alu.subtract),
          inc="V", deps=[f"{tg}ex2", f"{tg}m2"], tag=f"{tg}var")
        E("scalar", lambda e: e.activation(STAT[:, 4:5], STAT[:, 2:3], AF.Ln, bias=EPSV[:, 0:1]),
          inc="A", deps=[f"{tg}var", "epsv"], tag=f"{tg}sd")
        E("scalar", lambda e: e.activation(STAT[:, 5:6], STAT[:, 4:5], AF.Exp, scale=-0.5),
          inc="A", deps=[f"{tg}sd"], tag=f"{tg}rstd")
        E("vector", lambda e: e.tensor_scalar(
            XLN[:, :], xt_ap, STAT[:, 0:1], STAT[:, 5:6], Alu.subtract, Alu.mult),
            inc="V", deps=[f"{tg}rstd"] + list(extra_ln_deps), tag=f"{tg}ln")

    def emit_tp(tg, prev_xlt_users, sl, slot_war):
        for fc in range(8):
            d = [f"{tg}ln", "DCALL"]
            if fc == 0:
                d += prev_xlt_users
            E("tensor", lambda e, fc=fc: e.transpose(
                PT[0][:, fc * 128:(fc + 1) * 128], XLN[:, fc * 128:(fc + 1) * 128], IDE[:, :]),
                inc="T", deps=d, tag=f"{tg}tp{fc}")
        E("vector", lambda e, sl=sl: e.tensor_copy(XLTS[sl][:, :], PT[0][:, :]),
          inc="V", deps=[f"{tg}tp7"] + slot_war, tag=f"{tg}xlt")

    _l2prev = {0: None, 1: None}
    _l2cnt = [0]

    def emit_l2norm(tg, src_ap, pr, ncols, write_fn, dep, _ignored, extra_done_deps=()):
        par = _l2cnt[0] % 2
        _l2cnt[0] += 1
        o = 128 * par
        prev_tag = _l2prev[par]
        _l2prev[par] = tg
        sqd = [dep]
        if prev_tag:
            sqd.append(f"{prev_tag}ss")
        E("scalar", lambda e: e.activation(SQT[:, o:o + ncols], src_ap, AF.Square),
          inc="A", deps=sqd, tag=f"{tg}sq")
        ssd = [f"{tg}sq", "DCALL"]
        if prev_tag:
            ssd.append(f"{prev_tag}sd")
        else:
            ssd.append("sbc")
            ssd.append("bias64_1")
        E("tensor", lambda e: e.matmul(
            PS[4][:16, 256 + o:256 + o + ncols], SELP[:, pr * 16:(pr + 1) * 16], SQT[:, o:o + ncols],
            start=True, stop=True), inc="T", deps=ssd, tag=f"{tg}ss")
        sdd = [f"{tg}ss", "eps24"]
        if prev_tag:
            sdd.append(f"{prev_tag}done")
        E("scalar", lambda e: e.activation(RNB[:16, o:o + ncols], PS[4][:16, 256 + o:256 + o + ncols],
          AF.Ln, bias=E24[:16, 0:1]), inc="A", deps=sdd, tag=f"{tg}sd")
        rcd = [f"{tg}sd"]
        if prev_tag:
            rcd.append(f"{prev_tag}bc")
        E("scalar", lambda e: e.activation(RINV[:, o:o + ncols], RNB[:16, o:o + ncols],
          AF.Exp, scale=-0.5), inc="A", deps=rcd, tag=f"{tg}rc")
        bcd = [f"{tg}rc", "DCALL"]
        if prev_tag:
            bcd.append(f"{prev_tag}rb")
        else:
            bcd.append("hsv")
        E("tensor", lambda e: e.matmul(
            PS[5][:, o:o + ncols], SELBF[:, pr * 128:(pr + 1) * 128], RINV[:, o:o + ncols],
            start=True, stop=True), inc="T", deps=bcd, tag=f"{tg}bc")
        E("vector", lambda e: e.tensor_copy(RNB[:, o:o + ncols], PS[5][:, o:o + ncols]),
          inc="V", deps=[f"{tg}bc"], tag=f"{tg}rb")
        E("vector", lambda e: write_fn(e, o), inc="V",
          deps=[f"{tg}rb"] + list(extra_done_deps), tag=f"{tg}done")

    prev_chain = None
    for t in range(NTILE):
        xt_ap = XT[t % 2][:, :]
        tg = f"a{t}"
        xdeps = [f"a{t-2}ln"] if t >= 2 else []
        E("sync", lambda e, t=t, xt_ap=xt_ap: e.dma_start(
            out=xt_ap, in_=xfull[t * 128:(t + 1) * 128, :]), dinc=f"X{t % 2}", deps=xdeps, tag=f"{tg}x")
        lnd = [f"a{t-1}tp7"] if t >= 1 else []
        emit_ln(tg, xt_ap, f"{tg}x", extra_ln_deps=lnd)
        prevu = [f"a{t-1}xlt"] if t >= 1 else []
        swar = [f"a{t-2}xltfree"] if t >= 2 else []
        emit_tp(tg, prevu, t % 2, swar)
        for pr in range(NPAIR):
            bank = PS[pr % 2]
            slot = (pr // 2) * 128
            mmd = [f"{tg}xlt", "DCALL"]
            if t >= 1:
                mmd.append(f"a{t-1}k{pr}done")
            if pr >= 2:
                mmd.append(f"{tg}k{pr-2}done")
            for fc in range(8):
                E("tensor", lambda e, pr=pr, fc=fc, bank=bank, slot=slot, sl=t % 2: e.matmul(
                    bank[:, slot:slot + 128],
                    WBA[:, fc * DIM + pr * 128:fc * DIM + (pr + 1) * 128],
                    XLTS[sl][:, fc * 128:(fc + 1) * 128],
                    start=(fc == 0), stop=(fc == 7)), inc="T",
                    deps=(mmd if fc == 0 else [f"{tg}xlt"]),
                    tag=(f"{tg}k{pr}mm" if fc == 7 else None))
            def wk_out(e, o, pr=pr, t=t, bank=bank, slot=slot):
                return e.tensor_tensor(KR[pr][:, NMEM + t * 128: NMEM + (t + 1) * 128],
                                       bank[:, slot:slot + 128], RNB[:, o:o + 128], Alu.mult)
            emit_l2norm(f"{tg}k{pr}", bank[:, slot:slot + 128], pr, 128, wk_out,
                        f"{tg}k{pr}mm", prev_chain)
            prev_chain = f"{tg}k{pr}"
        for hv in range(2):
            vmd = [f"{tg}xlt", "DCALL"]
            if t >= 1:
                vmd.append(f"a{t-1}v{hv}ev")
            for fc in range(8):
                E("tensor", lambda e, hv=hv, fc=fc, sl=t % 2: e.matmul(
                    PS[2 + hv][:, :], XLTS[sl][:, fc * 128:(fc + 1) * 128],
                    WBB[:, fc * DIM + hv * 512: fc * DIM + (hv + 1) * 512],
                    start=(fc == 0), stop=(fc == 7)), inc="T",
                    deps=(vmd if fc == 0 else [f"{tg}xlt"]),
                    tag=(f"{tg}v{hv}mm" if fc == 7 else None))
            evd = [f"{tg}v{hv}mm"]
            if t >= 1:
                evd.append(f"a{t-1}vsp")
            else:
                evd.append("D0BALL")
            E("scalar", lambda e, hv=hv: e.copy(VST[:, hv * 512:(hv + 1) * 512], PS[2 + hv][:, :]),
              inc="A", deps=evd, tag=f"{tg}v{hv}ev")
        E("sync", lambda e, t=t: e.dma_start(
            out=vscr[NMEM + t * 128: NMEM + (t + 1) * 128, :], in_=VST[:, :]),
            dinc="SPV", deps=[f"{tg}v0ev", f"{tg}v1ev"], tag=f"{tg}vsp")
        sc.alias(f"{tg}xltfree", f"{tg}v1mm")

    for pr in range(NPAIR):
        tg = f"mn{pr}"
        cpd = ["DCALL", f"a{NTILE-1}k{pr}done"]
        if pr >= 1:
            cpd.append(f"mn{pr-1}done")
        E("vector", lambda e, pr=pr: e.tensor_copy(MCP[:, :], KR[pr][:, 0:NMEM]),
          inc="V", deps=cpd, tag=f"{tg}cp")
        def wmem(e, o, pr=pr):
            return e.tensor_tensor(KR[pr][:, 0:NMEM], MCP[:, :], RNB[:, o:o + NMEM], Alu.mult)
        emit_l2norm(tg, MCP[:, :], pr, NMEM, wmem, f"{tg}cp", prev_chain)
        prev_chain = tg

    for ck in range(8):
        E("gpsimd", lambda e, ck=ck: e.dma_start(
            out=WBA[:, ck * DIM:(ck + 1) * DIM], in_=wq[ck * 128:(ck + 1) * 128, :]),
            dinc="DW2", deps=[f"a{NTILE-1}k{pr}mm" for pr in range(NPAIR)], tag=f"wqc{ck}")
        E("gpsimd", lambda e, ck=ck: e.dma_start(
            out=WBB[:, ck * DIM:(ck + 1) * DIM], in_=wg[ck * 128:(ck + 1) * 128, :]),
            dinc="DW2", deps=[f"a{NTILE-1}v0mm", f"a{NTILE-1}v1mm"], tag=f"wgc{ck}")

    sc.tags["wqgall"] = [("DW2", 16 * 16)]

    for t in range(NQT):
        xt_ap = XT[t % 2][:, :]
        tg = f"b{t}"
        xdeps = [f"b{t-2}ln"] if t >= 2 else [f"a{NTILE-1}ln", f"mn{NPAIR-1}done"]
        E("sync", lambda e, t=t, xt_ap=xt_ap: e.dma_start(
            out=xt_ap, in_=xq[t * 128:(t + 1) * 128, :]), dinc=f"X{t % 2}", deps=xdeps, tag=f"{tg}x")
        lnd = [f"b{t-1}tp7"] if t >= 1 else [f"a{NTILE-1}tp7"]
        emit_ln(tg, xt_ap, f"{tg}x", extra_ln_deps=lnd)
        prevu = [f"b{t-1}xlt"] if t >= 1 else [f"a{NTILE-1}xlt"]
        swar = ([f"b{t-2}xltfree"] if t >= 2 else
                [f"a{NTILE-2+t}xltfree"])
        emit_tp(tg, prevu, (NTILE + t) % 2, swar)
        for pr in range(NPAIR):
            bank = PS[pr % 2]
            slot = (pr // 2) * 128
            mmd = [f"{tg}xlt", "wqgall"]
            if t >= 1:
                mmd.append(f"b{t-1}q{pr}sp")
            if pr >= 2:
                mmd.append(f"{tg}q{pr-2}done")
            elif t == 0:
                mmd.append(f"a{NTILE-1}k{6+pr}done")
            for fc in range(8):
                E("tensor", lambda e, pr=pr, fc=fc, bank=bank, slot=slot, sl=(NTILE + t) % 2: e.matmul(
                    bank[:, slot:slot + 128],
                    WBA[:, fc * DIM + pr * 128:fc * DIM + (pr + 1) * 128],
                    XLTS[sl][:, fc * 128:(fc + 1) * 128],
                    start=(fc == 0), stop=(fc == 7)), inc="T",
                    deps=(mmd if fc == 0 else [f"{tg}xlt"]),
                    tag=(f"{tg}q{pr}mm" if fc == 7 else None))
            def wq_out(e, o, pr=pr, bank=bank, slot=slot):
                return e.tensor_tensor(QNT[:, :], bank[:, slot:slot + 128], RNB[:, o:o + 128], Alu.mult)
            ed = [f"b{t-1}q{NPAIR-1}sp"] if t >= 1 else []
            if pr >= 1:
                ed = [f"{tg}q{pr-1}sp"]
            emit_l2norm(f"{tg}q{pr}", bank[:, slot:slot + 128], pr, 128, wq_out,
                        f"{tg}q{pr}mm", prev_chain, extra_done_deps=ed)
            prev_chain = f"{tg}q{pr}"
            E("sync", lambda e, t=t, pr=pr: e.dma_start(
                out=qscr[pr, :, t * 128:(t + 1) * 128], in_=QNT[:, :]),
                dinc="SPQ", deps=[f"{tg}q{pr}done"], tag=f"{tg}q{pr}sp")
        for pr in range(NPAIR):
            bank = PS[2 + pr % 2]
            slot = (pr // 2) * 128
            mmd = [f"{tg}xlt", "wqgall"]
            if t >= 1:
                mmd.append(f"b{t-1}g{pr}sg")
            if pr >= 2:
                mmd.append(f"{tg}g{pr-2}sg")
            elif t == 0:
                mmd.append(f"a{NTILE-1}v{pr % 2}ev")
            for fc in range(8):
                E("tensor", lambda e, pr=pr, fc=fc, bank=bank, slot=slot, sl=(NTILE + t) % 2: e.matmul(
                    bank[:, slot:slot + 128],
                    WBB[:, fc * DIM + pr * 128:fc * DIM + (pr + 1) * 128],
                    XLTS[sl][:, fc * 128:(fc + 1) * 128],
                    start=(fc == 0), stop=(fc == 7)), inc="T",
                    deps=(mmd if fc == 0 else [f"{tg}xlt"]),
                    tag=(f"{tg}g{pr}mm" if fc == 7 else None))
            sgd = [f"{tg}g{pr}mm", "DCALL"]
            if pr >= 1:
                sgd.append(f"{tg}g{pr-1}sp")
            elif t >= 1:
                sgd.append(f"b{t-1}g{NPAIR-1}sp")
            else:
                sgd.append(f"a{NTILE-1}vsp")
            E("scalar", lambda e, pr=pr, bank=bank, slot=slot: e.activation(
                VST[:, pr * 128:(pr + 1) * 128], bank[:, slot:slot + 128], AF.Sigmoid,
                bias=BGS[:, pr:pr + 1]), inc="A", deps=sgd, tag=f"{tg}g{pr}sg")
            E("sync", lambda e, t=t, pr=pr: e.dma_start(
                out=gscr[pr, :, t * 128:(t + 1) * 128], in_=VST[:, pr * 128:(pr + 1) * 128]),
                dinc="SPG", deps=[f"{tg}g{pr}sg"], tag=f"{tg}g{pr}sp")
        sc.alias(f"{tg}xltfree", f"{tg}g{NPAIR-1}mm")

    lb = f"b{NQT-1}"
    sc.tags["woall"] = [("DW2", 16 * 24)]
    wodeps = [f"{lb}q{pr}mm" for pr in range(NPAIR)]
    for ck in range(8):
        E("gpsimd", lambda e, ck=ck: e.dma_start(
            out=WBA[:, ck * DIM:(ck + 1) * DIM], in_=wo[ck * 128:(ck + 1) * 128, :]),
            dinc="DW2", deps=wodeps, tag=f"woc{ck}")
        E("vector", lambda e, ck=ck: e.tensor_scalar(
            WBA[:, ck * DIM:(ck + 1) * DIM], WBA[:, ck * DIM:(ck + 1) * DIM],
            HSV[:, ck:ck + 1], None, Alu.mult), inc="V",
            deps=["woall", "hsv"], tag=f"wos{ck}")

    sc.tags["spkall0"] = [("SPK0", 16 * 4)]
    sc.tags["spkall1"] = [("SPK1", 16 * 4)]

    NGRP = len(groups)

    def ginfo(n):
        w = [octet_width(n, i) for i in range(8)]
        return w, max(w) // 128

    def e_qs(n, pg):
        t = f"g{n}"
        qd = [f"b{n//2}q{pr}sp" for pr in range(NPAIR)]
        if pg is not None:
            qd.append(f"g{pg}qkdone")
        E("sync", lambda e, n=n: e.dma_start(
            out=QS.rearrange("p (a q) -> p a q", a=16)[0:64, 0::2, :],
            in_=qscr.rearrange("a p q -> p a q")[0:64, :, n * 64:(n + 1) * 64]),
            dinc="DQS", deps=qd + ["qz0", "qz1"], tag=f"{t}qsa")
        E("sync", lambda e, n=n: e.dma_start(
            out=QS.rearrange("p (a q) -> p a q", a=16)[64:128, 1::2, :],
            in_=qscr.rearrange("a p q -> p a q")[64:128, :, n * 64:(n + 1) * 64]),
            dinc="DQS", deps=[f"{t}qsa"], tag=f"{t}qs")

    def e_ez(n, first):
        t = f"g{n}"
        w, _ = ginfo(n)
        Wm = max(w)
        ezt = []
        for i in range(8):
            if w[i] < Wm:
                ezd = ([f"b{NQT-1}tp7", f"b{NQT-1}g{NPAIR-1}sg", f"b{NQT-1}g{NPAIR-1}sp",
                        f"b{NQT-1}g{NPAIR-1}mm", "kq8_7", "bdpost7"] if first else [])
                E("gpsimd", lambda e, i=i, wi=w[i], Wm=Wm: e.memset(
                    EB[:, i * KW + wi: i * KW + Wm], 0.0), inc="G",
                    deps=ezd, tag=f"{t}ez{i}")
                ezt.append(f"{t}ez{i}")
        return ezt

    def e_qk(n, jb, extra02):
        t = f"g{n}"
        for p in range(8):
            bank = PS[2 + p // 4]
            col = (p % 4) * 128
            mmd = [f"{t}qs", f"a{NTILE-1}k{p}done", f"mn{p}done", f"kz{p}"]
            if jb >= 1:
                mmd.append(f"{t}ev1_{jb-1}")
            else:
                mmd += extra02
            sc_tag = f"{t}qk{jb}" if p == 7 else None
            E("tensor", lambda e, p=p, jb=jb, bank=bank, col=col: e.matmul(
                bank[:, col:col + 128],
                KR[p][:, jb * 128:(jb + 1) * 128],
                QS[:, p * 128:(p + 1) * 128],
                start=True, stop=True),
                inc="T", deps=mmd, tag=sc_tag)

    def e_ev1(n, jb, pg, pNBg):
        t = f"g{n}"
        for bi in range(2):
            def ev1(e, jb=jb, bi=bi):
                src = PS[2 + bi].rearrange("p (hh i q) -> p hh i q", hh=8, i=8, q=8)
                dst = S1[jb % 2].rearrange("p (i hh q) -> p hh i q", i=8, hh=16, q=8)[
                    :, bi * 8:(bi + 1) * 8, :, :]
                return e.tensor_copy(dst, src)
            e1d = [f"{t}qk{jb}"]
            if jb >= 2:
                e1d.append(f"{t}t1d_{jb-2}")
            elif pg is not None:
                e1d.append(f"g{pg}t1d_{pNBg-2+jb}")
            E("vector", ev1, inc="V", deps=e1d, tag=f"{t}ev1b{jb}_{bi}")
        sc.alias(f"{t}ev1_{jb}", f"{t}ev1b{jb}_1")

    def e_t1(n, jb, extra_first):
        t = f"g{n}"
        w, _ = ginfo(n)
        live = [i for i in range(8) if w[i] > jb * 128]
        ptb = PT[jb % 2]
        for i in live:
            t1d = [f"{t}ev1_{jb}"]
            if i == live[0]:
                if jb >= 2:
                    t1d.append(f"{t}ev2_{jb-2}")
                else:
                    t1d += extra_first
            E("tensor", lambda e, i=i, jb=jb, ptb=ptb: e.transpose(
                ptb[:, i * 128:(i + 1) * 128], S1[jb % 2][:, i * 128:(i + 1) * 128], IDE[:, :]),
                inc="T", deps=t1d, tag=f"{t}t1_{jb}_{i}")
        sc.alias(f"{t}t1d_{jb}", f"{t}t1_{jb}_{live[-1]}")

    def e_ev2(n, jb, gT2, NBg, ezt):
        t = f"g{n}"
        w, _ = ginfo(n)
        live = [i for i in range(8) if w[i] > jb * 128]
        ptb = PT[jb % 2]
        def ev2(e, jb=jb, live=tuple(live), ptb=ptb):
            last = None
            for i in live:
                last = e.copy(EB[:, i * KW + jb * 128: i * KW + (jb + 1) * 128],
                              ptb[:, i * 128:(i + 1) * 128])
            return last
        e2d = [f"{t}t1_{jb}_{i}" for i in live] + (ezt if jb == 0 else [])
        if gT2 is not None and jb < NBg:
            e2d.append(f"g{gT2}t2ev_{jb}")
        E("scalar", ev2, inc="A", deps=e2d, tag=f"{t}ev2_{jb}")

    def e_B(n, pg):
        t = f"g{n}"
        w, _ = ginfo(n)
        for i in range(8):
            wi = w[i]
            nch = (wi + 511) // 512
            for ci in range(nch):
                c0 = ci * 512
                cw = min(512, wi - c0)
                bank = PS[4 + ((i + ci) % 2)]
                pmd = [f"{t}ev2_{(c0 + cw - 1) // 128}", "bdpre7"]
                if ci >= 2:
                    pmd.append(f"{t}exp{i}_{ci-2}")
                elif i >= 1:
                    pmd.append(f"{t}exp{i-1}_bank{(i + ci) % 2}last")
                elif pg is not None:
                    pmd.append(f"g{pg}gateA")
                    pmd.append(f"g{pg}gate")
                E("tensor", lambda e, i=i, c0=c0, cw=cw, bank=bank: e.matmul(
                    bank[:, :cw], BDPRE[:, :], EB[:, i * KW + c0: i * KW + c0 + cw],
                    start=True, stop=True), inc="T", deps=pmd, tag=f"{t}pm{i}_{ci}")
                E("scalar", lambda e, i=i, c0=c0, cw=cw, bank=bank: e.activation(
                    EB[:, i * KW + c0: i * KW + c0 + cw], bank[:, :cw], AF.Exp,
                    bias=MHATN[:, 0:1]), inc="A", deps=[f"{t}pm{i}_{ci}", "mhat"],
                    tag=f"{t}exp{i}_{ci}")
            sc.alias(f"{t}exp{i}_last", f"{t}exp{i}_{nch-1}")
            for bank in range(2):
                cis = [ci for ci in range(nch) if (i + ci) % 2 == bank]
                sc.alias(f"{t}exp{i}_bank{bank}last",
                         f"{t}exp{i}_{cis[-1]}" if cis else f"{t}exp{i}_{nch-1}")

    ezt_by_g = {}
    for gi, g in enumerate(groups):
        tg = f"g{g}"
        prevg = groups[gi - 1] if gi >= 1 else None
        nextg = groups[gi + 1] if gi + 1 < NGRP else None
        widths, NB = ginfo(g)
        Wmax = max(widths)
        if nextg is not None:
            nwidths, NBn = ginfo(nextg)

        if gi == 0:
            e_qs(g, None)
            ezt_by_g[g] = e_ez(g, True)
            for jb in range(NB):
                e_qk(g, jb, [])
                e_ev1(g, jb, None, 0)
                e_t1(g, jb, [])
                e_ev2(g, jb, None, 0, ezt_by_g[g])
            sc.alias(f"{tg}qkdone", f"{tg}qk{NB-1}")
            for jb in range(NB, NB + 2):
                sc.alias(f"{tg}ev2_{jb}", f"{tg}ev2_{NB-1}")
                sc.alias(f"{tg}t1d_{jb}", f"{tg}t1d_{NB-1}")
            e_B(g, None)

        S012 = widths[0] + widths[1] + widths[2]
        S47 = sum(widths[4:8])
        cs3 = int((0.9 * (widths[3] + S47) - 1.25 * S012 + 1000) / 2.15) // 64 * 64
        cs3 = max(64, min(widths[3], cs3))
        csplit = cs3 < widths[3]
        for i in range(8):
            wi = widths[i]
            q0w = octet_q0_worst(g, i)
            c0m = max(0, wi - 384)
            off = 384 - q0w - 5 + c0m
            mkd = [f"{tg}exp{i}_last", "DCALL"]
            E("vector", lambda e, i=i, c0m=c0m, off=off, wi=wi: e.tensor_tensor(
                EB[:, i * KW + c0m: i * KW + wi], EB[:, i * KW + c0m: i * KW + wi],
                MSK[:, off: off + wi - c0m], Alu.mult), inc="V", deps=mkd, tag=f"{tg}msk{i}")
            if i < 3 or (i == 3 and not csplit):
                cv = 64.0
            elif i == 3:
                cv = float(128 - (widths[3] - cs3))
            else:
                cv = float(128 - wi)
            cvd = [f"g{prevg}upd_{NROUND-1}"] if gi >= 1 else [f"b{NQT-1}g{NPAIR-1}mm"]
            E("gpsimd", lambda e, i=i, cv=cv: e.memset(CMPV[:, i:i + 1], cv), inc="G",
              deps=cvd, tag=f"{tg}cmpv{i}")
        lod = [f"g{prevg}stt7"] if gi >= 1 else []
        E("gpsimd", lambda e: e.memset(LO[:, :], 0), inc="G", deps=lod, tag=f"{tg}lo")
        zslots = []
        for i in range(4, 8):
            nch = (widths[i] + 2111) // 2112
            for ci in range(nch, 2):
                zslots.append((i - 3) * 2 + ci)
        zdep = [f"g{prevg}csum_{NROUND-1}"] if gi >= 1 else [f"b{NQT-1}g{NPAIR-1}mm"]
        if zslots:
            def zc(e, zslots=tuple(zslots)):
                last = None
                for sl in zslots:
                    last = e.memset(CNT4[:, sl:sl + 1], 0.0)
                return last
            E("gpsimd", zc, inc="G", deps=zdep, tag=f"{tg}cntz")

        if nextg is not None:
            e_qs(nextg, g)
            ezt_by_g[nextg] = e_ez(nextg, False)
            hx = ([f"g{prevg}ffnev0", f"g{prevg}ffnev1"]
                  if prevg is not None else [f"{tg}ev1_{NB-1}"])
            e_qk(nextg, 0, hx)
            e_ev1(nextg, 0, g, NB)
            e_qk(nextg, 1, [])
            e_ev1(nextg, 1, g, NB)
            t1x0 = [f"g{prevg}q8"] if prevg is not None else [f"{tg}ev2_{NB-1}"]
            e_t1(nextg, 0, t1x0)
            e_qk(nextg, 2, [])
            e_ev1(nextg, 2, g, NB)
            e_t1(nextg, 1, [f"{tg}ev2_{NB-1}"])
            e_qk(nextg, 3, [])
            e_ev1(nextg, 3, g, NB)

        prev = [f"{tg}lo"] + [f"{tg}msk{i}" for i in range(8)] + [f"{tg}cmpv{i}" for i in range(8)]
        if zslots:
            prev.append(f"{tg}cntz")
        if gi == 0:
            prev += [f"b{NQT-1}g{NPAIR-1}mm"]
        for r in range(NROUND):
            step = 1 << (13 - r)
            E("vector", lambda e, step=step: e.tensor_scalar(
                TF[:, :], LO[:, :], step, 65536, Alu.add, Alu.mult),
                inc="V", deps=prev, tag=f"{tg}thb_{r}")
            E("vector", lambda e: e.tensor_scalar(
                TFN[:, :], TF[:, :], 32768, -0x80000000, Alu.subtract, Alu.add),
                inc="V", deps=[f"{tg}thb_{r}"], tag=f"{tg}thd_{r}")
            cntdeps = []
            for i in range(4):
                wi = cs3 if (i == 3 and csplit) else widths[i]
                cd = [f"{tg}thd_{r}"]
                if i >= 1:
                    cd.append(f"{tg}cnt{i-1}_{r}")
                elif r >= 1:
                    cd.append(f"{tg}cnt3_{r-1}")
                E("vector", lambda e, i=i, wi=wi: e.tensor_scalar(
                    WBB[:, :wi], EB[:, i * KW: i * KW + wi], TF.bitcast(F32)[:, i:i + 1],
                    0.0, Alu.is_ge, Alu.add, accum_out=CNT[:, i:i + 1]),
                    inc="V", deps=cd, tag=f"{tg}cnt{i}_{r}")
                cntdeps.append(f"{tg}cnt{i}_{r}")
            prevact = f"{tg}ge_{r-1}" if r >= 1 else None
            aocts = ([3] if csplit else []) + [4, 5, 6, 7]
            nch3 = (widths[3] - cs3 + 2111) // 2112 if csplit else 0
            for i in aocts:
                a0 = cs3 if i == 3 else 0
                wi = widths[i] - a0
                nch = (wi + 2111) // 2112
                for ci in range(nch):
                    c0 = a0 + ci * 2112
                    cw = min(2112, a0 + wi - c0)
                    cd = [f"{tg}thd_{r}"]
                    if prevact:
                        cd.append(prevact)
                    E("scalar", lambda e, i=i, c0=c0, cw=cw, ci=ci: e.activation(
                        WBB[:, 4352:4352 + cw], EB[:, i * KW + c0: i * KW + c0 + cw],
                        AF.Sign, bias=TFN.bitcast(F32)[:, i:i + 1],
                        accum_out=CNT4[:, (i - 3) * 2 + ci: (i - 3) * 2 + ci + 1]),
                        inc="A", deps=cd, tag=f"{tg}cc{i}_{ci}_{r}")
                    prevact = f"{tg}cc{i}_{ci}_{r}"
                    cntdeps.append(prevact)
            v5 = CNT4.rearrange("p (o c) -> p o c", o=5)
            E("vector", lambda e, v5=v5: e.tensor_tensor(CNT[:, 4:8], v5[:, 1:5, 0], v5[:, 1:5, 1],
              Alu.add), inc="V", deps=cntdeps, tag=f"{tg}csum_{r}")
            gdep = [f"{tg}csum_{r}"]
            if csplit:
                if nch3 == 2:
                    E("vector", lambda e: e.tensor_tensor(CNT4[:, 0:1], CNT4[:, 0:1],
                      CNT4[:, 1:2], Alu.add), inc="V", deps=cntdeps, tag=f"{tg}s3_{r}")
                    s3d = [f"{tg}s3_{r}"]
                else:
                    s3d = cntdeps
                E("vector", lambda e: e.scalar_tensor_tensor(
                    CNT[:, 3:4], CNT[:, 3:4], 2.0, CNT4[:, 0:1], Alu.mult, Alu.add),
                    inc="V", deps=s3d + [f"{tg}cnt3_{r}"], tag=f"{tg}cb_{r}")
                gdep.append(f"{tg}cb_{r}")
            E("vector", lambda e: e.tensor_tensor(GEF[:, :], CNT[:, :], CMPV[:, :], Alu.is_ge),
              inc="V", deps=gdep, tag=f"{tg}ge_{r}")
            E("vector", lambda e, step=step: e.scalar_tensor_tensor(
                LO[:, :], GEF[:, :], step, LO[:, :], Alu.mult, Alu.add),
                inc="V", deps=[f"{tg}ge_{r}"], tag=f"{tg}upd_{r}")
            prev = [f"{tg}upd_{r}"]
        E("vector", lambda e: e.tensor_scalar(
            TF[:, :], LO[:, :], 16, None, Alu.logical_shift_left),
            inc="V", deps=prev, tag=f"{tg}ek")
        for i in range(8):
            wi = widths[i]
            E("vector", lambda e, i=i, wi=wi: e.scalar_tensor_tensor(
                EB[:, i * KW: i * KW + wi], EB[:, i * KW: i * KW + wi],
                TF.bitcast(F32)[:, i:i + 1], EB[:, i * KW: i * KW + wi],
                Alu.is_ge, Alu.mult, accum_out=SUM8[:, i:i + 1]),
                inc="V", deps=[f"{tg}ek"], tag=f"{tg}stt{i}")
            E("vector", lambda e, i=i: e.reciprocal(RS8[:, i:i + 1], SUM8[:, i:i + 1]),
              inc="V", deps=[f"{tg}stt{i}"], tag=f"{tg}rs{i}")
            bpd = [f"{tg}rs{i}", "bdpost7"]
            if gi >= 1:
                bpd.append(f"g{prevg}t2done{i}")
            E("vector", lambda e, i=i: e.tensor_scalar(
                BDP[i][:, :], BDPOST[:, :], RS8[:, i:i + 1], None, Alu.mult),
                inc="V", deps=bpd, tag=f"{tg}bdp{i}")

        for jb in range(min(3, NB)):
            vd = [f"a{NTILE-1}vsp", "D0BALL"]
            if gi >= 1:
                pj = pNB0 - 1 - ((pNB0 - 1 - jb) % 3)
                if pj >= 0:
                    vd.append(f"g{prevg}av_{pj}")
            E("sync", lambda e, jb=jb: e.dma_start(
                out=VS[jb % 3][:, :], in_=vscr[jb * 128:(jb + 1) * 128, :]),
                dinc=f"DV{jb % 3}", deps=vd, tag=f"{tg}vl_{jb}")
        LIM = max(NB, NBn) + 1 if nextg is not None else NB + 1
        for jb in range(LIM):
            if jb < NB:
                live = [i for i in range(8) if widths[i] > jb * 128]
                for i in live:
                    td = [f"{tg}bdp{i}"]
                    if i == live[0]:
                        if jb >= 1:
                            td.append(f"{tg}t2ev_{jb-1}")
                        elif prevg is not None:
                            td += [f"g{prevg}x1_0", f"g{prevg}x1_1"]
                    E("tensor", lambda e, i=i, jb=jb: e.matmul(
                        PS[i // 4][:, (i % 4) * 128:(i % 4) * 128 + 128],
                        EB[:, i * KW + jb * 128: i * KW + (jb + 1) * 128], BDP[i][:, :],
                        start=True, stop=True), inc="T", deps=td, tag=f"{tg}t2_{jb}_{i}")
                    if jb == widths[i] // 128 - 1:
                        sc.alias(f"{tg}t2done{i}", f"{tg}t2_{jb}_{i}")

                def evt2(e, jb=jb, live=tuple(live)):
                    last = None
                    for bi in range(2):
                        sub = [i for i in live if i // 4 == bi]
                        if not sub:
                            continue
                        if len(sub) == 4:
                            src = PS[bi].rearrange("p (q hh s) -> p hh q s", q=4, hh=16, s=8)
                            dst = ATT.rearrange("p (hh i s) -> p hh i s", hh=16, i=8, s=8)[
                                :, :, bi * 4:(bi + 1) * 4, :]
                            last = e.tensor_copy(dst, src)
                        else:
                            for i in sub:
                                src = PS[bi].rearrange("p (q hh s) -> p hh q s", q=4, hh=16, s=8)[
                                    :, :, i % 4, :]
                                dst = ATT.rearrange("p (hh i s) -> p hh i s", hh=16, i=8, s=8)[:, :, i, :]
                                last = e.tensor_copy(dst, src)
                    return last
                e2d = [f"{tg}t2_{jb}_{i}" for i in live]
                if jb >= 1:
                    e2d.append(f"{tg}av_{jb-1}")
                E("vector", evt2, inc="V", deps=e2d, tag=f"{tg}t2ev_{jb}")
                dead = [i for i in range(8) if i not in live]
                if dead:
                    def zatt(e, dead=tuple(dead)):
                        last = None
                        for i in dead:
                            last = e.memset(
                                ATT.rearrange("p (hh i s) -> p i hh s", hh=16, i=8, s=8)[:, i, :, :], 0.0)
                        return last
                    E("vector", zatt, inc="V", deps=[f"{tg}t2ev_{jb}"], tag=f"{tg}tz_{jb}")
            ja = jb - 1
            if 0 <= ja < NB:
                att_tag = f"{tg}tz_{ja}" if any(widths[i] <= ja * 128 for i in range(8)) \
                    else f"{tg}t2ev_{ja}"
                for h in range(H):
                    avd = [att_tag, f"{tg}vl_{ja}"]
                    if ja == 0 and h < 2:
                        avd.append(f"{tg}exp7_last")
                        if gi >= 1:
                            avd.append(f"g{prevg}gate")
                    par = h % 2
                    E("tensor", lambda e, h=h, ja=ja, NB=NB, par=par: e.matmul(
                        PS[4 + par][64 * par:64 * par + 64, (h // 2) * 64:(h // 2) * 64 + 64],
                        VS[ja % 3][:, h * 64:(h + 1) * 64], ATT[:, h * 64:(h + 1) * 64],
                        start=(ja == 0 and h < 2), stop=(ja == NB - 1 and h >= H - 2),
                        tile_position=(0, 64 * par), skip_group_check=True),
                        inc="T", deps=avd, tag=(f"{tg}av_{ja}" if h == H - 1 else None))
                if ja + 3 < NB:
                    E("sync", lambda e, ja=ja: e.dma_start(
                        out=VS[(ja + 3) % 3][:, :], in_=vscr[(ja + 3) * 128:(ja + 4) * 128, :]),
                        dinc=f"DV{(ja + 3) % 3}", deps=[f"{tg}av_{ja}"], tag=f"{tg}vl_{ja+3}")
            if nextg is not None:
                if 4 <= jb < NBn:
                    e_qk(nextg, jb, [])
                    e_ev1(nextg, jb, g, NB)
                if 2 <= ja < NBn:
                    e_t1(nextg, ja, [])
                if 0 <= ja < NBn:
                    e_ev2(nextg, ja, g, NB, ezt_by_g[nextg])
        if nextg is not None:
            tn = f"g{nextg}"
            sc.alias(f"{tn}qkdone", f"{tn}qk{NBn-1}")
            for jb in range(NBn, NBn + 2):
                sc.alias(f"{tn}ev2_{jb}", f"{tn}ev2_{NBn-1}")
                sc.alias(f"{tn}t1d_{jb}", f"{tn}t1d_{NBn-1}")
        sc.alias(f"{tg}avdone", f"{tg}av_{NB-1}")
        for jb in range(NB, NB + 4):
            sc.alias(f"{tg}av_{jb}", f"{tg}avdone")
        sc.alias(f"{tg}av_m1", f"{tg}avdone")

        gd = [f"b{g//2}g{pr}sp" for pr in range(NPAIR)]
        if gi >= 1:
            gd.append(f"g{prevg}gate")
        E("sync", lambda e, g=g: e.dma_start(
            out=GS.rearrange("p (a q) -> p a q", a=8),
            in_=gscr.rearrange("a p q -> p a q")[:, :, g * 64:(g + 1) * 64]),
            dinc="DGS", deps=gd, tag=f"{tg}gs")
        xrd = [f"g{prevg}ost"] if gi >= 1 else []
        E("sync", lambda e, g=g: e.dma_start(out=XRES[:, :], in_=xq[g * 64:(g + 1) * 64, :]),
          dinc="DXR", deps=xrd, tag=f"{tg}xr")
        E("vector", lambda e: e.tensor_tensor(AOUT[0:64, :], PS[4][0:64, :], GS[0:64, :], Alu.mult),
          inc="V", deps=[f"{tg}avdone", f"{tg}gs"], tag=f"{tg}gateA")
        E("vector", lambda e: e.tensor_tensor(AOUT[64:128, :], PS[5][64:128, :], GS[64:128, :], Alu.mult),
          inc="V", deps=[f"{tg}avdone", f"{tg}gs"], tag=f"{tg}gate")

        if nextg is not None:
            e_B(nextg, g)

        for hv in range(2):
            o2d = [f"{tg}gate", "wos7", f"{tg}t2ev_{NB-1}"]
            for ck in range(8):
                E("tensor", lambda e, hv=hv, ck=ck: e.matmul(
                    PS[hv][:64, :], AOUT[:, ck * 64:(ck + 1) * 64],
                    WBA[:, ck * DIM + hv * 512: ck * DIM + (hv + 1) * 512],
                    start=(ck == 0), stop=(ck == 7)), inc="T",
                    deps=(o2d if ck == 0 else []),
                    tag=(f"{tg}o2_{hv}" if ck == 7 else None))
            E("vector", lambda e, hv=hv: e.tensor_tensor(
                X1[:, hv * 512:(hv + 1) * 512], PS[hv][:64, :], XRES[:, hv * 512:(hv + 1) * 512],
                Alu.add), inc="V", deps=[f"{tg}o2_{hv}", f"{tg}xr"], tag=f"{tg}x1_{hv}")

        t2 = f"{tg}L"
        E("vector", lambda e: e.tensor_reduce(
            STAT[:64, 0:1], X1[:64, :], axis=mybir.AxisListType.X, op=Alu.add),
            inc="V", deps=[f"{tg}x1_0", f"{tg}x1_1"], tag=f"{t2}sum")
        E("scalar", lambda e: e.activation(YB[:64, :], X1[:64, :], AF.Square,
          accum_out=STAT[:64, 1:2]), inc="A", deps=[f"{tg}x1_0", f"{tg}x1_1"], tag=f"{t2}sq")

        E("vector", lambda e: e.tensor_scalar(STAT[:64, 0:1], STAT[:64, 0:1], 1.0 / DIM, None, Alu.mult),
          inc="V", deps=[f"{t2}sum"], tag=f"{t2}mean")
        E("vector", lambda e: e.tensor_scalar(STAT[:64, 2:3], STAT[:64, 1:2], 1.0 / DIM, None, Alu.mult),
          inc="V", deps=[f"{t2}sq"], tag=f"{t2}ex2")
        E("vector", lambda e: e.tensor_tensor(STAT[:64, 3:4], STAT[:64, 0:1], STAT[:64, 0:1], Alu.mult),
          inc="V", deps=[f"{t2}mean"], tag=f"{t2}m2")
        E("vector", lambda e: e.tensor_tensor(STAT[:64, 2:3], STAT[:64, 2:3], STAT[:64, 3:4], Alu.subtract),
          inc="V", deps=[f"{t2}ex2", f"{t2}m2"], tag=f"{t2}var")
        E("scalar", lambda e: e.activation(STAT[:64, 4:5], STAT[:64, 2:3], AF.Ln, bias=EPSV[:64, 0:1]),
          inc="A", deps=[f"{t2}var", "epsv"], tag=f"{t2}sd")
        E("scalar", lambda e: e.activation(STAT[:64, 5:6], STAT[:64, 4:5], AF.Exp, scale=-0.5),
          inc="A", deps=[f"{t2}sd"], tag=f"{t2}rstd")
        E("vector", lambda e: e.tensor_scalar(
            H2B[:64, :], X1[:64, :], STAT[:64, 0:1], STAT[:64, 5:6], Alu.subtract, Alu.mult),
            inc="V", deps=[f"{t2}rstd"], tag=f"{t2}ln")
        for fc in range(8):
            tpd = [f"{t2}ln"]
            if fc == 0:
                if nextg is not None:
                    tpd.append(f"g{nextg}ev2_{NBn-1}")
                else:
                    tpd.append(f"{tg}ev2_{NB-1}")
            E("tensor", lambda e, fc=fc: e.transpose(
                PT[0][:, fc * 64:(fc + 1) * 64], H2B[:64, fc * 128:(fc + 1) * 128], IDE[:64, :64]),
                inc="T", deps=tpd, tag=f"{t2}tp{fc}")
        q8d = [f"{t2}tp{fc}" for fc in range(8)]
        if gi >= 1:
            q8d.append(f"g{prevg}ffnmm1")
        E("vector", lambda e: e.tensor_scalar(
            XQ8[:, :], PT[0][:, :512], SREC[:, 2:3], None, Alu.mult),
            inc="V", deps=q8d, tag=f"{tg}q8")
        for ck in range(8):
            kld = ["spkall0", "spkall1"]
            if ck >= 2:
                kld.append(f"{tg}fm{ck-2}_1")
            elif gi >= 1:
                kld.append(f"g{prevg}fm{6+ck}_1")
            E("sync", lambda e, ck=ck: e.dma_start(
                out=KQ8R[:, (ck % 2) * DIM:(ck % 2 + 1) * DIM], in_=kscr[ck, :, :]),
                dinc=f"DK{ck % 2}", deps=kld, tag=f"{tg}kl{ck}")
            for hv in range(2):
                fd = [f"{tg}q8", f"{tg}kl{ck}"]
                if ck == 0:
                    fd.append(f"{tg}o2_{hv}")
                    if nextg is not None:
                        fd.append(f"g{nextg}ev1_{NBn-1}")
                E("tensor", lambda e, hv=hv, ck=ck: e.matmul(
                    PS[2 + hv][:64, :], XQ8[:, ck * 64:(ck + 1) * 64],
                    KQ8R[:, (ck % 2) * DIM + hv * 512: (ck % 2) * DIM + (hv + 1) * 512],
                    start=(ck == 0), stop=(ck == 7)), inc="T",
                    deps=fd, tag=f"{tg}fm{ck}_{hv}")
        for hv in range(2):
            sc.alias(f"{tg}ffnmm{hv}", f"{tg}fm7_{hv}")
            fe = [f"{tg}ffnmm{hv}", "sprod"]
            if gi >= 1:
                fe.append(f"g{prevg}osum")
            E("vector", lambda e, hv=hv: e.tensor_scalar(
                YB[:64, hv * 512:(hv + 1) * 512], PS[2 + hv][:64, :], SPROD[:64, 0:1], None,
                Alu.mult), inc="V", deps=fe, tag=f"{tg}ffnev{hv}")
        E("vector", lambda e: e.tensor_tensor(YB[:64, :], YB[:64, :], BIAS64[:, :], Alu.add),
          inc="V", deps=[f"{tg}ffnev0", f"{tg}ffnev1", "bias64"], tag=f"{tg}yb")
        od = [f"{tg}yb"]
        E("vector", lambda e: e.tensor_tensor(XRES[:64, :], X1[:64, :], YB[:64, :], Alu.add),
          inc="V", deps=od, tag=f"{tg}osum")
        E("sync", lambda e, g=g: e.dma_start(
            out=out[g * 64:(g + 1) * 64, :], in_=XRES[:64, :]), dinc="SPO",
            deps=[f"{tg}osum"], tag=f"{tg}ost")
        pNB0 = NB

    sc.tags["FINAL"] = [(nm, sc.cnt[nm]) for nm in ("SPO", "SPV", "SPQ", "SPG", "SPK0", "SPK1")]
    sc.steps["sync"].append((("FINAL",), lambda e: None, None, 0, 0))
    sc.build()
    return nc


def _shard_inputs(inputs):
    x = np.asarray(inputs["L0_attn_x"])
    owns = [np.concatenate([np.arange(8 * o, 8 * o + 8) for o in range(h, 512, 2)])
            for h in (0, 1)]
    shared = {
        "wq": np.asarray(inputs["L0_attn_Wq"]),
        "wk": np.asarray(inputs["L0_attn_Wk"]),
        "wv": np.asarray(inputs["L0_attn_Wv"]),
        "wg": np.asarray(inputs["L0_attn_Wg"]),
        "wo": np.asarray(inputs["L0_attn_Wo"]),
        "wpre": np.asarray(inputs["L0_attn_W_pre_th"]),
        "wpost": np.asarray(inputs["L0_attn_W_post_th"]),
        "memk": np.asarray(inputs["L0_attn_mem_k"]),
        "memv": np.asarray(inputs["L0_attn_mem_v"]),
        "hsc": np.ascontiguousarray(np.asarray(inputs["L0_attn_head_scale_params"]).reshape(H, 1)),
        "bg": np.asarray(inputs["L0_attn_bg"]),
        "fker": np.asarray(inputs["L0_ffn_kernel"]),
        "fbias": np.ascontiguousarray(np.asarray(inputs["L0_ffn_bias"]).reshape(1, DIM)),
        "s_in": np.ascontiguousarray(np.asarray(inputs["L0_ffn_input_scale"]).reshape(1, 1)),
        "s_k": np.ascontiguousarray(np.asarray(inputs["L0_ffn_kernel_scale"]).reshape(1, 1)),
    }
    cc = [host_consts(0), host_consts(1)]
    maps = []
    for c in range(8):
        b, half = c // 2, c % 2
        m = dict(shared)
        m["xfull"] = np.ascontiguousarray(x[b])
        m["xq"] = np.ascontiguousarray(x[b][owns[half]])
        m.update(cc[half])
        maps.append(m)
    return maps, owns


_NC_CACHE = {}


def kernel(**inputs):
    if "prog" not in _NC_CACHE:
        _NC_CACHE["prog"] = build_program()
    nc = _NC_CACHE["prog"]
    maps, owns = _shard_inputs(inputs)
    res = run_bass_kernel_spmd(nc, maps, core_ids=list(range(8)))
    outp = np.zeros((B, N, DIM), dtype=np.float32)
    for c in range(8):
        b, half = c // 2, c % 2
        outp[b][owns[half]] = res.results[c]["out"]
    return outp
